# version 10
# baseline (speedup 1.0000x reference)
import contextlib
import numpy as np
import concourse.bass as bass
import concourse.mybir as mybir
from concourse.bass_utils import run_bass_kernel_spmd

F32 = mybir.dt.float32
F32R = mybir.dt.float32r
BF16 = mybir.dt.bfloat16
AF = mybir.ActivationFunctionType
ALU = mybir.AluOpType
AX = mybir.AxisListType

NC = 8
T = 2048
NDC = 8
NTB = 4
TBS = 512
NE = 8
HQ = 4
NQ = 4
EPS = 1e-5

_CACHE = {}


class Ctr:
    __slots__ = ("h", "n")

    def __init__(self, h):
        self.h = h
        self.n = 0


def r2(ap, b):
    return ap.rearrange("p (a b) -> p a b", b=b)


def r3(ap, b, c):
    return ap.rearrange("p (a b c) -> p a b c", b=b, c=c)


def build_program():
    nc = bass.Bass("TRN2", target_bir_lowering=False)

    d_x = nc.dram_tensor("x", [128, NDC, T], F32, kind="ExternalInput").ap()
    d_y = nc.dram_tensor("y", [128, NDC, T], F32, kind="ExternalOutput").ap()
    d_wd1 = nc.dram_tensor("wd1", [NE, 128, 2048], F32R, kind="ExternalInput").ap()
    d_wd2 = nc.dram_tensor("wd2", [NE, 128, 2048], F32R, kind="ExternalInput").ap()
    d_wbase = nc.dram_tensor("wbase", [NDC, 128, 1024], F32, kind="ExternalInput").ap()
    d_wadn = nc.dram_tensor("wadn", [NE, 128, 512], F32, kind="ExternalInput").ap()
    d_waup = nc.dram_tensor("waup", [NE, 64, 1024], F32, kind="ExternalInput").ap()
    d_wqk = nc.dram_tensor("wqk", [128, NDC, 2048], F32, kind="ExternalInput").ap()
    d_wv = nc.dram_tensor("wv", [128, NDC, 1024], F32, kind="ExternalInput").ap()
    d_wog = nc.dram_tensor("wog", [16, 64, 1024], F32, kind="ExternalInput").ap()
    d_nid = nc.dram_tensor("nid", [128, 128], F32R, kind="ExternalInput").ap()
    d_bd1 = nc.dram_tensor("bd1", [128, 16], F32, kind="ExternalInput").ap()
    d_bd2 = nc.dram_tensor("bd2", [128, 64], F32, kind="ExternalInput").ap()
    d_bb = nc.dram_tensor("bb", [128, 8], F32, kind="ExternalInput").ap()
    d_bo = nc.dram_tensor("bo", [128, 8], F32, kind="ExternalInput").ap()
    d_bad = nc.dram_tensor("bad", [64, 8], F32, kind="ExternalInput").ap()
    d_bau = nc.dram_tensor("bau", [128, 64], F32, kind="ExternalInput").ap()
    d_pw8 = nc.dram_tensor("pw8", [1, 8], F32, kind="ExternalInput").ap()
    d_bq2 = nc.dram_tensor("bq2", [128, 16], F32, kind="ExternalInput").ap()
    d_bk2 = nc.dram_tensor("bk2", [128, 16], F32, kind="ExternalInput").ap()
    d_bvr = nc.dram_tensor("bvr", [1, 1024], F32, kind="ExternalInput").ap()
    d_xdr = nc.dram_tensor("xdr", [128, NDC, T], F32, kind="Internal").ap()
    d_s1 = nc.dram_tensor("dsc1", [1, 2048], F32, kind="Internal").ap()
    d_s2 = nc.dram_tensor("dsc2", [1, 2048], F32, kind="Internal").ap()
    d_sb = nc.dram_tensor("dscb", [1, 1024], BF16, kind="Internal").ap()
    d_s3 = nc.dram_tensor("dsc3", [1, 64], F32, kind="Internal").ap()

    SB0 = 16512
    _A = nc.alloc_sbuf_tensor_at

    def A(name, shape, dtype, offset):
        return _A(name, shape, dtype, offset=SB0 + offset)
    NID = A("NID", [128, 128], F32R, offset=0)
    BD1 = A("BD1", [128, 16], F32, offset=512)
    BD2 = A("BD2", [128, 64], F32, offset=576)
    BB = A("BB", [128, 8], F32, offset=832)
    BBH = A("BBH", [128, 8], F32, offset=864)
    BO = A("BO", [128, 8], F32, offset=896)
    BAU = A("BAU", [128, 64], F32, offset=928)
    BAD = A("BAD", [64, 8], F32, offset=1184)
    BADS = A("BADS", [64, 8], F32, offset=1216)
    OH = A("OH", [128, 8], F32, offset=1248)
    BCT = A("BCT", [128, 8], F32, offset=1280)
    BCOMB = A("BCOMB", [128, 8], F32, offset=1312)
    PW8 = A("PW8", [1, 8], F32, offset=1344)
    LS = A("LS", [1, 64], F32, offset=1376)
    LA = A("LA", [128, 256], F32, offset=1632)
    LP = A("LP", [128, 8], F32, offset=2656)
    ONESC = A("ONESC", [128, 16], BF16, offset=2688)
    ONES1C = A("ONES1C", [128, 8], F32, offset=2720)
    ONESB = A("ONESB", [1, 128], BF16, offset=2752)
    LROW = A("LROW", [1, 1024], F32, offset=3008)
    LROWB = A("LROWB", [1, 1024], BF16, offset=7104)
    BQ2 = A("BQ2", [128, 16], F32, offset=9152)
    BK2 = A("BK2", [128, 16], F32, offset=9216)
    BVR = A("BVR", [1, 1024], BF16, offset=9280)
    RTRC = A("RTRC", [128, 32], F32, offset=11328)
    LROW2 = A("LROW2", [1, 1024], F32, offset=206848)
    LROWB2 = A("LROWB2", [1, 1024], BF16, offset=9280)
    EPSC = A("EPSC", [1, 8], F32, offset=11456)
    FEN = A("FEN", [1, 16], F32, offset=11488)
    XR = A("XR", [128, NDC * T], F32, offset=12288)
    XL = A("XL", [128, NDC * T], BF16, offset=77824)
    XLR = A("XLR", [128, 4096], F32R, offset=110592)
    XLR2 = A("XLR2", [128, 4096], F32R, offset=190464)
    WD1 = A("WD1", [128, 4096], F32R, offset=126976)
    WD2 = A("WD2", [128, 4096], F32R, offset=143360)
    HBR = A("HBR", [128, 1024], F32R, offset=159744)
    ASB = A("ASB", [64, 2048], BF16, offset=163840)
    LNSQ = A("LNSQ", [128, 4096], BF16, offset=167936)
    MBRB = A("MBRB", [128, 1024], BF16, offset=176128)
    SQSC = A("SQSC", [128, 512], BF16, offset=178176)
    WASL = A("WASL", [128, 1024], F32, offset=179200)
    WBC = A("WBC", [128, 2048], BF16, offset=183296)
    WDSEL = A("WDSEL", [128, 512], BF16, offset=187392)
    WAUS = A("WAUS", [64, 1024], BF16, offset=188416)
    WQTQK = A("WQTQK", [128, 16384], BF16, offset=12288)
    QKV = A("QKV", [128, 16384], BF16, offset=45056)
    XRQ = A("XRQ", [128, 4096], F32, offset=77824)
    XLQ = A("XLQ", [128, 4096], BF16, offset=94208)
    VSB = A("VSB", [128, 4160], BF16, offset=102400)
    WQTVC = A("WQTVC", [128, 2048], BF16, offset=110720)
    ES = A("ES", [128, 2048], BF16, offset=114816)
    RCBT = A("RCBT", [128, 2048], F32, offset=118912)
    WOG = A("WOG", [64, 16384], BF16, offset=127104)
    LNSQA = A("LNSQA", [128, 4096], BF16, offset=159872)
    MBRBA = A("MBRBA", [128, 1024], BF16, offset=168064)

    SPA = nc.alloc_psum_tensor("SPA", [128, 1024], F32)
    SPB = nc.alloc_psum_tensor("SPB", [128, 1024], F32)
    VP = nc.alloc_psum_tensor("VP", [128, 2048], F32)

    es = contextlib.ExitStack()
    sdy = Ctr(es.enter_context(nc.semaphore("sdy")))
    sdg = Ctr(es.enter_context(nc.semaphore("sdg")))
    ts = Ctr(es.enter_context(nc.semaphore("ts")))
    ss = Ctr(es.enter_context(nc.semaphore("ss")))
    sv = Ctr(es.enter_context(nc.semaphore("sv")))
    sp = Ctr(es.enter_context(nc.semaphore("sp")))

    ops = []

    def op(eng, bf, waits=(), inc=None, by=1):
        wts = tuple((c, v) for (c, v) in waits if v > 0)
        if inc is not None:
            inc.n += by
        tgt = inc.n if inc is not None else 0

        def fn(e, bf=bf, wts=wts, inc=inc, by=by):
            for (c, val) in wts:
                e.wait_ge(c.h, val)
            ins = bf(e)
            if inc is not None:
                ins.then_inc(inc.h, by)
        ops.append((eng, fn))
        return tgt

    def Wv(c, v):
        return (c, v)

    def drain(eng):
        ops.append((eng, lambda e: e.drain()))

    _fsl = {"v": 0, "g": 2, "s": 4}

    def fence(eng, sem):
        drain(eng)
        i = _fsl[eng]
        if eng == "s":
            return op(eng, lambda e: e.square(FEN[0:1, i:i + 1], FEN[0:1, i:i + 1]),
                      inc=sem)
        return op(eng, lambda e: e.tensor_copy(FEN[0:1, i:i + 1], FEN[0:1, i + 1:i + 2]),
                  inc=sem)

    xdma = []
    for tb in range(NTB):
        xdma.append(op("y", lambda e, tb=tb: e.dma_start(
            r2(XR[:, :], T)[:, :, tb * TBS:(tb + 1) * TBS],
            d_x[:, :, tb * TBS:(tb + 1) * TBS]), inc=sdy, by=16))
    for dst, src in [(NID[:, :], d_nid), (BD1[:, :], d_bd1), (BD2[:, :], d_bd2),
                     (BB[:, :], d_bb), (BO[:, :], d_bo), (BAD[:, :], d_bad),
                     (BAU[:, :], d_bau), (PW8[:, :], d_pw8), (BQ2[:, :], d_bq2),
                     (BK2[:, :], d_bk2)]:
        op("y", lambda e, dst=dst, src=src: e.dma_start(dst, src), inc=sdy, by=16)
    consts_loaded = sdy.n
    op("v", lambda e: e.memset(ONESC[:, 0:1], 1.0 / 1024.0), inc=sv)
    op("v", lambda e: e.memset(ONES1C[:, 0:1], 1.0), inc=sv)
    op("v", lambda e: e.memset(ONESB[:, :], 1.0), inc=sv)
    op("v", lambda e: e.memset(EPSC[0:1, :], EPS), inc=sv)
    init_v = op("v", lambda e: e.tensor_scalar_mul(BBH[:, :], BB[:, :], 0.5),
                waits=[Wv(sdy, consts_loaded)], inc=sv)

    def emit_ln(xr3, xl3, lnsq, mbrb, want_xlr, war_wait, xlr3=None, lrow=None, lrowb=None):
        PM = VP[0:1, 1024:1536]
        PSQ = VP[0:1, 1536:2048]
        if lrow is None:
            lrow, lrowb = LROW, LROWB
        lnsq3 = r2(lnsq[:, :], 512)
        cast_done = []
        for dc in range(NDC):
            cast_done.append(op("v", lambda e, dc=dc: e.tensor_copy(
                xl3(dc), xr3[:, dc, :]), waits=war_wait if dc == 0 else (), inc=sv))
        sq = op("s", lambda e: e.activation(lnsq3, xr3, AF.Square),
                waits=war_wait, inc=ss)
        for dc in range(NDC):
            op("t", lambda e, dc=dc: e.matmul(
                PM, ONESC[:, 0:1], xl3(dc), start=(dc == 0), stop=(dc == NDC - 1)),
                waits=[Wv(sv, cast_done[dc]), Wv(sv, init_v)])
        mm_done = 0
        for dc in range(NDC):
            last = dc == NDC - 1
            mm_done = op("t", lambda e, dc=dc: e.matmul(
                PSQ, ONESC[:, 0:1], lnsq3[:, dc, :],
                start=(dc == 0), stop=(dc == NDC - 1)),
                waits=[Wv(ss, sq)] if dc == 0 else (), inc=ts if last else None)
        op("v", lambda e: e.tensor_copy(lrow[0:1, 0:512], PM), waits=[Wv(ts, mm_done)])
        drain("v")
        op("v", lambda e: e.tensor_copy(lrowb[0:1, 0:512], lrow[0:1, 0:512]))
        drain("v")
        op("v", lambda e: e.tensor_mul(lrow[0:1, 512:1024], lrow[0:1, 0:512],
                                       lrow[0:1, 0:512]))
        drain("v")
        var = op("v", lambda e: e.tensor_tensor(lrow[0:1, 512:1024], PSQ,
                                                lrow[0:1, 512:1024], ALU.subtract),
                 inc=sv)
        op("s", lambda e: e.activation(lrow[0:1, 0:512], lrow[0:1, 512:1024],
                                       AF.Ln, bias=EPSC[0:1, 0:1]), waits=[Wv(sv, var)])
        drain("s")
        rstd = op("s", lambda e: e.activation(lrowb[0:1, 512:1024], lrow[0:1, 0:512],
                                              AF.Exp, scale=-0.5), inc=ss)
        sbw = op("g", lambda e, lrowb=lrowb: e.dma_start(d_sb, lrowb[0:1, :]),
                 waits=[Wv(ss, rstd)], inc=sdg, by=16)
        bc = op("g", lambda e: e.dma_start(
            mbrb[:, :], bass.AP(d_sb.tensor, 0, [[0, 128], [1, 1024]])),
            waits=[Wv(sdg, sbw)], inc=sdg, by=16)
        vdone = []
        pdone = []
        for dc in range(4):
            tg = (xlr3[:, dc, :] if want_xlr else xl3(dc))
            vdone.append(op("v", lambda e, dc=dc, tg=tg: e.tensor_tensor(
                tg, xr3[:, dc, :], mbrb[:, 0:512], ALU.subtract),
                waits=[Wv(sdg, bc), Wv(ts, mm_done)], inc=sv))
        for dc in range(4, 8):
            tg = (xlr3[:, dc, :] if want_xlr else xl3(dc))
            pdone.append(op("g", lambda e, dc=dc, tg=tg: e.tensor_tensor(
                tg, xr3[:, dc, :], mbrb[:, 0:512], ALU.subtract),
                waits=[Wv(sdg, bc), Wv(ts, mm_done)], inc=sp))
        mdone_p = mdone_v = 0
        for dc in range(4):
            tg = (xlr3[:, dc, :] if want_xlr else xl3(dc))
            mdone_p = op("g", lambda e, dc=dc, tg=tg: e.tensor_tensor(
                tg, tg, mbrb[:, 512:1024], ALU.mult),
                waits=[Wv(sv, vdone[dc])], inc=sp)
        for dc in range(4, 8):
            tg = (xlr3[:, dc, :] if want_xlr else xl3(dc))
            mdone_v = op("v", lambda e, dc=dc, tg=tg: e.tensor_tensor(
                tg, tg, mbrb[:, 512:1024], ALU.mult),
                waits=[Wv(sp, pdone[dc - 4])], inc=sv)
        vf = fence("v", sv)
        if want_xlr:
            for dc in range(NDC):
                op("g", lambda e, dc=dc: e.tensor_copy(
                    xl3(dc), xlr3[:, dc, :]),
                    waits=[Wv(sv, mdone_v)] if dc == 0 else ())
        pf = fence("g", sp)
        return {"v": vf, "p": pf}

    def emit_ffn(call, xwait):
        xr3t = r2(XR[:, :], T)
        xl3t = r2(XL[:, :], T)
        k_total = NTB * NE

        d1_end = {}
        d2_end = {}
        relu_end = {}
        sq_end = {}
        wd1_ld = {}
        wd2_ld = {}

        def xlr_of(tb):
            return r2((XLR if tb % 2 == 0 else XLR2)[:, :], 512)

        def issue_wd1(k, waits):
            e_ = k % NE
            wd1_ld[k] = op("y", lambda e, k=k, e_=e_: e.dma_start(
                WD1[:, (k % 2) * 2048:(k % 2) * 2048 + 2048], d_wd1[e_, :, :]),
                waits=waits, inc=sdy, by=16)

        def issue_wd2(k, waits):
            e_ = k % NE
            wd2_ld[k] = op("y", lambda e, k=k, e_=e_: e.dma_start(
                WD2[:, (k % 2) * 2048:(k % 2) * 2048 + 2048], d_wd2[e_, :, :]),
                waits=waits, inc=sdy, by=16)

        issue_wd1(0, [Wv(sdy, consts_loaded)])
        issue_wd2(0, [])
        issue_wd1(1, [])
        issue_wd2(1, [])

        def emit_ln_tb(tb):
            war = [Wv(sdy, xwait[tb])]
            if tb >= 2:
                war.append(Wv(ts, d2_end[((tb - 2) * NE + NE - 1, NDC - 1)]))
            return emit_ln(
                xr3t[:, :, tb * TBS:(tb + 1) * TBS],
                lambda dc, tb=tb: xl3t[:, dc, tb * TBS:(tb + 1) * TBS],
                LNSQ, MBRB, True, war, xlr_of(tb),
                LROW if tb % 2 == 0 else LROW2,
                LROWB if tb % 2 == 0 else LROWB2)

        ln_done = {0: emit_ln_tb(0)}
        for tb in range(NTB):
            xlr3 = xlr_of(tb)
            for e_ in range(NE):
                k = tb * NE + e_
                if e_ == 1 and tb + 1 < NTB:
                    ln_done[tb + 1] = emit_ln_tb(tb + 1)
                wd1k = r2(WD1[:, (k % 2) * 2048:(k % 2) * 2048 + 2048], 256)
                wd2k = r2(WD2[:, (k % 2) * 2048:(k % 2) * 2048 + 2048], 1024)
                for hc in range(2):
                    pd1 = SPA[:, hc * 512:hc * 512 + 512]
                    cnt = 0
                    for dc in range(NDC):
                        waits = []
                        if dc == 0:
                            waits = [Wv(sdy, wd1_ld[k]),
                                     Wv(sv, ln_done[tb]["v"]),
                                     Wv(sp, ln_done[tb]["p"])]
                            if k >= 1:
                                waits.append(Wv(ss, relu_end.get((k - 1, hc), 0)))
                                waits.append(Wv(ts, d2_end[(k - 1, NDC - 1)]))
                        cnt = op("t", lambda e, dc=dc, hc=hc, pd1=pd1, wd1k=wd1k,
                                 xlr3=xlr3:
                                 e.matmul(pd1, wd1k[:, dc, hc * 128:hc * 128 + 128],
                                          xlr3[:, dc, :],
                                          start=(dc == 0), stop=(dc == 7)),
                                 waits=waits, inc=ts if dc == 7 else None)
                    d1_end[k] = cnt
                    relu_end[(k, hc)] = op(
                        "s", lambda e, hc=hc, pd1=pd1, e_=e_: e.activation(
                            HBR[:, hc * 512:hc * 512 + 512], pd1, AF.Relu,
                            bias=BD1[:, e_ * 2 + hc:e_ * 2 + hc + 1]),
                        waits=[Wv(ts, cnt)], inc=ss)
                if k + 2 < k_total:
                    issue_wd1(k + 2, [Wv(ts, d1_end[k])])
                for oc in range(NDC):
                    pd2 = SPB[:, (oc % 2) * 512:(oc % 2) * 512 + 512]
                    waits = [Wv(ss, relu_end[(k, 1)]), Wv(sdy, wd2_ld[k])]
                    if (k, oc - 2) in sq_end:
                        waits.append(Wv(ss, sq_end[(k, oc - 2)]))
                    elif (k - 1, oc + 6) in sq_end:
                        waits.append(Wv(ss, sq_end[(k - 1, oc + 6)]))
                    for hc in range(2):
                        op("t", lambda e, hc=hc, oc=oc, pd2=pd2, wd2k=wd2k:
                           e.matmul(pd2, wd2k[:, hc, oc * 128:oc * 128 + 128],
                                    HBR[:, hc * 512:hc * 512 + 512],
                                    start=(hc == 0), stop=False),
                           waits=waits if hc == 0 else ())
                    cnt = op("t", lambda e, oc=oc, pd2=pd2, xlr3=xlr3: e.matmul(
                        pd2, NID[:, :], xlr3[:, oc, :], start=False, stop=True),
                        inc=ts)
                    d2_end[(k, oc)] = cnt
                    col = e_ * 32 + tb * 8 + oc
                    sq_end[(k, oc)] = op(
                        "s", lambda e, pd2=pd2, e_=e_, oc=oc, col=col: e.activation(
                            SQSC[:, :], pd2, AF.Square,
                            bias=BD2[:, e_ * 8 + oc:e_ * 8 + oc + 1],
                            accum_out=LA[:, col:col + 1]),
                        waits=[Wv(ts, cnt)], inc=ss)
                if k + 2 < k_total:
                    issue_wd2(k + 2, [Wv(ts, d2_end[(k, NDC - 1)])])

        wb_dma = {}
        wb_cast = {}
        bu_end = {}
        ev_b = {}
        wb_dma[0] = op("y", lambda e: e.dma_start(WASL[:, :], d_wbase[0, :, :]),
                       waits=[Wv(sdy, wd2_ld[k_total - 1])], inc=sdy, by=16)
        vp_banks = [VP[:, i * 512:i * 512 + 512] for i in range(4)]
        for oc in range(NDC):
            wl = [Wv(sdy, wb_dma[oc])]
            if oc >= 2:
                wl.append(Wv(ts, bu_end[(oc - 2, 3)]))
            op("g", lambda e, oc=oc: e.tensor_scalar_mul(
                WBC[:, (oc % 2) * 1024:(oc % 2) * 1024 + 1024], WASL[:, :], 0.5),
                waits=wl)
            wb_cast[oc] = fence("g", sp)
            if oc + 1 < NDC:
                wb_dma[oc + 1] = op("y", lambda e, oc=oc: e.dma_start(
                    WASL[:, :], d_wbase[oc + 1, :, :]),
                    waits=[Wv(sp, wb_cast[oc])], inc=sdy, by=16)
            for tt in range(4):
                j = oc * 4 + tt
                pbu = vp_banks[j % 4]
                waits = [Wv(sp, wb_cast[oc])]
                if j >= 4:
                    waits.append(Wv(sv, ev_b[j - 4]))
                else:
                    waits.append(Wv(sv, ln_done[NTB - 1]["v"]))
                cnt = 0
                for dc in range(NDC):
                    cnt = op("t", lambda e, oc=oc, tt=tt, dc=dc, pbu=pbu: e.matmul(
                        pbu, r2(WBC[:, (oc % 2) * 1024:(oc % 2) * 1024 + 1024],
                                128)[:, dc, :],
                        xl3t[:, dc, tt * 512:tt * 512 + 512],
                        start=(dc == 0), stop=(dc == 7)),
                        waits=waits if dc == 0 else (),
                        inc=ts if dc == 7 else None)
                bu_end[(oc, tt)] = cnt
                ev_b[j] = op("v", lambda e, oc=oc, tt=tt, pbu=pbu:
                             e.scalar_tensor_tensor(
                                 xr3t[:, oc, tt * 512:tt * 512 + 512], pbu,
                                 BBH[:, oc:oc + 1],
                                 xr3t[:, oc, tt * 512:tt * 512 + 512],
                                 ALU.add, ALU.add),
                             waits=[Wv(ts, cnt)], inc=sv)

        drain("s")
        sfence = op("s", lambda e: e.square(LS[0:1, 48:49], LS[0:1, 48:49]),
                    waits=[Wv(ss, sq_end[(k_total - 1, NDC - 1)])], inc=ss)
        lpd = 0
        for e_ in range(NE):
            lpd = op("v", lambda e, e_=e_: e.tensor_reduce(
                LP[:, e_:e_ + 1], LA[:, e_ * 32:e_ * 32 + 32], AX.X, ALU.add),
                waits=[Wv(ss, sfence)] if e_ == 0 else (),
                inc=sv)
        lpd = fence("v", sv)
        pl_mm = op("t", lambda e: e.matmul(SPA[0:1, 0:8], ONES1C[:, 0:1], LP[:, :],
                                           start=True, stop=True),
                   waits=[Wv(sv, lpd), Wv(ss, relu_end[(k_total - 1, 1)])], inc=ts)
        op("v", lambda e: e.tensor_copy(LS[0:1, 0:8], SPA[0:1, 0:8]),
           waits=[Wv(ts, pl_mm)])
        drain("v")
        op("v", lambda e: e.tensor_reduce(LS[0:1, 8:9], LS[0:1, 0:8], AX.X, ALU.min))
        drain("v")
        op("v", lambda e: e.tensor_scalar(LS[0:1, 9:17], LS[0:1, 0:8], LS[0:1, 8:9],
                                          None, ALU.is_equal))
        drain("v")
        op("v", lambda e: e.tensor_mul(LS[0:1, 25:33], LS[0:1, 9:17], PW8[0:1, :]))
        drain("v")
        op("v", lambda e: e.tensor_reduce(LS[0:1, 33:34], LS[0:1, 25:33], AX.X,
                                          ALU.max))
        drain("v")
        op("v", lambda e: e.tensor_scalar(LS[0:1, 40:48], LS[0:1, 25:33],
                                          LS[0:1, 33:34], None, ALU.is_equal))
        oh_row = fence("v", sv)
        ohw = op("g", lambda e: e.dma_start(d_s3[0:1, 0:8], LS[0:1, 40:48]),
                 waits=[Wv(sv, oh_row)], inc=sdg, by=16)
        oh_bc = op("g", lambda e: e.dma_start(
            OH[:, :], bass.AP(d_s3.tensor, 0, [[0, 128], [1, 8]])),
            waits=[Wv(sdg, ohw)], inc=sdg, by=16)

        sel_d = 0
        dma_p = {}
        dma_p[0] = op("y", lambda e: e.dma_start(WASL[:, 0:512], d_wadn[0, :, :]),
                      waits=[Wv(sp, wb_cast[NDC - 1])], inc=sdy, by=16)
        dma_p[1] = op("y", lambda e: e.dma_start(WASL[:, 512:1024], d_wadn[1, :, :]),
                      inc=sdy, by=16)
        stt_d = {}
        for e_ in range(NE):
            waits = [Wv(sdy, dma_p[e_]), Wv(sdg, oh_bc)]
            if e_ == 0:
                stt_d[0] = op("v", lambda e: e.tensor_scalar(
                    WDSEL[:, :], WASL[:, 0:512], OH[:, 0:1], None, ALU.mult),
                    waits=waits, inc=sv)
            else:
                drain("v")
                stt_d[e_] = op("v", lambda e, e_=e_: e.scalar_tensor_tensor(
                    WDSEL[:, :], WASL[:, (e_ % 2) * 512:(e_ % 2) * 512 + 512],
                    OH[:, e_:e_ + 1], WDSEL[:, :], ALU.mult, ALU.add),
                    waits=waits, inc=sv)
            sel_d = stt_d[e_]
            if e_ == NE - 1:
                sel_d = fence("v", sv)
            if e_ + 2 < NE:
                dma_p[e_ + 2] = op("y", lambda e, e_=e_: e.dma_start(
                    WASL[:, (e_ % 2) * 512:(e_ % 2) * 512 + 512],
                    d_wadn[e_ + 2, :, :]), waits=[Wv(sv, stt_d[e_])], inc=sdy, by=16)
        sel_u = 0
        for e_ in range(NE):
            wl = [Wv(sv, sel_u)] if e_ >= 1 else [Wv(sv, sel_d)]
            ud = op("y", lambda e, e_=e_: e.dma_start(
                WASL[0:64, :], d_waup[e_, :, :]), waits=wl, inc=sdy, by=16)
            if e_ == 0:
                sel_u = op("v", lambda e: e.tensor_scalar(
                    WAUS[:, :], WASL[0:64, :], OH[0:64, 0:1], None, ALU.mult),
                    waits=[Wv(sdy, ud)], inc=sv)
            else:
                drain("v")
                sel_u = op("v", lambda e, e_=e_: e.scalar_tensor_tensor(
                    WAUS[:, :], WASL[0:64, :], OH[0:64, e_:e_ + 1], WAUS[:, :],
                    ALU.mult, ALU.add), waits=[Wv(sdy, ud)], inc=sv)
                if e_ == NE - 1:
                    sel_u = fence("v", sv)
        op("v", lambda e: e.tensor_mul(BADS[:, 0:1], BAD[:, 0:1], OH[0:64, 0:1]),
           waits=[Wv(sdg, oh_bc)])
        for e_ in range(1, NE):
            drain("v")
            op("v", lambda e, e_=e_: e.scalar_tensor_tensor(
                BADS[:, 0:1], BAD[:, e_:e_ + 1], OH[0:64, e_:e_ + 1], BADS[:, 0:1],
                ALU.mult, ALU.add))
        drain("v")
        op("v", lambda e: e.tensor_scalar_mul(BADS[:, 0:1], BADS[:, 0:1], 0.5))
        bads_done = fence("v", sv)
        for oc in range(NDC):
            op("v", lambda e, oc=oc: e.tensor_mul(
                BCT[:, 0:8], BAU[:, oc * 8:oc * 8 + 8], OH[:, :]),
                waits=[Wv(sdg, oh_bc)] if oc == 0 else ())
            drain("v")
            op("v", lambda e, oc=oc: e.tensor_reduce(BCOMB[:, oc:oc + 1], BCT[:, 0:8],
                                                     AX.X, ALU.add))
            drain("v")
        op("v", lambda e: e.tensor_scalar_mul(BCOMB[:, :], BCOMB[:, :], 0.5))
        bcomb_done = fence("v", sv)

        arelu = {}
        for tt in range(4):
            pda = SPA[0:64, (tt % 2) * 512:(tt % 2) * 512 + 512]
            cnt = 0
            for dc in range(NDC):
                waits = []
                if dc == 0:
                    waits = [Wv(sv, sel_d), Wv(sv, bads_done)]
                    if tt >= 2:
                        waits.append(Wv(ss, arelu[tt - 2]))
                cnt = op("t", lambda e, dc=dc, pda=pda, tt=tt: e.matmul(
                    pda, r2(WDSEL[:, :], 64)[:, dc, :],
                    xl3t[:, dc, tt * 512:tt * 512 + 512],
                    start=(dc == 0), stop=(dc == 7)),
                    waits=waits, inc=ts if dc == 7 else None)
            op("s", lambda e, pda=pda, tt=tt: e.activation(
                ASB[:, tt * 512:tt * 512 + 512], pda, AF.Relu, scale=0.5,
                bias=BADS[:, 0:1]), waits=[Wv(ts, cnt), Wv(sv, bads_done)])
            arelu[tt] = fence("s", ss)

        ev_done = {}
        for oc in range(NDC):
            for tt in range(4):
                j = oc * 4 + tt
                pbu = SPB[:, (j % 2) * 512:(j % 2) * 512 + 512]
                waits = [Wv(ss, arelu[tt]), Wv(sv, sel_u), Wv(sv, bcomb_done)]
                if j >= 2:
                    waits.append(Wv(sv, ev_done[j - 2]))
                else:
                    waits.append(Wv(ss, sq_end[(k_total - 1, NDC - 1 - (1 - j))]))
                cnt = op("t", lambda e, oc=oc, tt=tt, pbu=pbu: e.matmul(
                    pbu, WAUS[:, oc * 128:oc * 128 + 128],
                    ASB[:, tt * 512:tt * 512 + 512], start=True, stop=True),
                    waits=waits, inc=ts)
                ev_done[j] = op("v", lambda e, oc=oc, tt=tt, pbu=pbu:
                                e.scalar_tensor_tensor(
                                    xr3t[:, oc, tt * 512:tt * 512 + 512], pbu,
                                    BCOMB[:, oc:oc + 1],
                                    xr3t[:, oc, tt * 512:tt * 512 + 512],
                                    ALU.add, ALU.add),
                                waits=[Wv(ts, cnt), Wv(sv, ev_b[NDC * 4 - 1])],
                                inc=sv)
        return {"last_v": fence("v", sv)}

    def emit_attn(ffn1):
        spill = op("y", lambda e: e.dma_start(d_xdr[:, :, :], r2(XR[:, :], T)),
                   waits=[Wv(sv, ffn1["last_v"])], inc=sdy, by=16)
        bar = [Wv(ts, ts.n), Wv(ss, ss.n), Wv(sv, sv.n), Wv(sp, sp.n),
               Wv(sdy, sdy.n), Wv(sdg, sdg.n)]
        bv_ld = op("y", lambda e: e.dma_start(LROW[0:1, :], d_bvr), waits=bar,
                   inc=sdy, by=16)
        op("g", lambda e: e.tensor_copy(BVR[0:1, :], LROW[0:1, :]),
           waits=[Wv(sdy, bv_ld), *bar])
        bvr_c = fence("g", sp)
        qk_cast = []
        for dc in range(NDC):
            ld = op("y", lambda e, dc=dc: e.dma_start(RCBT[:, :], d_wqk[:, dc, :]),
                    waits=[Wv(sp, qk_cast[-1])] if dc else bar, inc=sdy, by=16)
            op("g", lambda e, dc=dc: e.tensor_copy(
                WQTQK[:, dc * 2048:dc * 2048 + 2048], RCBT[:, :]),
                waits=[Wv(sdy, ld)])
            qk_cast.append(fence("g", sp))
        wog_cast = []
        for gg in range(8):
            ld = op("y", lambda e, gg=gg: e.dma_start(
                r2(RCBT[0:64, :], 1024),
                bass.AP(d_wog.tensor, gg * 2 * 64 * 1024,
                        [[1024, 64], [65536, 2], [1, 1024]])),
                waits=[Wv(sp, qk_cast[-1] if gg == 0 else wog_cast[-1])],
                inc=sdy, by=16)
            op("g", lambda e, gg=gg: e.tensor_copy(
                WOG[:, gg * 2048:gg * 2048 + 2048], RCBT[0:64, :]),
                waits=[Wv(sdy, ld)])
            wog_cast.append(fence("g", sp))
        vsb_ones = op("v", lambda e: e.memset(
            r3(VSB[:, :], 16, 65)[:, :, :, 64:65], 1.0), waits=bar, inc=sv)

        xrq_load = {0: op("y", lambda e: e.dma_start(
            r2(XRQ[:, :], 512), d_xdr[:, :, 0:512]),
            waits=[Wv(sdy, spill)], inc=sdy, by=16)}

        qkv3 = r3(QKV[:, :], HQ, 2048)
        vsb4 = r3(VSB[:, :], 16, 65)
        xlq3 = r2(XLQ[:, :], 512)
        wqk4 = r3(WQTQK[:, :], 16, 128)
        vals3 = r2(QKV[0:64, 0:8192], 2048)

        qkv_last = 0
        last_wo_ev = 0
        for q in range(NQ):
            war = [Wv(sdy, xrq_load[q]), Wv(sp, bvr_c)]
            if q > 0:
                war.append(Wv(ts, qkv_last))
            lnd = emit_ln(r2(XRQ[:, :], 512), lambda dc: xlq3[:, dc, :],
                          LNSQA, MBRBA, False, war)
            pq_banks = [SPA[64:128, 0:512], SPA[64:128, 512:1024],
                        SPB[64:128, 0:512], SPB[64:128, 512:1024]]
            ev_cnt = {}
            for g in range(16):
                for qk in range(2):
                    i = g * 2 + qk
                    pqk = pq_banks[i % 4]
                    cnt = 0
                    for dc in range(NDC):
                        waits = []
                        if dc == 0:
                            waits = [Wv(sv, lnd["v"]), Wv(sp, lnd["p"]),
                                     Wv(sp, qk_cast[-1])]
                            if i >= 4:
                                c, wch = ev_cnt[i - 4]
                                waits.append(Wv(ss if wch == 0 else sv, c))
                        cnt = op("t", lambda e, g=g, qk=qk, dc=dc, pqk=pqk:
                                 e.matmul(pqk,
                                          wqk4[:, dc, g, qk * 64:qk * 64 + 64],
                                          xlq3[:, dc, :], start=(dc == 0),
                                          stop=(dc == 7), tile_position=(0, 64)),
                                 waits=waits, inc=ts if dc == 7 else None)
                    dst = qkv3[64:128, qk, :, g * 128:g * 128 + 128]
                    src = r2(pqk, 128)
                    if qk == 0:
                        c = op("s", lambda e, dst=dst, src=src, g=g: e.add(
                            dst, src, BQ2[64:128, g:g + 1]),
                            waits=[Wv(ts, cnt)], inc=ss)
                        ev_cnt[i] = (c, 0)
                    else:
                        c = op("v", lambda e, dst=dst, src=src, g=g: e.tensor_scalar(
                            dst, src, BK2[64:128, g:g + 1], None, ALU.add),
                            waits=[Wv(ts, cnt)], inc=sv)
                        ev_cnt[i] = (c, 1)
            qkv_ev_s = ev_cnt[30][0]
            qkv_ev_v = ev_cnt[31][0]
            pv_of = [SPA[:, 0:1024], SPB[:, 0:1024], VP[:, 0:1024], VP[:, 1024:2048]]
            wv_dma = {}
            vcast = {}
            vmm = {}
            for dc in range(NDC):
                wl = [Wv(ts, vmm.get((3, dc - 2), 0))] if dc >= 2 else \
                    [Wv(ss, qkv_ev_s), Wv(sv, qkv_ev_v)]
                wv_dma[dc] = op("y", lambda e, dc=dc: e.dma_start(
                    RCBT[:, (dc % 2) * 1024:(dc % 2) * 1024 + 1024],
                    d_wv[:, dc, :]), waits=wl, inc=sdy, by=16)
                op("g", lambda e, dc=dc: e.tensor_copy(
                    r2(WQTVC[:, :], 1024)[:, dc % 2, :],
                    RCBT[:, (dc % 2) * 1024:(dc % 2) * 1024 + 1024]),
                    waits=[Wv(sdy, wv_dma[dc])])
                vcast[dc] = fence("g", sp)
                for h in range(HQ):
                    pv = pv_of[h]
                    cnt = 0
                    for half in range(2):
                        waits = []
                        if half == 0:
                            waits.append(Wv(sp, vcast[dc]))
                            if dc == 0:
                                waits.append(Wv(ss, qkv_ev_s))
                                waits.append(Wv(sv, qkv_ev_v))
                        cnt = op("t", lambda e, h=h, dc=dc, half=half, pv=pv:
                                 e.matmul(pv[:, half * 512:half * 512 + 512],
                                          xlq3[:, dc, h * 128:h * 128 + 128],
                                          r2(WQTVC[:, :], 1024)[
                                              :, dc % 2,
                                              half * 512:half * 512 + 512],
                                          start=(dc == 0), stop=False),
                                 waits=waits,
                                 inc=ts if (half == 1 and h == HQ - 1) else None)
                    vmm[(h, dc)] = cnt
            vev = {}
            for h in range(HQ):
                pv = pv_of[h]
                cnt = 0
                for half in range(2):
                    cnt = op("t", lambda e, h=h, half=half, pv=pv: e.matmul(
                        pv[:, half * 512:half * 512 + 512], ONESB[:, :],
                        BVR[0:1, half * 512:half * 512 + 512],
                        start=False, stop=True),
                        waits=[Wv(sp, bvr_c)] if half == 0 else (),
                        inc=ts if half == 1 else None)
                op("v", lambda e, h=h, pv=pv: e.tensor_copy(
                    vsb4[:, h, :, 0:64], r2(pv, 64)),
                    waits=[Wv(ts, cnt), Wv(sv, vsb_ones)])
                vev[h] = fence("v", sv)
            exp_cnt = {}
            scores_cnt = {}
            attnv_cnt = {}
            dance_done = {}
            for h in range(HQ):
                units = [(t1h, g2) for t1h in range(2) for g2 in range(16)]
                for ui, (t1h, g2) in enumerate(units):
                    u = ui
                    sp_t = SPA if u % 2 == 0 else SPB
                    spu = sp_t[:, 0:1024]
                    cnt = 0
                    for kk in range(2):
                        waits = []
                        if kk == 0:
                            if (h, u - 2) in exp_cnt:
                                waits.append(Wv(ss, exp_cnt[(h, u - 2)]))
                            elif (h - 1, u + 30) in exp_cnt:
                                waits.append(Wv(ss, exp_cnt[(h - 1, u + 30)]))
                            else:
                                waits.append(Wv(sv, vev[HQ - 1]))
                        cnt = op("t", lambda e, h=h, t1h=t1h, g2=g2, kk=kk, spu=spu:
                                 e.matmul(
                                     spu[:, kk * 512:kk * 512 + 512],
                                     qkv3[64:128, 1, h, g2 * 128:g2 * 128 + 128],
                                     qkv3[64:128, 0, h, t1h * 1024 + kk * 512:
                                          t1h * 1024 + kk * 512 + 512],
                                     start=True, stop=True),
                                 waits=waits, inc=ts if kk == 1 else None)
                    scores_cnt[(h, u)] = cnt
                    exp_cnt[(h, u)] = op(
                        "s", lambda e, spu=spu, u=u: e.activation(
                            ES[:, (u % 2) * 1024:(u % 2) * 1024 + 1024], spu,
                            AF.Exp, scale=0.125),
                        waits=[Wv(ts, cnt)], inc=ss)
                    if ui >= 1:
                        emit_attnv(h, units[ui - 1][0], units[ui - 1][1],
                                   exp_cnt, attnv_cnt, dance_done, vev)
                    if ui == len(units) - 1:
                        emit_attnv(h, t1h, g2, exp_cnt, attnv_cnt, dance_done, vev)
                        for th in range(2):
                            emit_dance(h, th, attnv_cnt, dance_done)
            wo_ev = {}
            for oc in range(NDC):
                pwo = [SPA[:, 0:512], SPA[:, 512:1024], SPB[:, 0:512],
                       SPB[:, 512:1024]][oc % 4]
                cnt = 0
                for g1 in range(16):
                    waits = []
                    if g1 == 0:
                        waits = [Wv(sv, dance_done[(HQ - 1, 1)]),
                                 Wv(sp, wog_cast[-1])]
                        if oc >= 4:
                            waits.append(Wv(sv, wo_ev[oc - 4]))
                    cnt = op("t", lambda e, oc=oc, g1=g1, pwo=pwo: e.matmul(
                        pwo, WOG[:, g1 * 1024 + oc * 128:g1 * 1024 + oc * 128 + 128],
                        vals3[:, :, g1 * 128:g1 * 128 + 128],
                        start=(g1 == 0), stop=(g1 == 15)),
                        waits=waits, inc=ts if g1 == 15 else None)
                op("v", lambda e, oc=oc, pwo=pwo: e.scalar_tensor_tensor(
                    r2(XRQ[:, :], 512)[:, oc, :], pwo, BO[:, oc:oc + 1],
                    r2(XRQ[:, :], 512)[:, oc, :], ALU.add, ALU.add),
                    waits=[Wv(ts, cnt)])
                wo_ev[oc] = fence("v", sv)
            last_wo_ev = wo_ev[NDC - 1]
            wb = op("y", lambda e, q=q: e.dma_start(
                d_xdr[:, :, q * 512:q * 512 + 512], r2(XRQ[:, :], 512)),
                waits=[Wv(sv, wo_ev[NDC - 1])], inc=sdy, by=16)
            if q + 1 < NQ:
                xrq_load[q + 1] = op("y", lambda e, q=q: e.dma_start(
                    r2(XRQ[:, :], 512), d_xdr[:, :, (q + 1) * 512:(q + 1) * 512 + 512]),
                    waits=[Wv(sdy, wb)], inc=sdy, by=16)
            qkv_last = ts.n
        return {"wb_final": sdy.n, "last_v": last_wo_ev}

    def emit_attnv(h, t1h, g2, exp_cnt, attnv_cnt, dance_done, vev):
        u = t1h * 16 + g2
        esl = ES[:, (u % 2) * 1024:(u % 2) * 1024 + 1024]
        vph = VP[0:65, t1h * 1024:t1h * 1024 + 1024]
        vsb4 = r3(VSB[:, :], 16, 65)
        cnt = 0
        for kk in range(2):
            waits = []
            if kk == 0:
                waits.append(Wv(ss, exp_cnt[(h, u)]))
                if g2 == 0:
                    dk = dance_done.get((h - 1, t1h))
                    if dk is not None:
                        waits.append(Wv(sv, dk))
                    else:
                        waits.append(Wv(sv, vev[HQ - 1]))
            cnt = op("t", lambda e, h=h, g2=g2, kk=kk, vph=vph, esl=esl: e.matmul(
                vph[:, kk * 512:kk * 512 + 512], vsb4[:, h, g2, :],
                esl[:, kk * 512:kk * 512 + 512],
                start=(g2 == 0), stop=(g2 == 15)),
                waits=waits, inc=ts if (kk == 1 and g2 == 15) else None)
        if g2 == 15:
            attnv_cnt[(h, t1h)] = cnt

    _dprev = {}

    def emit_dance(h, t1h, attnv_cnt, dance_done):
        t0 = t1h * 1024
        op("v", lambda e, t0=t0: e.tensor_copy(
            RCBT[64:65, t0:t0 + 1024], VP[64:65, t0:t0 + 1024]),
            waits=[Wv(ts, attnv_cnt[(h, t1h)])])
        cp = fence("v", sv)
        d1_ = op("g", lambda e, t0=t0: e.dma_start(
            d_s1[0:1, t0:t0 + 1024], RCBT[64:65, t0:t0 + 1024]),
            waits=[Wv(sv, cp), Wv(sdg, _dprev.get(("s1", t0), 0))], inc=sdg, by=16)
        d2_ = op("g", lambda e, t0=t0: e.dma_start(
            RTRC[:, 0:8], bass.AP(d_s1.tensor, t0, [[8, 128], [1, 8]])),
            waits=[Wv(sdg, d1_)], inc=sdg, by=16)
        op("v", lambda e: e.reciprocal(RTRC[:, 16:24], RTRC[:, 0:8]),
                waits=[Wv(sdg, d2_)])
        rc = fence("v", sv)
        d3_ = op("g", lambda e, t0=t0: e.dma_start(
            bass.AP(d_s2.tensor, t0, [[8, 128], [1, 8]]), RTRC[:, 16:24]),
            waits=[Wv(sv, rc), Wv(sdg, _dprev.get(("s2", t0), 0))], inc=sdg, by=16)
        d4_ = op("g", lambda e, t0=t0: e.dma_start(
            RCBT[0:64, t0:t0 + 1024],
            bass.AP(d_s2.tensor, t0, [[0, 64], [1, 1024]])),
            waits=[Wv(sdg, d3_)], inc=sdg, by=16)
        _dprev[("s1", t0)] = d2_
        _dprev[("s2", t0)] = d4_
        op("v", lambda e, h=h, t0=t0: e.tensor_tensor(
            QKV[0:64, h * 2048 + t0:h * 2048 + t0 + 1024],
            VP[0:64, t0:t0 + 1024], RCBT[0:64, t0:t0 + 1024], ALU.mult),
            waits=[Wv(sdg, d4_)])
        dance_done[(h, t1h)] = fence("v", sv)

    ffn1 = emit_ffn(0, xdma)
    attn = emit_attn(ffn1)
    bar2 = [Wv(ts, ts.n), Wv(ss, ss.n), Wv(sv, sv.n), Wv(sp, sp.n), Wv(sdg, sdg.n)]
    xrl = op("y", lambda e: e.dma_start(r2(XR[:, :], T), d_xdr[:, :, :]),
             waits=[*bar2, Wv(sdy, attn["wb_final"])], inc=sdy, by=16)
    ffn2 = emit_ffn(1, [xrl, xrl, xrl, xrl])
    for dc in range(NDC):
        op("y", lambda e, dc=dc: e.dma_start(
            d_y[:, dc, :], r2(XR[:, :], T)[:, dc, :]),
            waits=[Wv(sv, ffn2["last_v"])] if dc == 0 else (), inc=sdy, by=16)
    fin_y = sdy.n
    op("y", lambda e: e.wait_ge(sdy.h, fin_y))
    op("g", lambda e: e.wait_ge(sdy.h, fin_y))

    with nc.Block() as block:
        @block.tensor
        def _(e):
            for eng, fn in ops:
                if eng == "t":
                    fn(e)

        @block.scalar
        def _(e):
            for eng, fn in ops:
                if eng == "s":
                    fn(e)

        @block.vector
        def _(e):
            for eng, fn in ops:
                if eng == "v":
                    fn(e)

        @block.gpsimd
        def _(e):
            for eng, fn in ops:
                if eng == "g":
                    fn(e)

        @block.sync
        def _(e):
            for eng, fn in ops:
                if eng == "y":
                    fn(e)

    es.close()
    return nc


def _prep(kw):
    x = kw["L0_attn_x"]
    wqkv = kw["L0_attn_Wqkv"]
    bqkv = kw["L0_attn_bqkv"]
    wo = kw["L0_attn_Wo"]
    bo = kw["L0_attn_bo"]
    wbase = kw["L0_ffn_W_base"]
    bbase = kw["L0_ffn_b_base"]
    wd1 = kw["L0_ffn_Wd1"]
    bd1 = kw["L0_ffn_bd1"]
    wd2 = kw["L0_ffn_Wd2"]
    bd2 = kw["L0_ffn_bd2"]
    wadn = kw["L0_ffn_Wa_down"]
    badn = kw["L0_ffn_ba_down"]
    waup = kw["L0_ffn_Wa_up"]
    baup = kw["L0_ffn_ba_up"]

    f = np.float32

    def c(a):
        return np.ascontiguousarray(np.asarray(a), dtype=f)

    shared = {}
    shared["wd1"] = c(np.asarray(wd1).reshape(8, 8, 128, 256)
                      .transpose(0, 2, 1, 3).reshape(8, 128, 2048))
    shared["wd2"] = c(np.asarray(wd2).reshape(8, 2, 128, 8, 128)
                      .transpose(0, 2, 1, 3, 4).reshape(8, 128, 2048))
    shared["wbase"] = c(np.asarray(wbase).reshape(8, 128, 8, 128)
                        .transpose(2, 1, 0, 3).reshape(8, 128, 1024))
    shared["wadn"] = c(np.asarray(wadn).reshape(8, 8, 128, 64)
                       .transpose(0, 2, 1, 3).reshape(8, 128, 512))
    shared["waup"] = c(waup)
    wq4 = np.asarray(wqkv).reshape(16, 3, 64, 8, 128)
    shared["wqk"] = c(wq4[:, 0:2].transpose(4, 3, 0, 1, 2).reshape(128, 8, 2048))
    shared["wv"] = c(wq4[:, 2].transpose(3, 2, 0, 1).reshape(128, 8, 1024))
    shared["wog"] = c(np.asarray(wo).T.reshape(16, 64, 1024))
    shared["nid"] = c(-np.eye(128))
    shared["bd1"] = c(np.asarray(bd1).reshape(8, 2, 128).transpose(2, 0, 1)
                      .reshape(128, 16))
    shared["bd2"] = c(np.asarray(bd2).reshape(8, 8, 128).transpose(2, 0, 1)
                      .reshape(128, 64))
    shared["bb"] = c(np.asarray(bbase).reshape(8, 128).T)
    shared["bo"] = c(np.asarray(bo).reshape(8, 128).T)
    shared["bad"] = c(np.asarray(badn).T)
    shared["bau"] = c(np.asarray(baup).reshape(8, 8, 128).transpose(2, 1, 0)
                      .reshape(128, 64))
    shared["pw8"] = c((2.0 ** -np.arange(8)).reshape(1, 8))
    bqr = np.asarray(bqkv).reshape(16, 3, 64)
    bq = np.zeros((128, 16), f)
    bk = np.zeros((128, 16), f)
    bq[64:128, :] = bqr[:, 0, :].T
    bk[64:128, :] = bqr[:, 1, :].T
    shared["bq2"] = bq
    shared["bk2"] = bk
    shared["bvr"] = c(bqr[:, 2, :].reshape(1, 1024))

    in_maps = []
    for ci in range(NC):
        m = dict(shared)
        m["x"] = c(np.asarray(x[ci]).reshape(2048, 8, 128).transpose(2, 1, 0))
        in_maps.append(m)
    return in_maps


def kernel(**kw):
    if "nc" not in _CACHE:
        _CACHE["nc"] = build_program()
    nc = _CACHE["nc"]
    in_maps = _prep(kw)
    res = run_bass_kernel_spmd(nc, in_maps, core_ids=list(range(NC)))
    outs = []
    for ci in range(NC):
        ycm = np.asarray(res.results[ci]["y"])
        outs.append(ycm.transpose(2, 1, 0).reshape(2048, 1024))
    return np.stack(outs, axis=0).astype(np.float32)


# version 20
# speedup vs baseline: 1.0043x; 1.0043x over previous
import contextlib
import numpy as np
import concourse.bass as bass
import concourse.mybir as mybir
from concourse.bass_utils import run_bass_kernel_spmd

F32 = mybir.dt.float32
F32R = mybir.dt.float32r
BF16 = mybir.dt.bfloat16
AF = mybir.ActivationFunctionType
ALU = mybir.AluOpType
AX = mybir.AxisListType

NC = 8
T = 2048
NDC = 8
NTB = 4
TBS = 512
NE = 8
HQ = 4
NQ = 4
EPS = 1e-5

_CACHE = {}


class Ctr:
    __slots__ = ("h", "n")

    def __init__(self, h):
        self.h = h
        self.n = 0


def r2(ap, b):
    return ap.rearrange("p (a b) -> p a b", b=b)


def r3(ap, b, c):
    return ap.rearrange("p (a b c) -> p a b c", b=b, c=c)


def build_program():
    nc = bass.Bass("TRN2", target_bir_lowering=False)

    d_x = nc.dram_tensor("x", [128, NDC, T], F32, kind="ExternalInput").ap()
    d_y = nc.dram_tensor("y", [128, NDC, T], F32, kind="ExternalOutput").ap()
    d_wd1 = nc.dram_tensor("wd1", [NE, 128, 2048], F32R, kind="ExternalInput").ap()
    d_wd2 = nc.dram_tensor("wd2", [NE, 128, 2048], F32R, kind="ExternalInput").ap()
    d_wbase = nc.dram_tensor("wbase", [NDC, 128, 1024], F32, kind="ExternalInput").ap()
    d_wadn = nc.dram_tensor("wadn", [NE, 128, 512], F32, kind="ExternalInput").ap()
    d_waup = nc.dram_tensor("waup", [NE, 64, 1024], F32, kind="ExternalInput").ap()
    d_wqk = nc.dram_tensor("wqk", [128, NDC, 2048], F32, kind="ExternalInput").ap()
    d_wv = nc.dram_tensor("wv", [128, NDC, 1024], F32, kind="ExternalInput").ap()
    d_wog = nc.dram_tensor("wog", [16, 64, 1024], F32, kind="ExternalInput").ap()
    d_nid = nc.dram_tensor("nid", [128, 128], F32R, kind="ExternalInput").ap()
    d_bd1 = nc.dram_tensor("bd1", [128, 16], F32, kind="ExternalInput").ap()
    d_bd2 = nc.dram_tensor("bd2", [128, 64], F32, kind="ExternalInput").ap()
    d_bb = nc.dram_tensor("bb", [128, 8], F32, kind="ExternalInput").ap()
    d_bo = nc.dram_tensor("bo", [128, 8], F32, kind="ExternalInput").ap()
    d_bad = nc.dram_tensor("bad", [64, 8], F32, kind="ExternalInput").ap()
    d_bau = nc.dram_tensor("bau", [128, 64], F32, kind="ExternalInput").ap()
    d_pw8 = nc.dram_tensor("pw8", [1, 8], F32, kind="ExternalInput").ap()
    d_bq2 = nc.dram_tensor("bq2", [128, 16], F32, kind="ExternalInput").ap()
    d_bk2 = nc.dram_tensor("bk2", [128, 16], F32, kind="ExternalInput").ap()
    d_bvr = nc.dram_tensor("bvr", [1, 1024], F32, kind="ExternalInput").ap()
    d_xdr = nc.dram_tensor("xdr", [128, NDC, T], F32, kind="Internal").ap()
    d_s1 = nc.dram_tensor("dsc1", [1, 2048], F32, kind="Internal").ap()
    d_s2 = nc.dram_tensor("dsc2", [1, 2048], F32, kind="Internal").ap()
    d_sb = nc.dram_tensor("dscb", [1, 1024], BF16, kind="Internal").ap()
    d_s3 = nc.dram_tensor("dsc3", [1, 64], F32, kind="Internal").ap()

    SB0 = 16512
    _A = nc.alloc_sbuf_tensor_at

    def A(name, shape, dtype, offset):
        return _A(name, shape, dtype, offset=SB0 + offset)
    NID = A("NID", [128, 128], F32R, offset=0)
    BD1 = A("BD1", [128, 16], F32, offset=512)
    BD2 = A("BD2", [128, 64], F32, offset=576)
    BB = A("BB", [128, 8], F32, offset=832)
    BBH = A("BBH", [128, 8], F32, offset=864)
    BO = A("BO", [128, 8], F32, offset=896)
    BAU = A("BAU", [128, 64], F32, offset=928)
    BAD = A("BAD", [64, 8], F32, offset=1184)
    BADS = A("BADS", [64, 8], F32, offset=1216)
    OH = A("OH", [128, 8], F32, offset=1248)
    BCT = A("BCT", [128, 8], F32, offset=1280)
    BCOMB = A("BCOMB", [128, 8], F32, offset=1312)
    PW8 = A("PW8", [1, 8], F32, offset=1344)
    LS = A("LS", [1, 64], F32, offset=1376)
    LA = A("LA", [128, 256], F32, offset=1632)
    LP = A("LP", [128, 8], F32, offset=2656)
    ONESC = A("ONESC", [128, 16], BF16, offset=2688)
    ONES1C = A("ONES1C", [128, 8], F32, offset=2720)
    ONESB = A("ONESB", [1, 128], BF16, offset=2752)
    LROW = A("LROW", [1, 1024], F32, offset=3008)
    LROWB = A("LROWB", [1, 1024], BF16, offset=7104)
    BQ2 = A("BQ2", [128, 16], F32, offset=9152)
    BK2 = A("BK2", [128, 16], F32, offset=9216)
    BVR = A("BVR", [1, 1024], BF16, offset=9280)
    RTRC = A("RTRC", [128, 32], F32, offset=11328)
    LROW2 = A("LROW2", [1, 1024], F32, offset=206848)
    LROWB2 = A("LROWB2", [1, 1024], BF16, offset=9280)
    EPSC = A("EPSC", [1, 8], F32, offset=11456)
    FEN = A("FEN", [1, 16], F32, offset=11488)
    XR = A("XR", [128, NDC * T], F32, offset=12288)
    XL = A("XL", [128, NDC * T], BF16, offset=77824)
    XLR = A("XLR", [128, 4096], F32R, offset=110592)
    XLR2 = A("XLR2", [128, 4096], F32R, offset=190464)
    WD1 = A("WD1", [128, 4096], F32R, offset=126976)
    WD2 = A("WD2", [128, 4096], F32R, offset=143360)
    HBR = A("HBR", [128, 1024], F32R, offset=159744)
    ASB = A("ASB", [64, 2048], BF16, offset=163840)
    LNSQ = A("LNSQ", [128, 4096], BF16, offset=167936)
    MBRB = A("MBRB", [128, 1024], BF16, offset=176128)
    SQSC = A("SQSC", [128, 512], BF16, offset=178176)
    WASL = A("WASL", [128, 1024], F32, offset=179200)
    WBC = A("WBC", [128, 2048], BF16, offset=183296)
    WDSEL = A("WDSEL", [128, 512], BF16, offset=187392)
    WAUS = A("WAUS", [64, 1024], BF16, offset=188416)
    WQTQK = A("WQTQK", [128, 16384], BF16, offset=12288)
    QKV = A("QKV", [128, 16384], BF16, offset=45056)
    XRQ = A("XRQ", [128, 4096], F32, offset=77824)
    XLQ = A("XLQ", [128, 4096], BF16, offset=94208)
    VSB = A("VSB", [128, 4160], BF16, offset=102400)
    WQTVC = A("WQTVC", [128, 2048], BF16, offset=110720)
    ES = A("ES", [128, 2048], BF16, offset=114816)
    RCBT = A("RCBT", [128, 2048], F32, offset=118912)
    WOG = A("WOG", [64, 16384], BF16, offset=127104)
    LNSQA = A("LNSQA", [128, 4096], BF16, offset=159872)
    MBRBA = A("MBRBA", [128, 1024], BF16, offset=168064)

    SPA = nc.alloc_psum_tensor("SPA", [128, 1024], F32)
    SPB = nc.alloc_psum_tensor("SPB", [128, 1024], F32)
    VP = nc.alloc_psum_tensor("VP", [128, 2048], F32)

    es = contextlib.ExitStack()
    sdy = Ctr(es.enter_context(nc.semaphore("sdy")))
    sdg = Ctr(es.enter_context(nc.semaphore("sdg")))
    ts = Ctr(es.enter_context(nc.semaphore("ts")))
    ss = Ctr(es.enter_context(nc.semaphore("ss")))
    sv = Ctr(es.enter_context(nc.semaphore("sv")))
    sp = Ctr(es.enter_context(nc.semaphore("sp")))

    ops = []

    def op(eng, bf, waits=(), inc=None, by=1):
        wts = tuple((c, v) for (c, v) in waits if v > 0)
        if inc is not None:
            inc.n += by
        tgt = inc.n if inc is not None else 0

        def fn(e, bf=bf, wts=wts, inc=inc, by=by):
            for (c, val) in wts:
                e.wait_ge(c.h, val)
            ins = bf(e)
            if inc is not None:
                ins.then_inc(inc.h, by)
        ops.append((eng, fn))
        return tgt

    def Wv(c, v):
        return (c, v)

    def drain(eng):
        ops.append((eng, lambda e: e.drain()))

    _fsl = {"v": 0, "g": 2, "s": 4}

    def fence(eng, sem):
        drain(eng)
        i = _fsl[eng]
        if eng == "s":
            return op(eng, lambda e: e.square(FEN[0:1, i:i + 1], FEN[0:1, i:i + 1]),
                      inc=sem)
        return op(eng, lambda e: e.tensor_copy(FEN[0:1, i:i + 1], FEN[0:1, i + 1:i + 2]),
                  inc=sem)

    xdma = []
    for tb in range(NTB):
        xdma.append(op("y", lambda e, tb=tb: e.dma_start(
            r2(XR[:, :], T)[:, :, tb * TBS:(tb + 1) * TBS],
            d_x[:, :, tb * TBS:(tb + 1) * TBS]), inc=sdy, by=16))
    for dst, src in [(NID[:, :], d_nid), (BD1[:, :], d_bd1), (BD2[:, :], d_bd2),
                     (BB[:, :], d_bb), (BO[:, :], d_bo), (BAD[:, :], d_bad),
                     (BAU[:, :], d_bau), (PW8[:, :], d_pw8), (BQ2[:, :], d_bq2),
                     (BK2[:, :], d_bk2)]:
        op("y", lambda e, dst=dst, src=src: e.dma_start(dst, src), inc=sdy, by=16)
    consts_loaded = sdy.n
    op("v", lambda e: e.memset(ONESC[:, 0:1], 1.0 / 1024.0), inc=sv)
    op("v", lambda e: e.memset(ONES1C[:, 0:1], 1.0), inc=sv)
    op("v", lambda e: e.memset(ONESB[:, :], 1.0), inc=sv)
    op("v", lambda e: e.memset(EPSC[0:1, :], EPS), inc=sv)
    init_v = op("v", lambda e: e.tensor_scalar_mul(BBH[:, :], BB[:, :], 0.5),
                waits=[Wv(sdy, consts_loaded)], inc=sv)

    def emit_ln(xr3, xl3, lnsq, mbrb, want_xlr, war_wait, xlr3=None, lrow=None, lrowb=None):
        PM = VP[0:1, 1024:1536]
        PSQ = VP[0:1, 1536:2048]
        if lrow is None:
            lrow, lrowb = LROW, LROWB
        lnsq3 = r2(lnsq[:, :], 512)
        cast_done = []
        for dc in range(NDC):
            cast_done.append(op("v", lambda e, dc=dc: e.tensor_copy(
                xl3(dc), xr3[:, dc, :]), waits=war_wait if dc == 0 else (), inc=sv))
        sq = op("s", lambda e: e.activation(lnsq3, xr3, AF.Square),
                waits=war_wait, inc=ss)
        for dc in range(NDC):
            op("t", lambda e, dc=dc: e.matmul(
                PM, ONESC[:, 0:1], xl3(dc), start=(dc == 0), stop=(dc == NDC - 1)),
                waits=[Wv(sv, cast_done[dc]), Wv(sv, init_v)])
        mm_done = 0
        for dc in range(NDC):
            last = dc == NDC - 1
            mm_done = op("t", lambda e, dc=dc: e.matmul(
                PSQ, ONESC[:, 0:1], lnsq3[:, dc, :],
                start=(dc == 0), stop=(dc == NDC - 1)),
                waits=[Wv(ss, sq)] if dc == 0 else (), inc=ts if last else None)
        op("v", lambda e: e.tensor_copy(lrow[0:1, 0:512], PM), waits=[Wv(ts, mm_done)])
        drain("v")
        op("v", lambda e: e.tensor_copy(lrowb[0:1, 0:512], lrow[0:1, 0:512]))
        drain("v")
        op("v", lambda e: e.tensor_mul(lrow[0:1, 512:1024], lrow[0:1, 0:512],
                                       lrow[0:1, 0:512]))
        drain("v")
        var = op("v", lambda e: e.tensor_tensor(lrow[0:1, 512:1024], PSQ,
                                                lrow[0:1, 512:1024], ALU.subtract),
                 inc=sv)
        op("s", lambda e: e.activation(lrow[0:1, 0:512], lrow[0:1, 512:1024],
                                       AF.Ln, bias=EPSC[0:1, 0:1]), waits=[Wv(sv, var)])
        drain("s")
        rstd = op("s", lambda e: e.activation(lrowb[0:1, 512:1024], lrow[0:1, 0:512],
                                              AF.Exp, scale=-0.5), inc=ss)
        sbw = op("g", lambda e, lrowb=lrowb: e.dma_start(d_sb, lrowb[0:1, :]),
                 waits=[Wv(ss, rstd)], inc=sdg, by=16)
        bc = op("g", lambda e: e.dma_start(
            mbrb[:, :], bass.AP(d_sb.tensor, 0, [[0, 128], [1, 1024]])),
            waits=[Wv(sdg, sbw)], inc=sdg, by=16)
        vdone = []
        pdone = []
        for dc in range(4):
            tg = (xlr3[:, dc, :] if want_xlr else xl3(dc))
            vdone.append(op("v", lambda e, dc=dc, tg=tg: e.tensor_tensor(
                tg, xr3[:, dc, :], mbrb[:, 0:512], ALU.subtract),
                waits=[Wv(sdg, bc), Wv(ts, mm_done)], inc=sv))
        for dc in range(4, 8):
            tg = (xlr3[:, dc, :] if want_xlr else xl3(dc))
            pdone.append(op("g", lambda e, dc=dc, tg=tg: e.tensor_tensor(
                tg, xr3[:, dc, :], mbrb[:, 0:512], ALU.subtract),
                waits=[Wv(sdg, bc), Wv(ts, mm_done)], inc=sp))
        mdone_p = mdone_v = 0
        for dc in range(4):
            tg = (xlr3[:, dc, :] if want_xlr else xl3(dc))
            mdone_p = op("g", lambda e, dc=dc, tg=tg: e.tensor_tensor(
                tg, tg, mbrb[:, 512:1024], ALU.mult),
                waits=[Wv(sv, vdone[dc])], inc=sp)
        for dc in range(4, 8):
            tg = (xlr3[:, dc, :] if want_xlr else xl3(dc))
            mdone_v = op("v", lambda e, dc=dc, tg=tg: e.tensor_tensor(
                tg, tg, mbrb[:, 512:1024], ALU.mult),
                waits=[Wv(sp, pdone[dc - 4])], inc=sv)
        vf = fence("v", sv)
        if want_xlr:
            for dc in range(NDC):
                op("g", lambda e, dc=dc: e.tensor_copy(
                    xl3(dc), xlr3[:, dc, :]),
                    waits=[Wv(sv, mdone_v)] if dc == 0 else ())
        pf = fence("g", sp)
        return {"v": vf, "p": pf}

    def emit_ffn(call, xwait):
        xr3t = r2(XR[:, :], T)
        xl3t = r2(XL[:, :], T)
        k_total = NTB * NE

        d1_end = {}
        d2_end = {}
        relu_end = {}
        sq_end = {}
        wd1_ld = {}
        wd2_ld = {}

        def xlr_of(tb):
            return r2((XLR if tb % 2 == 0 else XLR2)[:, :], 512)

        def issue_wd1(k, waits):
            e_ = k % NE
            wd1_ld[k] = op("y", lambda e, k=k, e_=e_: e.dma_start(
                WD1[:, (k % 2) * 2048:(k % 2) * 2048 + 2048], d_wd1[e_, :, :]),
                waits=waits, inc=sdy, by=16)

        def issue_wd2(k, waits):
            e_ = k % NE
            wd2_ld[k] = op("y", lambda e, k=k, e_=e_: e.dma_start(
                WD2[:, (k % 2) * 2048:(k % 2) * 2048 + 2048], d_wd2[e_, :, :]),
                waits=waits, inc=sdy, by=16)

        issue_wd1(0, [Wv(sdy, consts_loaded)])
        issue_wd2(0, [])
        issue_wd1(1, [])
        issue_wd2(1, [])

        def emit_ln_tb(tb):
            war = [Wv(sdy, xwait[tb])]
            if tb >= 2:
                war.append(Wv(ts, d2_end[((tb - 2) * NE + NE - 1, NDC - 1)]))
            return emit_ln(
                xr3t[:, :, tb * TBS:(tb + 1) * TBS],
                lambda dc, tb=tb: xl3t[:, dc, tb * TBS:(tb + 1) * TBS],
                LNSQ, MBRB, True, war, xlr_of(tb),
                LROW if tb % 2 == 0 else LROW2,
                LROWB if tb % 2 == 0 else LROWB2)

        ln_done = {0: emit_ln_tb(0)}
        for tb in range(NTB):
            xlr3 = xlr_of(tb)
            for e_ in range(NE):
                k = tb * NE + e_
                if e_ == 1 and tb + 1 < NTB:
                    ln_done[tb + 1] = emit_ln_tb(tb + 1)
                wd1k = r2(WD1[:, (k % 2) * 2048:(k % 2) * 2048 + 2048], 256)
                wd2k = r2(WD2[:, (k % 2) * 2048:(k % 2) * 2048 + 2048], 1024)
                for hc in range(2):
                    pd1 = SPA[:, hc * 512:hc * 512 + 512]
                    cnt = 0
                    for dc in range(NDC):
                        waits = []
                        if dc == 0:
                            waits = [Wv(sdy, wd1_ld[k]),
                                     Wv(sv, ln_done[tb]["v"]),
                                     Wv(sp, ln_done[tb]["p"])]
                            if k >= 1:
                                waits.append(Wv(ss, relu_end.get((k - 1, hc), 0)))
                                waits.append(Wv(ts, d2_end[(k - 1, NDC - 1)]))
                        cnt = op("t", lambda e, dc=dc, hc=hc, pd1=pd1, wd1k=wd1k,
                                 xlr3=xlr3:
                                 e.matmul(pd1, wd1k[:, dc, hc * 128:hc * 128 + 128],
                                          xlr3[:, dc, :],
                                          start=(dc == 0), stop=(dc == 7)),
                                 waits=waits, inc=ts if dc == 7 else None)
                    d1_end[k] = cnt
                    relu_end[(k, hc)] = op(
                        "s", lambda e, hc=hc, pd1=pd1, e_=e_: e.activation(
                            HBR[:, hc * 512:hc * 512 + 512], pd1, AF.Relu,
                            bias=BD1[:, e_ * 2 + hc:e_ * 2 + hc + 1]),
                        waits=[Wv(ts, cnt)], inc=ss)
                if k + 2 < k_total:
                    issue_wd1(k + 2, [Wv(ts, d1_end[k])])
                for oc in range(NDC):
                    pd2 = SPB[:, (oc % 2) * 512:(oc % 2) * 512 + 512]
                    waits = [Wv(ss, relu_end[(k, 1)]), Wv(sdy, wd2_ld[k])]
                    if (k, oc - 2) in sq_end:
                        waits.append(Wv(ss, sq_end[(k, oc - 2)]))
                    elif (k - 1, oc + 6) in sq_end:
                        waits.append(Wv(ss, sq_end[(k - 1, oc + 6)]))
                    for hc in range(2):
                        op("t", lambda e, hc=hc, oc=oc, pd2=pd2, wd2k=wd2k:
                           e.matmul(pd2, wd2k[:, hc, oc * 128:oc * 128 + 128],
                                    HBR[:, hc * 512:hc * 512 + 512],
                                    start=(hc == 0), stop=False),
                           waits=waits if hc == 0 else ())
                    cnt = op("t", lambda e, oc=oc, pd2=pd2, xlr3=xlr3: e.matmul(
                        pd2, NID[:, :], xlr3[:, oc, :], start=False, stop=True),
                        inc=ts)
                    d2_end[(k, oc)] = cnt
                    col = e_ * 32 + tb * 8 + oc
                    sq_end[(k, oc)] = op(
                        "s", lambda e, pd2=pd2, e_=e_, oc=oc, col=col: e.activation(
                            SQSC[:, :], pd2, AF.Square,
                            bias=BD2[:, e_ * 8 + oc:e_ * 8 + oc + 1],
                            accum_out=LA[:, col:col + 1]),
                        waits=[Wv(ts, cnt)], inc=ss)
                if k + 2 < k_total:
                    issue_wd2(k + 2, [Wv(ts, d2_end[(k, NDC - 1)])])

        wb_dma = {}
        wb_cast = {}
        bu_end = {}
        ev_b = {}
        wb_dma[0] = op("y", lambda e: e.dma_start(WASL[:, :], d_wbase[0, :, :]),
                       waits=[Wv(sdy, wd2_ld[k_total - 1])], inc=sdy, by=16)
        vp_banks = [VP[:, i * 512:i * 512 + 512] for i in range(4)]
        for oc in range(NDC):
            wl = [Wv(sdy, wb_dma[oc])]
            if oc >= 2:
                wl.append(Wv(ts, bu_end[(oc - 2, 3)]))
            op("g", lambda e, oc=oc: e.tensor_scalar_mul(
                WBC[:, (oc % 2) * 1024:(oc % 2) * 1024 + 1024], WASL[:, :], 0.5),
                waits=wl)
            wb_cast[oc] = fence("g", sp)
            if oc + 1 < NDC:
                wb_dma[oc + 1] = op("y", lambda e, oc=oc: e.dma_start(
                    WASL[:, :], d_wbase[oc + 1, :, :]),
                    waits=[Wv(sp, wb_cast[oc])], inc=sdy, by=16)
            for tt in range(4):
                j = oc * 4 + tt
                pbu = vp_banks[j % 4]
                waits = [Wv(sp, wb_cast[oc])]
                if j >= 4:
                    waits.append(Wv(sv, ev_b[j - 4]))
                else:
                    waits.append(Wv(sv, ln_done[NTB - 1]["v"]))
                cnt = 0
                for dc in range(NDC):
                    cnt = op("t", lambda e, oc=oc, tt=tt, dc=dc, pbu=pbu: e.matmul(
                        pbu, r2(WBC[:, (oc % 2) * 1024:(oc % 2) * 1024 + 1024],
                                128)[:, dc, :],
                        xl3t[:, dc, tt * 512:tt * 512 + 512],
                        start=(dc == 0), stop=(dc == 7)),
                        waits=waits if dc == 0 else (),
                        inc=ts if dc == 7 else None)
                bu_end[(oc, tt)] = cnt
                ev_b[j] = op("v", lambda e, oc=oc, tt=tt, pbu=pbu:
                             e.scalar_tensor_tensor(
                                 xr3t[:, oc, tt * 512:tt * 512 + 512], pbu,
                                 BBH[:, oc:oc + 1],
                                 xr3t[:, oc, tt * 512:tt * 512 + 512],
                                 ALU.add, ALU.add),
                             waits=[Wv(ts, cnt)], inc=sv)

        drain("s")
        sfence = op("s", lambda e: e.square(LS[0:1, 48:49], LS[0:1, 48:49]),
                    waits=[Wv(ss, sq_end[(k_total - 1, NDC - 1)])], inc=ss)
        lpd = 0
        for e_ in range(NE):
            lpd = op("v", lambda e, e_=e_: e.tensor_reduce(
                LP[:, e_:e_ + 1], LA[:, e_ * 32:e_ * 32 + 32], AX.X, ALU.add),
                waits=[Wv(ss, sfence)] if e_ == 0 else (),
                inc=sv)
        lpd = fence("v", sv)
        pl_mm = op("t", lambda e: e.matmul(SPA[0:1, 0:8], ONES1C[:, 0:1], LP[:, :],
                                           start=True, stop=True),
                   waits=[Wv(sv, lpd), Wv(ss, relu_end[(k_total - 1, 1)])], inc=ts)
        op("v", lambda e: e.tensor_copy(LS[0:1, 0:8], SPA[0:1, 0:8]),
           waits=[Wv(ts, pl_mm)])
        drain("v")
        op("v", lambda e: e.tensor_reduce(LS[0:1, 8:9], LS[0:1, 0:8], AX.X, ALU.min))
        drain("v")
        op("v", lambda e: e.tensor_scalar(LS[0:1, 9:17], LS[0:1, 0:8], LS[0:1, 8:9],
                                          None, ALU.is_equal))
        drain("v")
        op("v", lambda e: e.tensor_mul(LS[0:1, 25:33], LS[0:1, 9:17], PW8[0:1, :]))
        drain("v")
        op("v", lambda e: e.tensor_reduce(LS[0:1, 33:34], LS[0:1, 25:33], AX.X,
                                          ALU.max))
        drain("v")
        op("v", lambda e: e.tensor_scalar(LS[0:1, 40:48], LS[0:1, 25:33],
                                          LS[0:1, 33:34], None, ALU.is_equal))
        oh_row = fence("v", sv)
        ohw = op("g", lambda e: e.dma_start(d_s3[0:1, 0:8], LS[0:1, 40:48]),
                 waits=[Wv(sv, oh_row)], inc=sdg, by=16)
        oh_bc = op("g", lambda e: e.dma_start(
            OH[:, :], bass.AP(d_s3.tensor, 0, [[0, 128], [1, 8]])),
            waits=[Wv(sdg, ohw)], inc=sdg, by=16)

        sel_d = 0
        dma_p = {}
        dma_p[0] = op("y", lambda e: e.dma_start(WASL[:, 0:512], d_wadn[0, :, :]),
                      waits=[Wv(sp, wb_cast[NDC - 1])], inc=sdy, by=16)
        dma_p[1] = op("y", lambda e: e.dma_start(WASL[:, 512:1024], d_wadn[1, :, :]),
                      inc=sdy, by=16)
        stt_d = {}
        for e_ in range(NE):
            waits = [Wv(sdy, dma_p[e_]), Wv(sdg, oh_bc)]
            if e_ == 0:
                stt_d[0] = op("v", lambda e: e.tensor_scalar(
                    WDSEL[:, :], WASL[:, 0:512], OH[:, 0:1], None, ALU.mult),
                    waits=waits, inc=sv)
            else:
                drain("v")
                stt_d[e_] = op("v", lambda e, e_=e_: e.scalar_tensor_tensor(
                    WDSEL[:, :], WASL[:, (e_ % 2) * 512:(e_ % 2) * 512 + 512],
                    OH[:, e_:e_ + 1], WDSEL[:, :], ALU.mult, ALU.add),
                    waits=waits, inc=sv)
            sel_d = stt_d[e_]
            if e_ == NE - 1:
                sel_d = fence("v", sv)
            if e_ + 2 < NE:
                dma_p[e_ + 2] = op("y", lambda e, e_=e_: e.dma_start(
                    WASL[:, (e_ % 2) * 512:(e_ % 2) * 512 + 512],
                    d_wadn[e_ + 2, :, :]), waits=[Wv(sv, stt_d[e_])], inc=sdy, by=16)
        sel_u = 0
        for e_ in range(NE):
            wl = [Wv(sv, sel_u)] if e_ >= 1 else [Wv(sv, sel_d)]
            ud = op("y", lambda e, e_=e_: e.dma_start(
                WASL[0:64, :], d_waup[e_, :, :]), waits=wl, inc=sdy, by=16)
            if e_ == 0:
                sel_u = op("v", lambda e: e.tensor_scalar(
                    WAUS[:, :], WASL[0:64, :], OH[0:64, 0:1], None, ALU.mult),
                    waits=[Wv(sdy, ud)], inc=sv)
            else:
                drain("v")
                sel_u = op("v", lambda e, e_=e_: e.scalar_tensor_tensor(
                    WAUS[:, :], WASL[0:64, :], OH[0:64, e_:e_ + 1], WAUS[:, :],
                    ALU.mult, ALU.add), waits=[Wv(sdy, ud)], inc=sv)
                if e_ == NE - 1:
                    sel_u = fence("v", sv)
        op("v", lambda e: e.tensor_mul(BADS[:, 0:1], BAD[:, 0:1], OH[0:64, 0:1]),
           waits=[Wv(sdg, oh_bc)])
        for e_ in range(1, NE):
            drain("v")
            op("v", lambda e, e_=e_: e.scalar_tensor_tensor(
                BADS[:, 0:1], BAD[:, e_:e_ + 1], OH[0:64, e_:e_ + 1], BADS[:, 0:1],
                ALU.mult, ALU.add))
        drain("v")
        op("v", lambda e: e.tensor_scalar_mul(BADS[:, 0:1], BADS[:, 0:1], 0.5))
        bads_done = fence("v", sv)
        for oc in range(NDC):
            op("v", lambda e, oc=oc: e.tensor_mul(
                BCT[:, 0:8], BAU[:, oc * 8:oc * 8 + 8], OH[:, :]),
                waits=[Wv(sdg, oh_bc)] if oc == 0 else ())
            drain("v")
            op("v", lambda e, oc=oc: e.tensor_reduce(BCOMB[:, oc:oc + 1], BCT[:, 0:8],
                                                     AX.X, ALU.add))
            drain("v")
        op("v", lambda e: e.tensor_scalar_mul(BCOMB[:, :], BCOMB[:, :], 0.5))
        bcomb_done = fence("v", sv)

        arelu = {}
        for tt in range(4):
            pda = SPA[0:64, (tt % 2) * 512:(tt % 2) * 512 + 512]
            cnt = 0
            for dc in range(NDC):
                waits = []
                if dc == 0:
                    waits = [Wv(sv, sel_d), Wv(sv, bads_done)]
                    if tt >= 2:
                        waits.append(Wv(ss, arelu[tt - 2]))
                cnt = op("t", lambda e, dc=dc, pda=pda, tt=tt: e.matmul(
                    pda, r2(WDSEL[:, :], 64)[:, dc, :],
                    xl3t[:, dc, tt * 512:tt * 512 + 512],
                    start=(dc == 0), stop=(dc == 7)),
                    waits=waits, inc=ts if dc == 7 else None)
            op("s", lambda e, pda=pda, tt=tt: e.activation(
                ASB[:, tt * 512:tt * 512 + 512], pda, AF.Relu, scale=0.5,
                bias=BADS[:, 0:1]), waits=[Wv(ts, cnt), Wv(sv, bads_done)])
            arelu[tt] = fence("s", ss)

        ev_done = {}
        for oc in range(NDC):
            for tt in range(4):
                j = oc * 4 + tt
                pbu = SPB[:, (j % 2) * 512:(j % 2) * 512 + 512]
                waits = [Wv(ss, arelu[tt]), Wv(sv, sel_u), Wv(sv, bcomb_done)]
                if j >= 2:
                    waits.append(Wv(sv, ev_done[j - 2]))
                else:
                    waits.append(Wv(ss, sq_end[(k_total - 1, NDC - 1 - (1 - j))]))
                cnt = op("t", lambda e, oc=oc, tt=tt, pbu=pbu: e.matmul(
                    pbu, WAUS[:, oc * 128:oc * 128 + 128],
                    ASB[:, tt * 512:tt * 512 + 512], start=True, stop=True),
                    waits=waits, inc=ts)
                ev_done[j] = op("v", lambda e, oc=oc, tt=tt, pbu=pbu:
                                e.scalar_tensor_tensor(
                                    xr3t[:, oc, tt * 512:tt * 512 + 512], pbu,
                                    BCOMB[:, oc:oc + 1],
                                    xr3t[:, oc, tt * 512:tt * 512 + 512],
                                    ALU.add, ALU.add),
                                waits=[Wv(ts, cnt), Wv(sv, ev_b[NDC * 4 - 1])],
                                inc=sv)
        return {"last_v": fence("v", sv)}

    def emit_attn(ffn1):
        spill = op("y", lambda e: e.dma_start(d_xdr[:, :, :], r2(XR[:, :], T)),
                   waits=[Wv(sv, ffn1["last_v"])], inc=sdy, by=16)
        bar = [Wv(ts, ts.n), Wv(ss, ss.n), Wv(sv, sv.n), Wv(sp, sp.n),
               Wv(sdy, sdy.n), Wv(sdg, sdg.n)]
        bv_ld = op("y", lambda e: e.dma_start(LROW[0:1, :], d_bvr), waits=bar,
                   inc=sdy, by=16)
        op("g", lambda e: e.tensor_copy(BVR[0:1, :], LROW[0:1, :]),
           waits=[Wv(sdy, bv_ld), *bar])
        bvr_c = fence("g", sp)
        qk_cast = []
        for dc in range(NDC):
            ld = op("y", lambda e, dc=dc: e.dma_start(RCBT[:, :], d_wqk[:, dc, :]),
                    waits=[Wv(sp, qk_cast[-1])] if dc else bar, inc=sdy, by=16)
            op("g", lambda e, dc=dc: e.tensor_copy(
                WQTQK[:, dc * 2048:dc * 2048 + 2048], RCBT[:, :]),
                waits=[Wv(sdy, ld)])
            qk_cast.append(fence("g", sp))
        wog_cast = []
        for gg in range(8):
            ld = op("y", lambda e, gg=gg: e.dma_start(
                r2(RCBT[0:64, :], 1024),
                bass.AP(d_wog.tensor, gg * 2 * 64 * 1024,
                        [[1024, 64], [65536, 2], [1, 1024]])),
                waits=[Wv(sp, qk_cast[-1] if gg == 0 else wog_cast[-1])],
                inc=sdy, by=16)
            op("g", lambda e, gg=gg: e.tensor_copy(
                WOG[:, gg * 2048:gg * 2048 + 2048], RCBT[0:64, :]),
                waits=[Wv(sdy, ld)])
            wog_cast.append(fence("g", sp))
        vsb_ones = op("v", lambda e: e.memset(
            r3(VSB[:, :], 16, 65)[:, :, :, 64:65], 1.0), waits=bar, inc=sv)

        xrq_load = {0: op("y", lambda e: e.dma_start(
            r2(XRQ[:, :], 512), d_xdr[:, :, 0:512]),
            waits=[Wv(sdy, spill)], inc=sdy, by=16)}

        qkv3 = r3(QKV[:, :], HQ, 2048)
        vsb4 = r3(VSB[:, :], 16, 65)
        xlq3 = r2(XLQ[:, :], 512)
        wqk4 = r3(WQTQK[:, :], 16, 128)
        vals3 = r2(QKV[0:64, 0:8192], 2048)

        qkv_last = 0
        last_wo_ev = 0
        wbs = {}
        for q in range(NQ):
            war = [Wv(sdy, xrq_load[q]), Wv(sp, bvr_c)]
            if q > 0:
                war.append(Wv(ts, qkv_last))
            lnd = emit_ln(r2(XRQ[:, :], 512), lambda dc: xlq3[:, dc, :],
                          LNSQA, MBRBA, False, war)
            pq_banks = [SPA[64:128, 0:512], SPA[64:128, 512:1024],
                        SPB[64:128, 0:512], SPB[64:128, 512:1024]]
            ev_cnt = {}
            for g in range(16):
                for qk in range(2):
                    i = g * 2 + qk
                    pqk = pq_banks[i % 4]
                    cnt = 0
                    for dc in range(NDC):
                        waits = []
                        if dc == 0:
                            waits = [Wv(sv, lnd["v"]), Wv(sp, lnd["p"]),
                                     Wv(sp, qk_cast[-1])]
                            if i >= 4:
                                c, wch = ev_cnt[i - 4]
                                waits.append(Wv(ss if wch == 0 else sv, c))
                        cnt = op("t", lambda e, g=g, qk=qk, dc=dc, pqk=pqk:
                                 e.matmul(pqk,
                                          wqk4[:, dc, g, qk * 64:qk * 64 + 64],
                                          xlq3[:, dc, :], start=(dc == 0),
                                          stop=(dc == 7), tile_position=(0, 64)),
                                 waits=waits, inc=ts if dc == 7 else None)
                    dst = qkv3[64:128, qk, :, g * 128:g * 128 + 128]
                    src = r2(pqk, 128)
                    if qk == 0:
                        c = op("s", lambda e, dst=dst, src=src, g=g: e.add(
                            dst, src, BQ2[64:128, g:g + 1]),
                            waits=[Wv(ts, cnt)], inc=ss)
                        ev_cnt[i] = (c, 0)
                    else:
                        c = op("v", lambda e, dst=dst, src=src, g=g: e.tensor_scalar(
                            dst, src, BK2[64:128, g:g + 1], None, ALU.add),
                            waits=[Wv(ts, cnt)], inc=sv)
                        ev_cnt[i] = (c, 1)
            qkv_ev_s = ev_cnt[30][0]
            qkv_ev_v = ev_cnt[31][0]
            pv_of = [SPA[:, 0:1024], SPB[:, 0:1024], VP[:, 0:1024], VP[:, 1024:2048]]
            wv_dma = {}
            vcast = {}
            vmm = {}
            for dc in range(NDC):
                wl = [Wv(ts, vmm.get((3, dc - 2), 0))] if dc >= 2 else \
                    [Wv(ss, qkv_ev_s), Wv(sv, qkv_ev_v)]
                wv_dma[dc] = op("y", lambda e, dc=dc: e.dma_start(
                    RCBT[:, (dc % 2) * 1024:(dc % 2) * 1024 + 1024],
                    d_wv[:, dc, :]), waits=wl, inc=sdy, by=16)
                op("g", lambda e, dc=dc: e.tensor_copy(
                    r2(WQTVC[:, :], 1024)[:, dc % 2, :],
                    RCBT[:, (dc % 2) * 1024:(dc % 2) * 1024 + 1024]),
                    waits=[Wv(sdy, wv_dma[dc])])
                vcast[dc] = fence("g", sp)
                for h in range(HQ):
                    pv = pv_of[h]
                    cnt = 0
                    for half in range(2):
                        waits = []
                        if half == 0:
                            waits.append(Wv(sp, vcast[dc]))
                            if dc == 0:
                                waits.append(Wv(ss, qkv_ev_s))
                                waits.append(Wv(sv, qkv_ev_v))
                        cnt = op("t", lambda e, h=h, dc=dc, half=half, pv=pv:
                                 e.matmul(pv[:, half * 512:half * 512 + 512],
                                          xlq3[:, dc, h * 128:h * 128 + 128],
                                          r2(WQTVC[:, :], 1024)[
                                              :, dc % 2,
                                              half * 512:half * 512 + 512],
                                          start=(dc == 0), stop=False),
                                 waits=waits,
                                 inc=ts if (half == 1 and h == HQ - 1) else None)
                    vmm[(h, dc)] = cnt
            vev = {}
            for h in range(HQ):
                pv = pv_of[h]
                cnt = 0
                for half in range(2):
                    cnt = op("t", lambda e, h=h, half=half, pv=pv: e.matmul(
                        pv[:, half * 512:half * 512 + 512], ONESB[:, :],
                        BVR[0:1, half * 512:half * 512 + 512],
                        start=False, stop=True),
                        waits=[Wv(sp, bvr_c)] if half == 0 else (),
                        inc=ts if half == 1 else None)
                op("v", lambda e, h=h, pv=pv: e.tensor_copy(
                    vsb4[:, h, :, 0:64], r2(pv, 64)),
                    waits=[Wv(ts, cnt), Wv(sv, vsb_ones)])
                vev[h] = fence("v", sv)
            exp_cnt = {}
            scores_cnt = {}
            attnv_cnt = {}
            dance_done = {}
            for h in range(HQ):
                units = [(t1h, g2) for t1h in range(2) for g2 in range(16)]
                for ui, (t1h, g2) in enumerate(units):
                    u = ui
                    sp_t = SPA if u % 2 == 0 else SPB
                    spu = sp_t[:, 0:1024]
                    cnt = 0
                    for kk in range(2):
                        waits = []
                        if kk == 0:
                            if (h, u - 2) in exp_cnt:
                                waits.append(Wv(ss, exp_cnt[(h, u - 2)]))
                            elif (h - 1, u + 30) in exp_cnt:
                                waits.append(Wv(ss, exp_cnt[(h - 1, u + 30)]))
                            else:
                                waits.append(Wv(sv, vev[HQ - 1]))
                        cnt = op("t", lambda e, h=h, t1h=t1h, g2=g2, kk=kk, spu=spu:
                                 e.matmul(
                                     spu[:, kk * 512:kk * 512 + 512],
                                     qkv3[64:128, 1, h, g2 * 128:g2 * 128 + 128],
                                     qkv3[64:128, 0, h, t1h * 1024 + kk * 512:
                                          t1h * 1024 + kk * 512 + 512],
                                     start=True, stop=True),
                                 waits=waits, inc=ts if kk == 1 else None)
                    scores_cnt[(h, u)] = cnt
                    exp_cnt[(h, u)] = op(
                        "s", lambda e, spu=spu, u=u: e.activation(
                            ES[:, (u % 2) * 1024:(u % 2) * 1024 + 1024], spu,
                            AF.Exp, scale=0.125),
                        waits=[Wv(ts, cnt)], inc=ss)
                    if ui >= 1:
                        emit_attnv(h, units[ui - 1][0], units[ui - 1][1],
                                   exp_cnt, attnv_cnt, dance_done, vev)
                    if ui == len(units) - 1:
                        emit_attnv(h, t1h, g2, exp_cnt, attnv_cnt, dance_done, vev)
                        for th in range(2):
                            emit_dance(h, th, attnv_cnt, dance_done)
            wo_ev = {}
            for oc in range(NDC):
                pwo = [SPA[:, 0:512], SPA[:, 512:1024], SPB[:, 0:512],
                       SPB[:, 512:1024]][oc % 4]
                cnt = 0
                for g1 in range(16):
                    waits = []
                    if g1 == 0:
                        waits = [Wv(sv, dance_done[(HQ - 1, 1)]),
                                 Wv(sp, wog_cast[-1])]
                        if oc >= 4:
                            waits.append(Wv(sv, wo_ev[oc - 4]))
                    cnt = op("t", lambda e, oc=oc, g1=g1, pwo=pwo: e.matmul(
                        pwo, WOG[:, g1 * 1024 + oc * 128:g1 * 1024 + oc * 128 + 128],
                        vals3[:, :, g1 * 128:g1 * 128 + 128],
                        start=(g1 == 0), stop=(g1 == 15)),
                        waits=waits, inc=ts if g1 == 15 else None)
                op("v", lambda e, oc=oc, pwo=pwo: e.scalar_tensor_tensor(
                    r2(XRQ[:, :], 512)[:, oc, :], pwo, BO[:, oc:oc + 1],
                    r2(XRQ[:, :], 512)[:, oc, :], ALU.add, ALU.add),
                    waits=[Wv(ts, cnt)])
                wo_ev[oc] = fence("v", sv)
            last_wo_ev = wo_ev[NDC - 1]
            wb = op("y", lambda e, q=q: e.dma_start(
                d_xdr[:, :, q * 512:q * 512 + 512], r2(XRQ[:, :], 512)),
                waits=[Wv(sv, wo_ev[NDC - 1])], inc=sdy, by=16)
            wbs[q] = wb
            if q + 1 < NQ:
                xrq_load[q + 1] = op("y", lambda e, q=q: e.dma_start(
                    r2(XRQ[:, :], 512), d_xdr[:, :, (q + 1) * 512:(q + 1) * 512 + 512]),
                    waits=[Wv(sdy, wb)], inc=sdy, by=16)
            qkv_last = ts.n
        return {"wb_final": sdy.n, "last_v": last_wo_ev, "wb": wbs}

    def emit_attnv(h, t1h, g2, exp_cnt, attnv_cnt, dance_done, vev):
        u = t1h * 16 + g2
        esl = ES[:, (u % 2) * 1024:(u % 2) * 1024 + 1024]
        vph = VP[0:65, t1h * 1024:t1h * 1024 + 1024]
        vsb4 = r3(VSB[:, :], 16, 65)
        cnt = 0
        for kk in range(2):
            waits = []
            if kk == 0:
                waits.append(Wv(ss, exp_cnt[(h, u)]))
                if g2 == 0:
                    dk = dance_done.get((h - 1, t1h))
                    if dk is not None:
                        waits.append(Wv(sv, dk))
                    else:
                        waits.append(Wv(sv, vev[HQ - 1]))
            cnt = op("t", lambda e, h=h, g2=g2, kk=kk, vph=vph, esl=esl: e.matmul(
                vph[:, kk * 512:kk * 512 + 512], vsb4[:, h, g2, :],
                esl[:, kk * 512:kk * 512 + 512],
                start=(g2 == 0), stop=(g2 == 15)),
                waits=waits, inc=ts if (kk == 1 and g2 == 15) else None)
        if g2 == 15:
            attnv_cnt[(h, t1h)] = cnt

    _dprev = {}

    def emit_dance(h, t1h, attnv_cnt, dance_done):
        t0 = t1h * 1024
        op("v", lambda e, t0=t0: e.tensor_copy(
            RCBT[64:65, t0:t0 + 1024], VP[64:65, t0:t0 + 1024]),
            waits=[Wv(ts, attnv_cnt[(h, t1h)])])
        cp = fence("v", sv)
        d1_ = op("g", lambda e, t0=t0: e.dma_start(
            d_s1[0:1, t0:t0 + 1024], RCBT[64:65, t0:t0 + 1024]),
            waits=[Wv(sv, cp), Wv(sdg, _dprev.get(("s1", t0), 0))], inc=sdg, by=16)
        d2_ = op("g", lambda e, t0=t0: e.dma_start(
            RTRC[:, 0:8], bass.AP(d_s1.tensor, t0, [[8, 128], [1, 8]])),
            waits=[Wv(sdg, d1_)], inc=sdg, by=16)
        op("v", lambda e: e.reciprocal(RTRC[:, 16:24], RTRC[:, 0:8]),
                waits=[Wv(sdg, d2_)])
        rc = fence("v", sv)
        d3_ = op("g", lambda e, t0=t0: e.dma_start(
            bass.AP(d_s2.tensor, t0, [[8, 128], [1, 8]]), RTRC[:, 16:24]),
            waits=[Wv(sv, rc), Wv(sdg, _dprev.get(("s2", t0), 0))], inc=sdg, by=16)
        d4_ = op("g", lambda e, t0=t0: e.dma_start(
            RCBT[0:64, t0:t0 + 1024],
            bass.AP(d_s2.tensor, t0, [[0, 64], [1, 1024]])),
            waits=[Wv(sdg, d3_)], inc=sdg, by=16)
        _dprev[("s1", t0)] = d2_
        _dprev[("s2", t0)] = d4_
        op("v", lambda e, h=h, t0=t0: e.tensor_tensor(
            QKV[0:64, h * 2048 + t0:h * 2048 + t0 + 1024],
            VP[0:64, t0:t0 + 1024], RCBT[0:64, t0:t0 + 1024], ALU.mult),
            waits=[Wv(sdg, d4_)])
        dance_done[(h, t1h)] = fence("v", sv)

    ffn1 = emit_ffn(0, xdma)
    attn = emit_attn(ffn1)
    bar2 = [Wv(ts, ts.n), Wv(ss, ss.n), Wv(sv, sv.n), Wv(sp, sp.n), Wv(sdg, sdg.n)]
    xrl = []
    for tb in range(NTB):
        xrl.append(op("y", lambda e, tb=tb: e.dma_start(
            r2(XR[:, :], T)[:, :, tb * TBS:(tb + 1) * TBS],
            d_xdr[:, :, tb * TBS:(tb + 1) * TBS]),
            waits=[*bar2, Wv(sdy, attn["wb"][tb])] if tb == 0 else
            [Wv(sdy, attn["wb"][tb])], inc=sdy, by=16))
    ffn2 = emit_ffn(1, xrl)
    for dc in range(NDC):
        op("y", lambda e, dc=dc: e.dma_start(
            d_y[:, dc, :], r2(XR[:, :], T)[:, dc, :]),
            waits=[Wv(sv, ffn2["last_v"])] if dc == 0 else (), inc=sdy, by=16)
    fin_y = sdy.n
    op("y", lambda e: e.wait_ge(sdy.h, fin_y))
    op("g", lambda e: e.wait_ge(sdy.h, fin_y))

    with nc.Block() as block:
        @block.tensor
        def _(e):
            for eng, fn in ops:
                if eng == "t":
                    fn(e)

        @block.scalar
        def _(e):
            for eng, fn in ops:
                if eng == "s":
                    fn(e)

        @block.vector
        def _(e):
            for eng, fn in ops:
                if eng == "v":
                    fn(e)

        @block.gpsimd
        def _(e):
            for eng, fn in ops:
                if eng == "g":
                    fn(e)

        @block.sync
        def _(e):
            for eng, fn in ops:
                if eng == "y":
                    fn(e)

    es.close()
    return nc


def _prep(kw):
    x = kw["L0_attn_x"]
    wqkv = kw["L0_attn_Wqkv"]
    bqkv = kw["L0_attn_bqkv"]
    wo = kw["L0_attn_Wo"]
    bo = kw["L0_attn_bo"]
    wbase = kw["L0_ffn_W_base"]
    bbase = kw["L0_ffn_b_base"]
    wd1 = kw["L0_ffn_Wd1"]
    bd1 = kw["L0_ffn_bd1"]
    wd2 = kw["L0_ffn_Wd2"]
    bd2 = kw["L0_ffn_bd2"]
    wadn = kw["L0_ffn_Wa_down"]
    badn = kw["L0_ffn_ba_down"]
    waup = kw["L0_ffn_Wa_up"]
    baup = kw["L0_ffn_ba_up"]

    f = np.float32

    def c(a):
        return np.ascontiguousarray(np.asarray(a), dtype=f)

    shared = {}
    shared["wd1"] = c(np.asarray(wd1).reshape(8, 8, 128, 256)
                      .transpose(0, 2, 1, 3).reshape(8, 128, 2048))
    shared["wd2"] = c(np.asarray(wd2).reshape(8, 2, 128, 8, 128)
                      .transpose(0, 2, 1, 3, 4).reshape(8, 128, 2048))
    shared["wbase"] = c(np.asarray(wbase).reshape(8, 128, 8, 128)
                        .transpose(2, 1, 0, 3).reshape(8, 128, 1024))
    shared["wadn"] = c(np.asarray(wadn).reshape(8, 8, 128, 64)
                       .transpose(0, 2, 1, 3).reshape(8, 128, 512))
    shared["waup"] = c(waup)
    wq4 = np.asarray(wqkv).reshape(16, 3, 64, 8, 128)
    shared["wqk"] = c(wq4[:, 0:2].transpose(4, 3, 0, 1, 2).reshape(128, 8, 2048))
    shared["wv"] = c(wq4[:, 2].transpose(3, 2, 0, 1).reshape(128, 8, 1024))
    shared["wog"] = c(np.asarray(wo).T.reshape(16, 64, 1024))
    shared["nid"] = c(-np.eye(128))
    shared["bd1"] = c(np.asarray(bd1).reshape(8, 2, 128).transpose(2, 0, 1)
                      .reshape(128, 16))
    shared["bd2"] = c(np.asarray(bd2).reshape(8, 8, 128).transpose(2, 0, 1)
                      .reshape(128, 64))
    shared["bb"] = c(np.asarray(bbase).reshape(8, 128).T)
    shared["bo"] = c(np.asarray(bo).reshape(8, 128).T)
    shared["bad"] = c(np.asarray(badn).T)
    shared["bau"] = c(np.asarray(baup).reshape(8, 8, 128).transpose(2, 1, 0)
                      .reshape(128, 64))
    shared["pw8"] = c((2.0 ** -np.arange(8)).reshape(1, 8))
    bqr = np.asarray(bqkv).reshape(16, 3, 64)
    bq = np.zeros((128, 16), f)
    bk = np.zeros((128, 16), f)
    bq[64:128, :] = bqr[:, 0, :].T
    bk[64:128, :] = bqr[:, 1, :].T
    shared["bq2"] = bq
    shared["bk2"] = bk
    shared["bvr"] = c(bqr[:, 2, :].reshape(1, 1024))

    in_maps = []
    for ci in range(NC):
        m = dict(shared)
        m["x"] = c(np.asarray(x[ci]).reshape(2048, 8, 128).transpose(2, 1, 0))
        in_maps.append(m)
    return in_maps


def kernel(**kw):
    if "nc" not in _CACHE:
        _CACHE["nc"] = build_program()
    nc = _CACHE["nc"]
    in_maps = _prep(kw)
    res = run_bass_kernel_spmd(nc, in_maps, core_ids=list(range(NC)))
    outs = []
    for ci in range(NC):
        ycm = np.asarray(res.results[ci]["y"])
        outs.append(ycm.transpose(2, 1, 0).reshape(2048, 1024))
    return np.stack(outs, axis=0).astype(np.float32)


# version 23
# speedup vs baseline: 1.0069x; 1.0026x over previous
import contextlib
import numpy as np
import concourse.bass as bass
import concourse.mybir as mybir
from concourse.bass_utils import run_bass_kernel_spmd

F32 = mybir.dt.float32
F32R = mybir.dt.float32r
BF16 = mybir.dt.bfloat16
AF = mybir.ActivationFunctionType
ALU = mybir.AluOpType
AX = mybir.AxisListType

NC = 8
T = 2048
NDC = 8
NTB = 4
TBS = 512
NE = 8
HQ = 4
NQ = 4
EPS = 1e-5

_CACHE = {}


class Ctr:
    __slots__ = ("h", "n")

    def __init__(self, h):
        self.h = h
        self.n = 0


def r2(ap, b):
    return ap.rearrange("p (a b) -> p a b", b=b)


def r3(ap, b, c):
    return ap.rearrange("p (a b c) -> p a b c", b=b, c=c)


def build_program():
    nc = bass.Bass("TRN2", target_bir_lowering=False)

    d_x = nc.dram_tensor("x", [128, NDC, T], F32, kind="ExternalInput").ap()
    d_y = nc.dram_tensor("y", [128, NDC, T], F32, kind="ExternalOutput").ap()
    d_wd1 = nc.dram_tensor("wd1", [NE, 128, 2048], F32R, kind="ExternalInput").ap()
    d_wd2 = nc.dram_tensor("wd2", [NE, 128, 2048], F32R, kind="ExternalInput").ap()
    d_wbase = nc.dram_tensor("wbase", [NDC, 128, 1024], F32, kind="ExternalInput").ap()
    d_wadn = nc.dram_tensor("wadn", [NE, 128, 512], F32, kind="ExternalInput").ap()
    d_waup = nc.dram_tensor("waup", [NE, 64, 1024], F32, kind="ExternalInput").ap()
    d_wqk = nc.dram_tensor("wqk", [128, NDC, 2048], F32, kind="ExternalInput").ap()
    d_wv = nc.dram_tensor("wv", [128, NDC, 1024], F32, kind="ExternalInput").ap()
    d_wog = nc.dram_tensor("wog", [16, 64, 1024], F32, kind="ExternalInput").ap()
    d_nid = nc.dram_tensor("nid", [128, 128], F32R, kind="ExternalInput").ap()
    d_bd1 = nc.dram_tensor("bd1", [128, 16], F32, kind="ExternalInput").ap()
    d_bd2 = nc.dram_tensor("bd2", [128, 64], F32, kind="ExternalInput").ap()
    d_bb = nc.dram_tensor("bb", [128, 8], F32, kind="ExternalInput").ap()
    d_bo = nc.dram_tensor("bo", [128, 8], F32, kind="ExternalInput").ap()
    d_bad = nc.dram_tensor("bad", [64, 8], F32, kind="ExternalInput").ap()
    d_bau = nc.dram_tensor("bau", [128, 64], F32, kind="ExternalInput").ap()
    d_pw8 = nc.dram_tensor("pw8", [1, 8], F32, kind="ExternalInput").ap()
    d_bq2 = nc.dram_tensor("bq2", [128, 16], F32, kind="ExternalInput").ap()
    d_bk2 = nc.dram_tensor("bk2", [128, 16], F32, kind="ExternalInput").ap()
    d_bvr = nc.dram_tensor("bvr", [1, 1024], F32, kind="ExternalInput").ap()
    d_xdr = nc.dram_tensor("xdr", [128, NDC, T], F32, kind="Internal").ap()
    d_s1 = nc.dram_tensor("dsc1", [1, 2048], F32, kind="Internal").ap()
    d_s2 = nc.dram_tensor("dsc2", [1, 2048], F32, kind="Internal").ap()
    d_sb = nc.dram_tensor("dscb", [1, 1024], BF16, kind="Internal").ap()
    d_s3 = nc.dram_tensor("dsc3", [1, 64], F32, kind="Internal").ap()

    SB0 = 16512
    _A = nc.alloc_sbuf_tensor_at

    def A(name, shape, dtype, offset):
        return _A(name, shape, dtype, offset=SB0 + offset)
    NID = A("NID", [128, 128], F32R, offset=0)
    BD1 = A("BD1", [128, 16], F32, offset=512)
    BD2 = A("BD2", [128, 64], F32, offset=576)
    BB = A("BB", [128, 8], F32, offset=832)
    BBH = A("BBH", [128, 8], F32, offset=864)
    BO = A("BO", [128, 8], F32, offset=896)
    BAU = A("BAU", [128, 64], F32, offset=928)
    BAD = A("BAD", [64, 8], F32, offset=1184)
    BADS = A("BADS", [64, 8], F32, offset=1216)
    OH = A("OH", [128, 8], F32, offset=1248)
    BCT = A("BCT", [128, 8], F32, offset=1280)
    BCOMB = A("BCOMB", [128, 8], F32, offset=1312)
    PW8 = A("PW8", [1, 8], F32, offset=1344)
    LS = A("LS", [1, 64], F32, offset=1376)
    LA = A("LA", [128, 256], F32, offset=1632)
    LP = A("LP", [128, 8], F32, offset=2656)
    ONESC = A("ONESC", [128, 16], BF16, offset=2688)
    ONES1C = A("ONES1C", [128, 8], F32, offset=2720)
    ONESB = A("ONESB", [1, 128], BF16, offset=2752)
    LROW = A("LROW", [1, 1024], F32, offset=3008)
    LROWB = A("LROWB", [1, 1024], BF16, offset=7104)
    BQ2 = A("BQ2", [128, 16], F32, offset=9152)
    BK2 = A("BK2", [128, 16], F32, offset=9216)
    BVR = A("BVR", [1, 1024], BF16, offset=9280)
    RTRC = A("RTRC", [128, 32], F32, offset=11328)
    LROW2 = A("LROW2", [1, 1024], F32, offset=206848)
    LROWB2 = A("LROWB2", [1, 1024], BF16, offset=9280)
    EPSC = A("EPSC", [1, 8], F32, offset=11456)
    FEN = A("FEN", [1, 16], F32, offset=11488)
    XR = A("XR", [128, NDC * T], F32, offset=12288)
    XL = A("XL", [128, NDC * T], BF16, offset=77824)
    XLR = A("XLR", [128, 4096], F32R, offset=110592)
    XLR2 = A("XLR2", [128, 4096], F32R, offset=190464)
    WD1 = A("WD1", [128, 4096], F32R, offset=126976)
    WD2 = A("WD2", [128, 4096], F32R, offset=143360)
    HBR = A("HBR", [128, 1024], F32R, offset=159744)
    ASB = A("ASB", [64, 2048], BF16, offset=163840)
    LNSQ = A("LNSQ", [128, 4096], BF16, offset=167936)
    MBRB = A("MBRB", [128, 1024], BF16, offset=176128)
    SQSC = A("SQSC", [128, 512], BF16, offset=178176)
    WASL = A("WASL", [128, 1024], F32, offset=179200)
    WBC = A("WBC", [128, 2048], BF16, offset=183296)
    WDSEL = A("WDSEL", [128, 512], BF16, offset=187392)
    WAUS = A("WAUS", [64, 1024], BF16, offset=188416)
    WQTQK = A("WQTQK", [128, 16384], BF16, offset=12288)
    QKV = A("QKV", [128, 16384], BF16, offset=45056)
    XRQ = A("XRQ", [128, 4096], F32, offset=77824)
    XLQ = A("XLQ", [128, 4096], BF16, offset=94208)
    VSB = A("VSB", [128, 4160], BF16, offset=102400)
    WQTVC = A("WQTVC", [128, 2048], BF16, offset=110720)
    ES = A("ES", [128, 2048], BF16, offset=114816)
    RCBT = A("RCBT", [128, 2048], F32, offset=118912)
    WOG = A("WOG", [64, 16384], BF16, offset=127104)
    LNSQA = A("LNSQA", [128, 4096], BF16, offset=159872)
    MBRBA = A("MBRBA", [128, 1024], BF16, offset=168064)
    XRQ2 = A("XRQ2", [128, 4096], F32, offset=170112)

    SPA = nc.alloc_psum_tensor("SPA", [128, 1024], F32)
    SPB = nc.alloc_psum_tensor("SPB", [128, 1024], F32)
    VP = nc.alloc_psum_tensor("VP", [128, 2048], F32)

    es = contextlib.ExitStack()
    sdy = Ctr(es.enter_context(nc.semaphore("sdy")))
    sdg = Ctr(es.enter_context(nc.semaphore("sdg")))
    ts = Ctr(es.enter_context(nc.semaphore("ts")))
    ss = Ctr(es.enter_context(nc.semaphore("ss")))
    sv = Ctr(es.enter_context(nc.semaphore("sv")))
    sp = Ctr(es.enter_context(nc.semaphore("sp")))

    ops = []

    def op(eng, bf, waits=(), inc=None, by=1):
        wts = tuple((c, v) for (c, v) in waits if v > 0)
        if inc is not None:
            inc.n += by
        tgt = inc.n if inc is not None else 0

        def fn(e, bf=bf, wts=wts, inc=inc, by=by):
            for (c, val) in wts:
                e.wait_ge(c.h, val)
            ins = bf(e)
            if inc is not None:
                ins.then_inc(inc.h, by)
        ops.append((eng, fn))
        return tgt

    def Wv(c, v):
        return (c, v)

    def drain(eng):
        ops.append((eng, lambda e: e.drain()))

    _fsl = {"v": 0, "g": 2, "s": 4}

    def fence(eng, sem):
        drain(eng)
        i = _fsl[eng]
        if eng == "s":
            return op(eng, lambda e: e.square(FEN[0:1, i:i + 1], FEN[0:1, i:i + 1]),
                      inc=sem)
        return op(eng, lambda e: e.tensor_copy(FEN[0:1, i:i + 1], FEN[0:1, i + 1:i + 2]),
                  inc=sem)

    xdma = []
    for tb in range(NTB):
        xdma.append(op("y", lambda e, tb=tb: e.dma_start(
            r2(XR[:, :], T)[:, :, tb * TBS:(tb + 1) * TBS],
            d_x[:, :, tb * TBS:(tb + 1) * TBS]), inc=sdy, by=16))
    for dst, src in [(NID[:, :], d_nid), (BD1[:, :], d_bd1), (BD2[:, :], d_bd2),
                     (BB[:, :], d_bb), (BO[:, :], d_bo), (BAD[:, :], d_bad),
                     (BAU[:, :], d_bau), (PW8[:, :], d_pw8), (BQ2[:, :], d_bq2),
                     (BK2[:, :], d_bk2)]:
        op("y", lambda e, dst=dst, src=src: e.dma_start(dst, src), inc=sdy, by=16)
    consts_loaded = sdy.n
    op("v", lambda e: e.memset(ONESC[:, 0:1], 1.0 / 1024.0), inc=sv)
    op("v", lambda e: e.memset(ONES1C[:, 0:1], 1.0), inc=sv)
    op("v", lambda e: e.memset(ONESB[:, :], 1.0), inc=sv)
    op("v", lambda e: e.memset(EPSC[0:1, :], EPS), inc=sv)
    init_v = op("v", lambda e: e.tensor_scalar_mul(BBH[:, :], BB[:, :], 0.5),
                waits=[Wv(sdy, consts_loaded)], inc=sv)

    def emit_ln(xr3, xl3, lnsq, mbrb, want_xlr, war_wait, xlr3=None, lrow=None, lrowb=None):
        PM = VP[0:1, 1024:1536]
        PSQ = VP[0:1, 1536:2048]
        if lrow is None:
            lrow, lrowb = LROW, LROWB
        lnsq3 = r2(lnsq[:, :], 512)
        cast_done = []
        for dc in range(NDC):
            cast_done.append(op("v", lambda e, dc=dc: e.tensor_copy(
                xl3(dc), xr3[:, dc, :]), waits=war_wait if dc == 0 else (), inc=sv))
        sq = op("s", lambda e: e.activation(lnsq3, xr3, AF.Square),
                waits=war_wait, inc=ss)
        for dc in range(NDC):
            op("t", lambda e, dc=dc: e.matmul(
                PM, ONESC[:, 0:1], xl3(dc), start=(dc == 0), stop=(dc == NDC - 1)),
                waits=[Wv(sv, cast_done[dc]), Wv(sv, init_v)])
        mm_done = 0
        for dc in range(NDC):
            last = dc == NDC - 1
            mm_done = op("t", lambda e, dc=dc: e.matmul(
                PSQ, ONESC[:, 0:1], lnsq3[:, dc, :],
                start=(dc == 0), stop=(dc == NDC - 1)),
                waits=[Wv(ss, sq)] if dc == 0 else (), inc=ts if last else None)
        op("v", lambda e: e.tensor_copy(lrow[0:1, 0:512], PM), waits=[Wv(ts, mm_done)])
        drain("v")
        op("v", lambda e: e.tensor_copy(lrowb[0:1, 0:512], lrow[0:1, 0:512]))
        drain("v")
        op("v", lambda e: e.tensor_mul(lrow[0:1, 512:1024], lrow[0:1, 0:512],
                                       lrow[0:1, 0:512]))
        drain("v")
        var = op("v", lambda e: e.tensor_tensor(lrow[0:1, 512:1024], PSQ,
                                                lrow[0:1, 512:1024], ALU.subtract),
                 inc=sv)
        op("s", lambda e: e.activation(lrow[0:1, 0:512], lrow[0:1, 512:1024],
                                       AF.Ln, bias=EPSC[0:1, 0:1]), waits=[Wv(sv, var)])
        drain("s")
        rstd = op("s", lambda e: e.activation(lrowb[0:1, 512:1024], lrow[0:1, 0:512],
                                              AF.Exp, scale=-0.5), inc=ss)
        sbw = op("g", lambda e, lrowb=lrowb: e.dma_start(d_sb, lrowb[0:1, :]),
                 waits=[Wv(ss, rstd)], inc=sdg, by=16)
        bc = op("g", lambda e: e.dma_start(
            mbrb[:, :], bass.AP(d_sb.tensor, 0, [[0, 128], [1, 1024]])),
            waits=[Wv(sdg, sbw)], inc=sdg, by=16)
        vdone = []
        pdone = []
        for dc in range(4):
            tg = (xlr3[:, dc, :] if want_xlr else xl3(dc))
            vdone.append(op("v", lambda e, dc=dc, tg=tg: e.tensor_tensor(
                tg, xr3[:, dc, :], mbrb[:, 0:512], ALU.subtract),
                waits=[Wv(sdg, bc), Wv(ts, mm_done)], inc=sv))
        for dc in range(4, 8):
            tg = (xlr3[:, dc, :] if want_xlr else xl3(dc))
            pdone.append(op("g", lambda e, dc=dc, tg=tg: e.tensor_tensor(
                tg, xr3[:, dc, :], mbrb[:, 0:512], ALU.subtract),
                waits=[Wv(sdg, bc), Wv(ts, mm_done)], inc=sp))
        mdone_p = mdone_v = 0
        for dc in range(4):
            tg = (xlr3[:, dc, :] if want_xlr else xl3(dc))
            mdone_p = op("g", lambda e, dc=dc, tg=tg: e.tensor_tensor(
                tg, tg, mbrb[:, 512:1024], ALU.mult),
                waits=[Wv(sv, vdone[dc])], inc=sp)
        for dc in range(4, 8):
            tg = (xlr3[:, dc, :] if want_xlr else xl3(dc))
            mdone_v = op("v", lambda e, dc=dc, tg=tg: e.tensor_tensor(
                tg, tg, mbrb[:, 512:1024], ALU.mult),
                waits=[Wv(sp, pdone[dc - 4])], inc=sv)
        vf = fence("v", sv)
        if want_xlr:
            for dc in range(NDC):
                op("g", lambda e, dc=dc: e.tensor_copy(
                    xl3(dc), xlr3[:, dc, :]),
                    waits=[Wv(sv, mdone_v)] if dc == 0 else ())
        pf = fence("g", sp)
        return {"v": vf, "p": pf}

    def emit_ffn(call, xwait):
        xr3t = r2(XR[:, :], T)
        xl3t = r2(XL[:, :], T)
        k_total = NTB * NE

        d1_end = {}
        d2_end = {}
        relu_end = {}
        sq_end = {}
        wd1_ld = {}
        wd2_ld = {}

        def xlr_of(tb):
            return r2((XLR if tb % 2 == 0 else XLR2)[:, :], 512)

        def issue_wd1(k, waits):
            e_ = k % NE
            wd1_ld[k] = op("y", lambda e, k=k, e_=e_: e.dma_start(
                WD1[:, (k % 2) * 2048:(k % 2) * 2048 + 2048], d_wd1[e_, :, :]),
                waits=waits, inc=sdy, by=16)

        def issue_wd2(k, waits):
            e_ = k % NE
            wd2_ld[k] = op("y", lambda e, k=k, e_=e_: e.dma_start(
                WD2[:, (k % 2) * 2048:(k % 2) * 2048 + 2048], d_wd2[e_, :, :]),
                waits=waits, inc=sdy, by=16)

        issue_wd1(0, [Wv(sdy, consts_loaded)])
        issue_wd2(0, [])
        issue_wd1(1, [])
        issue_wd2(1, [])

        def emit_ln_tb(tb):
            war = [Wv(sdy, xwait[tb])]
            if tb >= 2:
                war.append(Wv(ts, d2_end[((tb - 2) * NE + NE - 1, NDC - 1)]))
            return emit_ln(
                xr3t[:, :, tb * TBS:(tb + 1) * TBS],
                lambda dc, tb=tb: xl3t[:, dc, tb * TBS:(tb + 1) * TBS],
                LNSQ, MBRB, True, war, xlr_of(tb),
                LROW if tb % 2 == 0 else LROW2,
                LROWB if tb % 2 == 0 else LROWB2)

        ln_done = {0: emit_ln_tb(0)}
        for tb in range(NTB):
            xlr3 = xlr_of(tb)
            for e_ in range(NE):
                k = tb * NE + e_
                if e_ == 1 and tb + 1 < NTB:
                    ln_done[tb + 1] = emit_ln_tb(tb + 1)
                wd1k = r2(WD1[:, (k % 2) * 2048:(k % 2) * 2048 + 2048], 256)
                wd2k = r2(WD2[:, (k % 2) * 2048:(k % 2) * 2048 + 2048], 1024)
                for hc in range(2):
                    pd1 = SPA[:, hc * 512:hc * 512 + 512]
                    cnt = 0
                    for dc in range(NDC):
                        waits = []
                        if dc == 0:
                            waits = [Wv(sdy, wd1_ld[k]),
                                     Wv(sv, ln_done[tb]["v"]),
                                     Wv(sp, ln_done[tb]["p"])]
                            if k >= 1:
                                waits.append(Wv(ss, relu_end.get((k - 1, hc), 0)))
                                waits.append(Wv(ts, d2_end[(k - 1, NDC - 1)]))
                        cnt = op("t", lambda e, dc=dc, hc=hc, pd1=pd1, wd1k=wd1k,
                                 xlr3=xlr3:
                                 e.matmul(pd1, wd1k[:, dc, hc * 128:hc * 128 + 128],
                                          xlr3[:, dc, :],
                                          start=(dc == 0), stop=(dc == 7)),
                                 waits=waits, inc=ts if dc == 7 else None)
                    d1_end[k] = cnt
                    relu_end[(k, hc)] = op(
                        "s", lambda e, hc=hc, pd1=pd1, e_=e_: e.activation(
                            HBR[:, hc * 512:hc * 512 + 512], pd1, AF.Relu,
                            bias=BD1[:, e_ * 2 + hc:e_ * 2 + hc + 1]),
                        waits=[Wv(ts, cnt)], inc=ss)
                if k + 2 < k_total:
                    issue_wd1(k + 2, [Wv(ts, d1_end[k])])
                for oc in range(NDC):
                    pd2 = SPB[:, (oc % 2) * 512:(oc % 2) * 512 + 512]
                    waits = [Wv(ss, relu_end[(k, 1)]), Wv(sdy, wd2_ld[k])]
                    if (k, oc - 2) in sq_end:
                        waits.append(Wv(ss, sq_end[(k, oc - 2)]))
                    elif (k - 1, oc + 6) in sq_end:
                        waits.append(Wv(ss, sq_end[(k - 1, oc + 6)]))
                    for hc in range(2):
                        op("t", lambda e, hc=hc, oc=oc, pd2=pd2, wd2k=wd2k:
                           e.matmul(pd2, wd2k[:, hc, oc * 128:oc * 128 + 128],
                                    HBR[:, hc * 512:hc * 512 + 512],
                                    start=(hc == 0), stop=False),
                           waits=waits if hc == 0 else ())
                    cnt = op("t", lambda e, oc=oc, pd2=pd2, xlr3=xlr3: e.matmul(
                        pd2, NID[:, :], xlr3[:, oc, :], start=False, stop=True),
                        inc=ts)
                    d2_end[(k, oc)] = cnt
                    col = e_ * 32 + tb * 8 + oc
                    sq_end[(k, oc)] = op(
                        "s", lambda e, pd2=pd2, e_=e_, oc=oc, col=col: e.activation(
                            SQSC[:, :], pd2, AF.Square,
                            bias=BD2[:, e_ * 8 + oc:e_ * 8 + oc + 1],
                            accum_out=LA[:, col:col + 1]),
                        waits=[Wv(ts, cnt)], inc=ss)
                if k + 2 < k_total:
                    issue_wd2(k + 2, [Wv(ts, d2_end[(k, NDC - 1)])])

        wb_dma = {}
        wb_cast = {}
        bu_end = {}
        ev_b = {}
        wb_dma[0] = op("y", lambda e: e.dma_start(WASL[:, :], d_wbase[0, :, :]),
                       waits=[Wv(sdy, wd2_ld[k_total - 1])], inc=sdy, by=16)
        vp_banks = [VP[:, i * 512:i * 512 + 512] for i in range(4)]
        for oc in range(NDC):
            wl = [Wv(sdy, wb_dma[oc])]
            if oc >= 2:
                wl.append(Wv(ts, bu_end[(oc - 2, 3)]))
            op("g", lambda e, oc=oc: e.tensor_scalar_mul(
                WBC[:, (oc % 2) * 1024:(oc % 2) * 1024 + 1024], WASL[:, :], 0.5),
                waits=wl)
            wb_cast[oc] = fence("g", sp)
            if oc + 1 < NDC:
                wb_dma[oc + 1] = op("y", lambda e, oc=oc: e.dma_start(
                    WASL[:, :], d_wbase[oc + 1, :, :]),
                    waits=[Wv(sp, wb_cast[oc])], inc=sdy, by=16)
            for tt in range(4):
                j = oc * 4 + tt
                pbu = vp_banks[j % 4]
                waits = [Wv(sp, wb_cast[oc])]
                if j >= 4:
                    waits.append(Wv(sv, ev_b[j - 4]))
                else:
                    waits.append(Wv(sv, ln_done[NTB - 1]["v"]))
                cnt = 0
                for dc in range(NDC):
                    cnt = op("t", lambda e, oc=oc, tt=tt, dc=dc, pbu=pbu: e.matmul(
                        pbu, r2(WBC[:, (oc % 2) * 1024:(oc % 2) * 1024 + 1024],
                                128)[:, dc, :],
                        xl3t[:, dc, tt * 512:tt * 512 + 512],
                        start=(dc == 0), stop=(dc == 7)),
                        waits=waits if dc == 0 else (),
                        inc=ts if dc == 7 else None)
                bu_end[(oc, tt)] = cnt
                ev_b[j] = op("v", lambda e, oc=oc, tt=tt, pbu=pbu:
                             e.scalar_tensor_tensor(
                                 xr3t[:, oc, tt * 512:tt * 512 + 512], pbu,
                                 BBH[:, oc:oc + 1],
                                 xr3t[:, oc, tt * 512:tt * 512 + 512],
                                 ALU.add, ALU.add),
                             waits=[Wv(ts, cnt)], inc=sv)

        drain("s")
        sfence = op("s", lambda e: e.square(LS[0:1, 48:49], LS[0:1, 48:49]),
                    waits=[Wv(ss, sq_end[(k_total - 1, NDC - 1)])], inc=ss)
        lpd = 0
        for e_ in range(NE):
            lpd = op("v", lambda e, e_=e_: e.tensor_reduce(
                LP[:, e_:e_ + 1], LA[:, e_ * 32:e_ * 32 + 32], AX.X, ALU.add),
                waits=[Wv(ss, sfence)] if e_ == 0 else (),
                inc=sv)
        lpd = fence("v", sv)
        pl_mm = op("t", lambda e: e.matmul(SPA[0:1, 0:8], ONES1C[:, 0:1], LP[:, :],
                                           start=True, stop=True),
                   waits=[Wv(sv, lpd), Wv(ss, relu_end[(k_total - 1, 1)])], inc=ts)
        op("v", lambda e: e.tensor_copy(LS[0:1, 0:8], SPA[0:1, 0:8]),
           waits=[Wv(ts, pl_mm)])
        drain("v")
        op("v", lambda e: e.tensor_reduce(LS[0:1, 8:9], LS[0:1, 0:8], AX.X, ALU.min))
        drain("v")
        op("v", lambda e: e.tensor_scalar(LS[0:1, 9:17], LS[0:1, 0:8], LS[0:1, 8:9],
                                          None, ALU.is_equal))
        drain("v")
        op("v", lambda e: e.tensor_mul(LS[0:1, 25:33], LS[0:1, 9:17], PW8[0:1, :]))
        drain("v")
        op("v", lambda e: e.tensor_reduce(LS[0:1, 33:34], LS[0:1, 25:33], AX.X,
                                          ALU.max))
        drain("v")
        op("v", lambda e: e.tensor_scalar(LS[0:1, 40:48], LS[0:1, 25:33],
                                          LS[0:1, 33:34], None, ALU.is_equal))
        oh_row = fence("v", sv)
        ohw = op("g", lambda e: e.dma_start(d_s3[0:1, 0:8], LS[0:1, 40:48]),
                 waits=[Wv(sv, oh_row)], inc=sdg, by=16)
        oh_bc = op("g", lambda e: e.dma_start(
            OH[:, :], bass.AP(d_s3.tensor, 0, [[0, 128], [1, 8]])),
            waits=[Wv(sdg, ohw)], inc=sdg, by=16)

        sel_d = 0
        dma_p = {}
        dma_p[0] = op("y", lambda e: e.dma_start(WASL[:, 0:512], d_wadn[0, :, :]),
                      waits=[Wv(sp, wb_cast[NDC - 1])], inc=sdy, by=16)
        dma_p[1] = op("y", lambda e: e.dma_start(WASL[:, 512:1024], d_wadn[1, :, :]),
                      inc=sdy, by=16)
        stt_d = {}
        for e_ in range(NE):
            waits = [Wv(sdy, dma_p[e_]), Wv(sdg, oh_bc)]
            if e_ == 0:
                stt_d[0] = op("v", lambda e: e.tensor_scalar(
                    WDSEL[:, :], WASL[:, 0:512], OH[:, 0:1], None, ALU.mult),
                    waits=waits, inc=sv)
            else:
                drain("v")
                stt_d[e_] = op("v", lambda e, e_=e_: e.scalar_tensor_tensor(
                    WDSEL[:, :], WASL[:, (e_ % 2) * 512:(e_ % 2) * 512 + 512],
                    OH[:, e_:e_ + 1], WDSEL[:, :], ALU.mult, ALU.add),
                    waits=waits, inc=sv)
            sel_d = stt_d[e_]
            if e_ == NE - 1:
                sel_d = fence("v", sv)
            if e_ + 2 < NE:
                dma_p[e_ + 2] = op("y", lambda e, e_=e_: e.dma_start(
                    WASL[:, (e_ % 2) * 512:(e_ % 2) * 512 + 512],
                    d_wadn[e_ + 2, :, :]), waits=[Wv(sv, stt_d[e_])], inc=sdy, by=16)
        sel_u = 0
        for e_ in range(NE):
            wl = [Wv(sv, sel_u)] if e_ >= 1 else [Wv(sv, sel_d)]
            ud = op("y", lambda e, e_=e_: e.dma_start(
                WASL[0:64, :], d_waup[e_, :, :]), waits=wl, inc=sdy, by=16)
            if e_ == 0:
                sel_u = op("v", lambda e: e.tensor_scalar(
                    WAUS[:, :], WASL[0:64, :], OH[0:64, 0:1], None, ALU.mult),
                    waits=[Wv(sdy, ud)], inc=sv)
            else:
                drain("v")
                sel_u = op("v", lambda e, e_=e_: e.scalar_tensor_tensor(
                    WAUS[:, :], WASL[0:64, :], OH[0:64, e_:e_ + 1], WAUS[:, :],
                    ALU.mult, ALU.add), waits=[Wv(sdy, ud)], inc=sv)
                if e_ == NE - 1:
                    sel_u = fence("v", sv)
        op("v", lambda e: e.tensor_mul(BADS[:, 0:1], BAD[:, 0:1], OH[0:64, 0:1]),
           waits=[Wv(sdg, oh_bc)])
        for e_ in range(1, NE):
            drain("v")
            op("v", lambda e, e_=e_: e.scalar_tensor_tensor(
                BADS[:, 0:1], BAD[:, e_:e_ + 1], OH[0:64, e_:e_ + 1], BADS[:, 0:1],
                ALU.mult, ALU.add))
        drain("v")
        op("v", lambda e: e.tensor_scalar_mul(BADS[:, 0:1], BADS[:, 0:1], 0.5))
        bads_done = fence("v", sv)
        for oc in range(NDC):
            op("v", lambda e, oc=oc: e.tensor_mul(
                BCT[:, 0:8], BAU[:, oc * 8:oc * 8 + 8], OH[:, :]),
                waits=[Wv(sdg, oh_bc)] if oc == 0 else ())
            drain("v")
            op("v", lambda e, oc=oc: e.tensor_reduce(BCOMB[:, oc:oc + 1], BCT[:, 0:8],
                                                     AX.X, ALU.add))
            drain("v")
        op("v", lambda e: e.tensor_scalar_mul(BCOMB[:, :], BCOMB[:, :], 0.5))
        bcomb_done = fence("v", sv)

        arelu = {}
        for tt in range(4):
            pda = SPA[0:64, (tt % 2) * 512:(tt % 2) * 512 + 512]
            cnt = 0
            for dc in range(NDC):
                waits = []
                if dc == 0:
                    waits = [Wv(sv, sel_d), Wv(sv, bads_done)]
                    if tt >= 2:
                        waits.append(Wv(ss, arelu[tt - 2]))
                cnt = op("t", lambda e, dc=dc, pda=pda, tt=tt: e.matmul(
                    pda, r2(WDSEL[:, :], 64)[:, dc, :],
                    xl3t[:, dc, tt * 512:tt * 512 + 512],
                    start=(dc == 0), stop=(dc == 7)),
                    waits=waits, inc=ts if dc == 7 else None)
            op("s", lambda e, pda=pda, tt=tt: e.activation(
                ASB[:, tt * 512:tt * 512 + 512], pda, AF.Relu, scale=0.5,
                bias=BADS[:, 0:1]), waits=[Wv(ts, cnt), Wv(sv, bads_done)])
            arelu[tt] = fence("s", ss)

        ev_done = {}
        for oc in range(NDC):
            for tt in range(4):
                j = oc * 4 + tt
                pbu = SPB[:, (j % 2) * 512:(j % 2) * 512 + 512]
                waits = [Wv(ss, arelu[tt]), Wv(sv, sel_u), Wv(sv, bcomb_done)]
                if j >= 2:
                    waits.append(Wv(sv, ev_done[j - 2]))
                else:
                    waits.append(Wv(ss, sq_end[(k_total - 1, NDC - 1 - (1 - j))]))
                cnt = op("t", lambda e, oc=oc, tt=tt, pbu=pbu: e.matmul(
                    pbu, WAUS[:, oc * 128:oc * 128 + 128],
                    ASB[:, tt * 512:tt * 512 + 512], start=True, stop=True),
                    waits=waits, inc=ts)
                ev_done[j] = op("v", lambda e, oc=oc, tt=tt, pbu=pbu:
                                e.scalar_tensor_tensor(
                                    xr3t[:, oc, tt * 512:tt * 512 + 512], pbu,
                                    BCOMB[:, oc:oc + 1],
                                    xr3t[:, oc, tt * 512:tt * 512 + 512],
                                    ALU.add, ALU.add),
                                waits=[Wv(ts, cnt), Wv(sv, ev_b[NDC * 4 - 1])],
                                inc=sv)
        return {"last_v": fence("v", sv)}

    def emit_attn(ffn1):
        spill = op("y", lambda e: e.dma_start(d_xdr[:, :, :], r2(XR[:, :], T)),
                   waits=[Wv(sv, ffn1["last_v"])], inc=sdy, by=16)
        bar = [Wv(ts, ts.n), Wv(ss, ss.n), Wv(sv, sv.n), Wv(sp, sp.n),
               Wv(sdy, sdy.n), Wv(sdg, sdg.n)]
        bv_ld = op("y", lambda e: e.dma_start(LROW[0:1, :], d_bvr), waits=bar,
                   inc=sdy, by=16)
        op("g", lambda e: e.tensor_copy(BVR[0:1, :], LROW[0:1, :]),
           waits=[Wv(sdy, bv_ld), *bar])
        bvr_c = fence("g", sp)
        qk_cast = []
        for dc in range(NDC):
            ld = op("y", lambda e, dc=dc: e.dma_start(RCBT[:, :], d_wqk[:, dc, :]),
                    waits=[Wv(sp, qk_cast[-1])] if dc else bar, inc=sdy, by=16)
            op("g", lambda e, dc=dc: e.tensor_copy(
                WQTQK[:, dc * 2048:dc * 2048 + 2048], RCBT[:, :]),
                waits=[Wv(sdy, ld)])
            qk_cast.append(fence("g", sp))
        wog_cast = []
        for gg in range(8):
            ld = op("y", lambda e, gg=gg: e.dma_start(
                r2(RCBT[0:64, :], 1024),
                bass.AP(d_wog.tensor, gg * 2 * 64 * 1024,
                        [[1024, 64], [65536, 2], [1, 1024]])),
                waits=[Wv(sp, qk_cast[-1] if gg == 0 else wog_cast[-1])],
                inc=sdy, by=16)
            op("g", lambda e, gg=gg: e.tensor_copy(
                WOG[:, gg * 2048:gg * 2048 + 2048], RCBT[0:64, :]),
                waits=[Wv(sdy, ld)])
            wog_cast.append(fence("g", sp))
        vsb_ones = op("v", lambda e: e.memset(
            r3(VSB[:, :], 16, 65)[:, :, :, 64:65], 1.0), waits=bar, inc=sv)

        xrq_load = {0: op("y", lambda e: e.dma_start(
            r2(XRQ[:, :], 512), d_xdr[:, :, 0:512]),
            waits=[Wv(sdy, spill)], inc=sdy, by=16)}

        def xrq_of(q):
            return r2((XRQ if q % 2 == 0 else XRQ2)[:, :], 512)

        qkv3 = r3(QKV[:, :], HQ, 2048)
        vsb4 = r3(VSB[:, :], 16, 65)
        xlq3 = r2(XLQ[:, :], 512)
        wqk4 = r3(WQTQK[:, :], 16, 128)
        vals3 = r2(QKV[0:64, 0:8192], 2048)

        qkv_last = 0
        last_wo_ev = 0
        wbs = {}
        for q in range(NQ):
            war = [Wv(sdy, xrq_load[q]), Wv(sp, bvr_c)]
            if q > 0:
                war.append(Wv(ts, qkv_last))
            lnd = emit_ln(xrq_of(q), lambda dc: xlq3[:, dc, :],
                          LNSQA, MBRBA, False, war)
            pq_banks = [SPA[64:128, 0:512], SPA[64:128, 512:1024],
                        SPB[64:128, 0:512], SPB[64:128, 512:1024]]
            ev_cnt = {}
            for g in range(16):
                for qk in range(2):
                    i = g * 2 + qk
                    pqk = pq_banks[i % 4]
                    cnt = 0
                    for dc in range(NDC):
                        waits = []
                        if dc == 0:
                            waits = [Wv(sv, lnd["v"]), Wv(sp, lnd["p"]),
                                     Wv(sp, qk_cast[-1])]
                            if i >= 4:
                                c, wch = ev_cnt[i - 4]
                                waits.append(Wv(ss if wch == 0 else sv, c))
                        cnt = op("t", lambda e, g=g, qk=qk, dc=dc, pqk=pqk:
                                 e.matmul(pqk,
                                          wqk4[:, dc, g, qk * 64:qk * 64 + 64],
                                          xlq3[:, dc, :], start=(dc == 0),
                                          stop=(dc == 7), tile_position=(0, 64)),
                                 waits=waits, inc=ts if dc == 7 else None)
                    dst = qkv3[64:128, qk, :, g * 128:g * 128 + 128]
                    src = r2(pqk, 128)
                    if qk == 0:
                        c = op("s", lambda e, dst=dst, src=src, g=g: e.add(
                            dst, src, BQ2[64:128, g:g + 1]),
                            waits=[Wv(ts, cnt)], inc=ss)
                        ev_cnt[i] = (c, 0)
                    else:
                        c = op("v", lambda e, dst=dst, src=src, g=g: e.tensor_scalar(
                            dst, src, BK2[64:128, g:g + 1], None, ALU.add),
                            waits=[Wv(ts, cnt)], inc=sv)
                        ev_cnt[i] = (c, 1)
            qkv_ev_s = ev_cnt[30][0]
            qkv_ev_v = ev_cnt[31][0]
            pv_of = [SPA[:, 0:1024], SPB[:, 0:1024], VP[:, 0:1024], VP[:, 1024:2048]]
            wv_dma = {}
            vcast = {}
            vmm = {}
            for dc in range(NDC):
                wl = [Wv(ts, vmm.get((3, dc - 2), 0))] if dc >= 2 else \
                    [Wv(ss, qkv_ev_s), Wv(sv, qkv_ev_v)]
                wv_dma[dc] = op("y", lambda e, dc=dc: e.dma_start(
                    RCBT[:, (dc % 2) * 1024:(dc % 2) * 1024 + 1024],
                    d_wv[:, dc, :]), waits=wl, inc=sdy, by=16)
                op("g", lambda e, dc=dc: e.tensor_copy(
                    r2(WQTVC[:, :], 1024)[:, dc % 2, :],
                    RCBT[:, (dc % 2) * 1024:(dc % 2) * 1024 + 1024]),
                    waits=[Wv(sdy, wv_dma[dc])])
                vcast[dc] = fence("g", sp)
                for h in range(HQ):
                    pv = pv_of[h]
                    cnt = 0
                    for half in range(2):
                        waits = []
                        if half == 0:
                            waits.append(Wv(sp, vcast[dc]))
                            if dc == 0:
                                waits.append(Wv(ss, qkv_ev_s))
                                waits.append(Wv(sv, qkv_ev_v))
                        cnt = op("t", lambda e, h=h, dc=dc, half=half, pv=pv:
                                 e.matmul(pv[:, half * 512:half * 512 + 512],
                                          xlq3[:, dc, h * 128:h * 128 + 128],
                                          r2(WQTVC[:, :], 1024)[
                                              :, dc % 2,
                                              half * 512:half * 512 + 512],
                                          start=(dc == 0), stop=False),
                                 waits=waits,
                                 inc=ts if (half == 1 and h == HQ - 1) else None)
                    vmm[(h, dc)] = cnt
            vev = {}
            for h in range(HQ):
                pv = pv_of[h]
                cnt = 0
                for half in range(2):
                    cnt = op("t", lambda e, h=h, half=half, pv=pv: e.matmul(
                        pv[:, half * 512:half * 512 + 512], ONESB[:, :],
                        BVR[0:1, half * 512:half * 512 + 512],
                        start=False, stop=True),
                        waits=[Wv(sp, bvr_c)] if half == 0 else (),
                        inc=ts if half == 1 else None)
                op("v", lambda e, h=h, pv=pv: e.tensor_copy(
                    vsb4[:, h, :, 0:64], r2(pv, 64)),
                    waits=[Wv(ts, cnt), Wv(sv, vsb_ones)])
                vev[h] = fence("v", sv)
            exp_cnt = {}
            scores_cnt = {}
            attnv_cnt = {}
            dance_done = {}
            for h in range(HQ):
                units = [(t1h, g2) for t1h in range(2) for g2 in range(16)]
                for ui, (t1h, g2) in enumerate(units):
                    u = ui
                    sp_t = SPA if u % 2 == 0 else SPB
                    spu = sp_t[:, 0:1024]
                    cnt = 0
                    for kk in range(2):
                        waits = []
                        if kk == 0:
                            if (h, u - 2) in exp_cnt:
                                waits.append(Wv(ss, exp_cnt[(h, u - 2)]))
                            elif (h - 1, u + 30) in exp_cnt:
                                waits.append(Wv(ss, exp_cnt[(h - 1, u + 30)]))
                            else:
                                waits.append(Wv(sv, vev[HQ - 1]))
                        cnt = op("t", lambda e, h=h, t1h=t1h, g2=g2, kk=kk, spu=spu:
                                 e.matmul(
                                     spu[:, kk * 512:kk * 512 + 512],
                                     qkv3[64:128, 1, h, g2 * 128:g2 * 128 + 128],
                                     qkv3[64:128, 0, h, t1h * 1024 + kk * 512:
                                          t1h * 1024 + kk * 512 + 512],
                                     start=True, stop=True),
                                 waits=waits, inc=ts if kk == 1 else None)
                    scores_cnt[(h, u)] = cnt
                    exp_cnt[(h, u)] = op(
                        "s", lambda e, spu=spu, u=u: e.activation(
                            ES[:, (u % 2) * 1024:(u % 2) * 1024 + 1024], spu,
                            AF.Exp, scale=0.125),
                        waits=[Wv(ts, cnt)], inc=ss)
                    if ui >= 1:
                        emit_attnv(h, units[ui - 1][0], units[ui - 1][1],
                                   exp_cnt, attnv_cnt, dance_done, vev)
                    if ui == len(units) - 1:
                        emit_attnv(h, t1h, g2, exp_cnt, attnv_cnt, dance_done, vev)
                        for th in range(2):
                            emit_dance(h, th, attnv_cnt, dance_done)
            wo_ev = {}
            for oc in range(NDC):
                pwo = [SPA[:, 0:512], SPA[:, 512:1024], SPB[:, 0:512],
                       SPB[:, 512:1024]][oc % 4]
                cnt = 0
                for g1 in range(16):
                    waits = []
                    if g1 == 0:
                        waits = [Wv(sv, dance_done[(HQ - 1, 1)]),
                                 Wv(sp, wog_cast[-1])]
                        if oc >= 4:
                            waits.append(Wv(sv, wo_ev[oc - 4]))
                    cnt = op("t", lambda e, oc=oc, g1=g1, pwo=pwo: e.matmul(
                        pwo, WOG[:, g1 * 1024 + oc * 128:g1 * 1024 + oc * 128 + 128],
                        vals3[:, :, g1 * 128:g1 * 128 + 128],
                        start=(g1 == 0), stop=(g1 == 15)),
                        waits=waits, inc=ts if g1 == 15 else None)
                op("v", lambda e, oc=oc, pwo=pwo, q=q: e.scalar_tensor_tensor(
                    xrq_of(q)[:, oc, :], pwo, BO[:, oc:oc + 1],
                    xrq_of(q)[:, oc, :], ALU.add, ALU.add),
                    waits=[Wv(ts, cnt)])
                wo_ev[oc] = fence("v", sv)
            last_wo_ev = wo_ev[NDC - 1]
            wb = op("y", lambda e, q=q: e.dma_start(
                d_xdr[:, :, q * 512:q * 512 + 512], xrq_of(q)),
                waits=[Wv(sv, wo_ev[NDC - 1])] + (
                    [Wv(sdy, spill)] if q == 0 else []), inc=sdy, by=16)
            wbs[q] = wb
            if q + 1 < NQ:
                wl = [Wv(sdy, spill)]
                if q >= 1:
                    wl.append(Wv(sdy, wbs[q - 1]))
                xrq_load[q + 1] = op("y", lambda e, q=q: e.dma_start(
                    xrq_of(q + 1), d_xdr[:, :, (q + 1) * 512:(q + 1) * 512 + 512]),
                    waits=wl, inc=sdy, by=16)
            qkv_last = ts.n
        return {"wb_final": sdy.n, "last_v": last_wo_ev, "wb": wbs}

    def emit_attnv(h, t1h, g2, exp_cnt, attnv_cnt, dance_done, vev):
        u = t1h * 16 + g2
        esl = ES[:, (u % 2) * 1024:(u % 2) * 1024 + 1024]
        vph = VP[0:65, t1h * 1024:t1h * 1024 + 1024]
        vsb4 = r3(VSB[:, :], 16, 65)
        cnt = 0
        for kk in range(2):
            waits = []
            if kk == 0:
                waits.append(Wv(ss, exp_cnt[(h, u)]))
                if g2 == 0:
                    dk = dance_done.get((h - 1, t1h))
                    if dk is not None:
                        waits.append(Wv(sv, dk))
                    else:
                        waits.append(Wv(sv, vev[HQ - 1]))
            cnt = op("t", lambda e, h=h, g2=g2, kk=kk, vph=vph, esl=esl: e.matmul(
                vph[:, kk * 512:kk * 512 + 512], vsb4[:, h, g2, :],
                esl[:, kk * 512:kk * 512 + 512],
                start=(g2 == 0), stop=(g2 == 15)),
                waits=waits, inc=ts if (kk == 1 and g2 == 15) else None)
        if g2 == 15:
            attnv_cnt[(h, t1h)] = cnt

    _dprev = {}

    def emit_dance(h, t1h, attnv_cnt, dance_done):
        t0 = t1h * 1024
        op("v", lambda e, t0=t0: e.tensor_copy(
            RCBT[64:65, t0:t0 + 1024], VP[64:65, t0:t0 + 1024]),
            waits=[Wv(ts, attnv_cnt[(h, t1h)])])
        cp = fence("v", sv)
        d1_ = op("g", lambda e, t0=t0: e.dma_start(
            d_s1[0:1, t0:t0 + 1024], RCBT[64:65, t0:t0 + 1024]),
            waits=[Wv(sv, cp), Wv(sdg, _dprev.get(("s1", t0), 0))], inc=sdg, by=16)
        d2_ = op("g", lambda e, t0=t0: e.dma_start(
            RTRC[:, 0:8], bass.AP(d_s1.tensor, t0, [[8, 128], [1, 8]])),
            waits=[Wv(sdg, d1_)], inc=sdg, by=16)
        op("v", lambda e: e.reciprocal(RTRC[:, 16:24], RTRC[:, 0:8]),
                waits=[Wv(sdg, d2_)])
        rc = fence("v", sv)
        d3_ = op("g", lambda e, t0=t0: e.dma_start(
            bass.AP(d_s2.tensor, t0, [[8, 128], [1, 8]]), RTRC[:, 16:24]),
            waits=[Wv(sv, rc), Wv(sdg, _dprev.get(("s2", t0), 0))], inc=sdg, by=16)
        d4_ = op("g", lambda e, t0=t0: e.dma_start(
            RCBT[0:64, t0:t0 + 1024],
            bass.AP(d_s2.tensor, t0, [[0, 64], [1, 1024]])),
            waits=[Wv(sdg, d3_)], inc=sdg, by=16)
        _dprev[("s1", t0)] = d2_
        _dprev[("s2", t0)] = d4_
        op("v", lambda e, h=h, t0=t0: e.tensor_tensor(
            QKV[0:64, h * 2048 + t0:h * 2048 + t0 + 1024],
            VP[0:64, t0:t0 + 1024], RCBT[0:64, t0:t0 + 1024], ALU.mult),
            waits=[Wv(sdg, d4_)])
        dance_done[(h, t1h)] = fence("v", sv)

    ffn1 = emit_ffn(0, xdma)
    attn = emit_attn(ffn1)
    bar2 = [Wv(ts, ts.n), Wv(ss, ss.n), Wv(sv, sv.n), Wv(sp, sp.n), Wv(sdg, sdg.n)]
    xrl = []
    for tb in range(NTB):
        xrl.append(op("y", lambda e, tb=tb: e.dma_start(
            r2(XR[:, :], T)[:, :, tb * TBS:(tb + 1) * TBS],
            d_xdr[:, :, tb * TBS:(tb + 1) * TBS]),
            waits=[*bar2, Wv(sdy, attn["wb_final"])] if tb == 0 else
            [Wv(sdy, attn["wb"][tb])], inc=sdy, by=16))
    ffn2 = emit_ffn(1, xrl)
    for dc in range(NDC):
        op("y", lambda e, dc=dc: e.dma_start(
            d_y[:, dc, :], r2(XR[:, :], T)[:, dc, :]),
            waits=[Wv(sv, ffn2["last_v"])] if dc == 0 else (), inc=sdy, by=16)
    fin_y = sdy.n
    op("y", lambda e: e.wait_ge(sdy.h, fin_y))
    op("g", lambda e: e.wait_ge(sdy.h, fin_y))

    with nc.Block() as block:
        @block.tensor
        def _(e):
            for eng, fn in ops:
                if eng == "t":
                    fn(e)

        @block.scalar
        def _(e):
            for eng, fn in ops:
                if eng == "s":
                    fn(e)

        @block.vector
        def _(e):
            for eng, fn in ops:
                if eng == "v":
                    fn(e)

        @block.gpsimd
        def _(e):
            for eng, fn in ops:
                if eng == "g":
                    fn(e)

        @block.sync
        def _(e):
            for eng, fn in ops:
                if eng == "y":
                    fn(e)

    es.close()
    return nc


def _prep(kw):
    x = kw["L0_attn_x"]
    wqkv = kw["L0_attn_Wqkv"]
    bqkv = kw["L0_attn_bqkv"]
    wo = kw["L0_attn_Wo"]
    bo = kw["L0_attn_bo"]
    wbase = kw["L0_ffn_W_base"]
    bbase = kw["L0_ffn_b_base"]
    wd1 = kw["L0_ffn_Wd1"]
    bd1 = kw["L0_ffn_bd1"]
    wd2 = kw["L0_ffn_Wd2"]
    bd2 = kw["L0_ffn_bd2"]
    wadn = kw["L0_ffn_Wa_down"]
    badn = kw["L0_ffn_ba_down"]
    waup = kw["L0_ffn_Wa_up"]
    baup = kw["L0_ffn_ba_up"]

    f = np.float32

    def c(a):
        return np.ascontiguousarray(np.asarray(a), dtype=f)

    shared = {}
    shared["wd1"] = c(np.asarray(wd1).reshape(8, 8, 128, 256)
                      .transpose(0, 2, 1, 3).reshape(8, 128, 2048))
    shared["wd2"] = c(np.asarray(wd2).reshape(8, 2, 128, 8, 128)
                      .transpose(0, 2, 1, 3, 4).reshape(8, 128, 2048))
    shared["wbase"] = c(np.asarray(wbase).reshape(8, 128, 8, 128)
                        .transpose(2, 1, 0, 3).reshape(8, 128, 1024))
    shared["wadn"] = c(np.asarray(wadn).reshape(8, 8, 128, 64)
                       .transpose(0, 2, 1, 3).reshape(8, 128, 512))
    shared["waup"] = c(waup)
    wq4 = np.asarray(wqkv).reshape(16, 3, 64, 8, 128)
    shared["wqk"] = c(wq4[:, 0:2].transpose(4, 3, 0, 1, 2).reshape(128, 8, 2048))
    shared["wv"] = c(wq4[:, 2].transpose(3, 2, 0, 1).reshape(128, 8, 1024))
    shared["wog"] = c(np.asarray(wo).T.reshape(16, 64, 1024))
    shared["nid"] = c(-np.eye(128))
    shared["bd1"] = c(np.asarray(bd1).reshape(8, 2, 128).transpose(2, 0, 1)
                      .reshape(128, 16))
    shared["bd2"] = c(np.asarray(bd2).reshape(8, 8, 128).transpose(2, 0, 1)
                      .reshape(128, 64))
    shared["bb"] = c(np.asarray(bbase).reshape(8, 128).T)
    shared["bo"] = c(np.asarray(bo).reshape(8, 128).T)
    shared["bad"] = c(np.asarray(badn).T)
    shared["bau"] = c(np.asarray(baup).reshape(8, 8, 128).transpose(2, 1, 0)
                      .reshape(128, 64))
    shared["pw8"] = c((2.0 ** -np.arange(8)).reshape(1, 8))
    bqr = np.asarray(bqkv).reshape(16, 3, 64)
    bq = np.zeros((128, 16), f)
    bk = np.zeros((128, 16), f)
    bq[64:128, :] = bqr[:, 0, :].T
    bk[64:128, :] = bqr[:, 1, :].T
    shared["bq2"] = bq
    shared["bk2"] = bk
    shared["bvr"] = c(bqr[:, 2, :].reshape(1, 1024))

    in_maps = []
    for ci in range(NC):
        m = dict(shared)
        m["x"] = c(np.asarray(x[ci]).reshape(2048, 8, 128).transpose(2, 1, 0))
        in_maps.append(m)
    return in_maps


def kernel(**kw):
    if "nc" not in _CACHE:
        _CACHE["nc"] = build_program()
    nc = _CACHE["nc"]
    in_maps = _prep(kw)
    res = run_bass_kernel_spmd(nc, in_maps, core_ids=list(range(NC)))
    outs = []
    for ci in range(NC):
        ycm = np.asarray(res.results[ci]["y"])
        outs.append(ycm.transpose(2, 1, 0).reshape(2048, 1024))
    return np.stack(outs, axis=0).astype(np.float32)


# version 28 (final)
# speedup vs baseline: 1.0559x; 1.0487x over previous
import contextlib
import numpy as np
import concourse.bass as bass
import concourse.mybir as mybir
from concourse.bass_utils import run_bass_kernel_spmd

F32 = mybir.dt.float32
F32R = mybir.dt.float32r
BF16 = mybir.dt.bfloat16
AF = mybir.ActivationFunctionType
ALU = mybir.AluOpType
AX = mybir.AxisListType

NC = 8
T = 2048
NDC = 8
NTB = 4
TBS = 512
NE = 8
HQ = 4
NQ = 4
EPS = 1e-5

_CACHE = {}


class Ctr:
    __slots__ = ("h", "n")

    def __init__(self, h):
        self.h = h
        self.n = 0


def r2(ap, b):
    return ap.rearrange("p (a b) -> p a b", b=b)


def r3(ap, b, c):
    return ap.rearrange("p (a b c) -> p a b c", b=b, c=c)


def build_program():
    nc = bass.Bass("TRN2", target_bir_lowering=False)

    d_x = nc.dram_tensor("x", [128, NDC, T], F32, kind="ExternalInput").ap()
    d_y = nc.dram_tensor("y", [128, NDC, T], F32, kind="ExternalOutput").ap()
    d_wd1 = nc.dram_tensor("wd1", [NE, 128, 2048], F32R, kind="ExternalInput").ap()
    d_wd2 = nc.dram_tensor("wd2", [NE, 128, 2048], F32R, kind="ExternalInput").ap()
    d_wbase = nc.dram_tensor("wbase", [NDC, 128, 1024], F32, kind="ExternalInput").ap()
    d_wadn = nc.dram_tensor("wadn", [NE, 128, 512], F32, kind="ExternalInput").ap()
    d_waup = nc.dram_tensor("waup", [NE, 64, 1024], F32, kind="ExternalInput").ap()
    d_wqk = nc.dram_tensor("wqk", [128, NDC, 2048], F32, kind="ExternalInput").ap()
    d_wv = nc.dram_tensor("wv", [128, NDC, 1024], F32, kind="ExternalInput").ap()
    d_wog = nc.dram_tensor("wog", [16, 64, 1024], F32, kind="ExternalInput").ap()
    d_nid = nc.dram_tensor("nid", [128, 128], F32R, kind="ExternalInput").ap()
    d_bd1 = nc.dram_tensor("bd1", [128, 16], F32, kind="ExternalInput").ap()
    d_bd2 = nc.dram_tensor("bd2", [128, 64], F32, kind="ExternalInput").ap()
    d_bb = nc.dram_tensor("bb", [128, 8], F32, kind="ExternalInput").ap()
    d_bo = nc.dram_tensor("bo", [128, 8], F32, kind="ExternalInput").ap()
    d_bad = nc.dram_tensor("bad", [64, 8], F32, kind="ExternalInput").ap()
    d_bau = nc.dram_tensor("bau", [128, 64], F32, kind="ExternalInput").ap()
    d_pw8 = nc.dram_tensor("pw8", [1, 8], F32, kind="ExternalInput").ap()
    d_bq2 = nc.dram_tensor("bq2", [128, 16], F32, kind="ExternalInput").ap()
    d_bk2 = nc.dram_tensor("bk2", [128, 16], F32, kind="ExternalInput").ap()
    d_bvr = nc.dram_tensor("bvr", [1, 1024], F32, kind="ExternalInput").ap()
    d_xdr = nc.dram_tensor("xdr", [128, NDC, T], F32, kind="Internal").ap()
    d_s1 = nc.dram_tensor("dsc1", [1, 2048], F32, kind="Internal").ap()
    d_s2 = nc.dram_tensor("dsc2", [1, 2048], F32, kind="Internal").ap()
    d_sb = nc.dram_tensor("dscb", [1, 1024], BF16, kind="Internal").ap()
    d_s3 = nc.dram_tensor("dsc3", [1, 64], F32, kind="Internal").ap()

    SB0 = 16512
    _A = nc.alloc_sbuf_tensor_at

    def A(name, shape, dtype, offset):
        return _A(name, shape, dtype, offset=SB0 + offset)
    NID = A("NID", [128, 128], F32R, offset=0)
    BD1 = A("BD1", [128, 16], F32, offset=512)
    BD2 = A("BD2", [128, 64], F32, offset=576)
    BB = A("BB", [128, 8], F32, offset=832)
    BBH = A("BBH", [128, 8], F32, offset=864)
    BO = A("BO", [128, 8], F32, offset=896)
    BAU = A("BAU", [128, 64], F32, offset=928)
    BAD = A("BAD", [64, 8], F32, offset=1184)
    BADS = A("BADS", [64, 8], F32, offset=1216)
    OH = A("OH", [128, 8], F32, offset=1248)
    BCT = A("BCT", [128, 8], F32, offset=1280)
    BCOMB = A("BCOMB", [128, 8], F32, offset=1312)
    PW8 = A("PW8", [1, 8], F32, offset=1344)
    LS = A("LS", [1, 64], F32, offset=1376)
    LA = A("LA", [128, 256], F32, offset=1632)
    LP = A("LP", [128, 8], F32, offset=2656)
    ONESC = A("ONESC", [128, 16], BF16, offset=2688)
    ONES1C = A("ONES1C", [128, 8], F32, offset=2720)
    ONESB = A("ONESB", [1, 128], BF16, offset=2752)
    LROW = A("LROW", [1, 1024], F32, offset=3008)
    LROWB = A("LROWB", [1, 1024], BF16, offset=7104)
    BQ2 = A("BQ2", [128, 16], F32, offset=9152)
    BK2 = A("BK2", [128, 16], F32, offset=9216)
    BVR = A("BVR", [1, 1024], BF16, offset=9280)
    RTRC = A("RTRC", [128, 32], F32, offset=11328)
    LROW2 = A("LROW2", [1, 1024], F32, offset=206848)
    LROWB2 = A("LROWB2", [1, 1024], BF16, offset=9280)
    EPSC = A("EPSC", [1, 8], F32, offset=11456)
    FEN = A("FEN", [1, 16], F32, offset=11488)
    XR = A("XR", [128, NDC * T], F32, offset=12288)
    XL = A("XL", [128, NDC * T], BF16, offset=77824)
    XLR = A("XLR", [128, 4096], F32R, offset=110592)
    XLR2 = A("XLR2", [128, 4096], F32R, offset=190464)
    WD1 = A("WD1", [128, 4096], F32R, offset=126976)
    WD2 = A("WD2", [128, 4096], F32R, offset=143360)
    HBR = A("HBR", [128, 1024], F32R, offset=159744)
    ASB = A("ASB", [64, 2048], BF16, offset=163840)
    LNSQ = A("LNSQ", [128, 4096], BF16, offset=167936)
    MBRB = A("MBRB", [128, 1024], BF16, offset=176128)
    SQSC = A("SQSC", [128, 512], BF16, offset=178176)
    WASL = A("WASL", [128, 1024], F32, offset=179200)
    WBC = A("WBC", [128, 2048], BF16, offset=183296)
    WDSEL = A("WDSEL", [128, 512], BF16, offset=187392)
    WAUS = A("WAUS", [64, 1024], BF16, offset=188416)
    WQTQK = A("WQTQK", [128, 16384], BF16, offset=12288)
    QKV = A("QKV", [128, 16384], BF16, offset=45056)
    XRQ = A("XRQ", [128, 4096], F32, offset=77824)
    XLQ = A("XLQ", [128, 4096], BF16, offset=94208)
    VSB = A("VSB", [128, 4160], BF16, offset=102400)
    WQTVC = A("WQTVC", [128, 2048], BF16, offset=110720)
    ES = A("ES", [128, 2048], BF16, offset=114816)
    RCBT = A("RCBT", [128, 2048], F32, offset=118912)
    WOG = A("WOG", [64, 16384], BF16, offset=127104)
    LNSQA = A("LNSQA", [128, 4096], BF16, offset=159872)
    MBRBA = A("MBRBA", [128, 1024], BF16, offset=168064)
    XRQ2 = A("XRQ2", [128, 4096], F32, offset=170112)

    SPA = nc.alloc_psum_tensor("SPA", [128, 1024], F32)
    SPB = nc.alloc_psum_tensor("SPB", [128, 1024], F32)
    VP = nc.alloc_psum_tensor("VP", [128, 2048], F32)

    es = contextlib.ExitStack()
    sdy = Ctr(es.enter_context(nc.semaphore("sdy")))
    sdg = Ctr(es.enter_context(nc.semaphore("sdg")))
    ts = Ctr(es.enter_context(nc.semaphore("ts")))
    ss = Ctr(es.enter_context(nc.semaphore("ss")))
    sv = Ctr(es.enter_context(nc.semaphore("sv")))
    sp = Ctr(es.enter_context(nc.semaphore("sp")))

    ops = []

    def op(eng, bf, waits=(), inc=None, by=1):
        wts = tuple((c, v) for (c, v) in waits if v > 0)
        if inc is not None:
            inc.n += by
        tgt = inc.n if inc is not None else 0

        def fn(e, bf=bf, wts=wts, inc=inc, by=by):
            for (c, val) in wts:
                e.wait_ge(c.h, val)
            ins = bf(e)
            if inc is not None:
                ins.then_inc(inc.h, by)
        ops.append((eng, fn))
        return tgt

    def Wv(c, v):
        return (c, v)

    def drain(eng):
        ops.append((eng, lambda e: e.drain()))

    _fsl = {"v": 0, "g": 2, "s": 4}

    def fence(eng, sem):
        drain(eng)
        i = _fsl[eng]
        if eng == "s":
            return op(eng, lambda e: e.square(FEN[0:1, i:i + 1], FEN[0:1, i:i + 1]),
                      inc=sem)
        return op(eng, lambda e: e.tensor_copy(FEN[0:1, i:i + 1], FEN[0:1, i + 1:i + 2]),
                  inc=sem)

    xdma = []
    for tb in range(NTB):
        xdma.append(op("y", lambda e, tb=tb: e.dma_start(
            r2(XR[:, :], T)[:, :, tb * TBS:(tb + 1) * TBS],
            d_x[:, :, tb * TBS:(tb + 1) * TBS]), inc=sdy, by=16))
    for dst, src in [(NID[:, :], d_nid), (BD1[:, :], d_bd1), (BD2[:, :], d_bd2),
                     (BB[:, :], d_bb), (BO[:, :], d_bo), (BAD[:, :], d_bad),
                     (BAU[:, :], d_bau), (PW8[:, :], d_pw8), (BQ2[:, :], d_bq2),
                     (BK2[:, :], d_bk2)]:
        op("y", lambda e, dst=dst, src=src: e.dma_start(dst, src), inc=sdy, by=16)
    consts_loaded = sdy.n
    op("v", lambda e: e.memset(ONESC[:, 0:1], 1.0 / 1024.0), inc=sv)
    op("v", lambda e: e.memset(ONES1C[:, 0:1], 1.0), inc=sv)
    op("v", lambda e: e.memset(ONESB[:, :], 1.0), inc=sv)
    op("v", lambda e: e.memset(EPSC[0:1, :], EPS), inc=sv)
    init_v = op("v", lambda e: e.tensor_scalar_mul(BBH[:, :], BB[:, :], 0.5),
                waits=[Wv(sdy, consts_loaded)], inc=sv)

    def emit_ln(xr3, xl3, lnsq, mbrb, want_xlr, war_wait, xlr3=None, lrow=None, lrowb=None):
        PM = VP[0:1, 1024:1536]
        PSQ = VP[0:1, 1536:2048]
        if lrow is None:
            lrow, lrowb = LROW, LROWB
        lnsq3 = r2(lnsq[:, :], 512)
        cast_done = []
        for dc in range(NDC):
            cast_done.append(op("v", lambda e, dc=dc: e.tensor_copy(
                xl3(dc), xr3[:, dc, :]), waits=war_wait if dc == 0 else (), inc=sv))
        sq = op("s", lambda e: e.activation(lnsq3, xr3, AF.Square),
                waits=war_wait, inc=ss)
        for dc in range(NDC):
            op("t", lambda e, dc=dc: e.matmul(
                PM, ONESC[:, 0:1], xl3(dc), start=(dc == 0), stop=(dc == NDC - 1)),
                waits=[Wv(sv, cast_done[dc]), Wv(sv, init_v)])
        mm_done = 0
        for dc in range(NDC):
            last = dc == NDC - 1
            mm_done = op("t", lambda e, dc=dc: e.matmul(
                PSQ, ONESC[:, 0:1], lnsq3[:, dc, :],
                start=(dc == 0), stop=(dc == NDC - 1)),
                waits=[Wv(ss, sq)] if dc == 0 else (), inc=ts if last else None)
        op("v", lambda e: e.tensor_copy(lrow[0:1, 0:512], PM), waits=[Wv(ts, mm_done)])
        drain("v")
        op("v", lambda e: e.tensor_copy(lrowb[0:1, 0:512], lrow[0:1, 0:512]))
        drain("v")
        op("v", lambda e: e.tensor_mul(lrow[0:1, 512:1024], lrow[0:1, 0:512],
                                       lrow[0:1, 0:512]))
        drain("v")
        var = op("v", lambda e: e.tensor_tensor(lrow[0:1, 512:1024], PSQ,
                                                lrow[0:1, 512:1024], ALU.subtract),
                 inc=sv)
        op("s", lambda e: e.activation(lrow[0:1, 0:512], lrow[0:1, 512:1024],
                                       AF.Ln, bias=EPSC[0:1, 0:1]), waits=[Wv(sv, var)])
        drain("s")
        rstd = op("s", lambda e: e.activation(lrowb[0:1, 512:1024], lrow[0:1, 0:512],
                                              AF.Exp, scale=-0.5), inc=ss)
        sbw = op("g", lambda e, lrowb=lrowb: e.dma_start(d_sb, lrowb[0:1, :]),
                 waits=[Wv(ss, rstd)], inc=sdg, by=16)
        bc = op("g", lambda e: e.dma_start(
            mbrb[:, :], bass.AP(d_sb.tensor, 0, [[0, 128], [1, 1024]])),
            waits=[Wv(sdg, sbw)], inc=sdg, by=16)
        vdone = []
        pdone = []
        for dc in range(4):
            tg = (xlr3[:, dc, :] if want_xlr else xl3(dc))
            vdone.append(op("v", lambda e, dc=dc, tg=tg: e.tensor_tensor(
                tg, xr3[:, dc, :], mbrb[:, 0:512], ALU.subtract),
                waits=[Wv(sdg, bc), Wv(ts, mm_done)], inc=sv))
        for dc in range(4, 8):
            tg = (xlr3[:, dc, :] if want_xlr else xl3(dc))
            pdone.append(op("g", lambda e, dc=dc, tg=tg: e.tensor_tensor(
                tg, xr3[:, dc, :], mbrb[:, 0:512], ALU.subtract),
                waits=[Wv(sdg, bc), Wv(ts, mm_done)], inc=sp))
        mdone_p = mdone_v = 0
        for dc in range(4):
            tg = (xlr3[:, dc, :] if want_xlr else xl3(dc))
            mdone_p = op("g", lambda e, dc=dc, tg=tg: e.tensor_tensor(
                tg, tg, mbrb[:, 512:1024], ALU.mult),
                waits=[Wv(sv, vdone[dc])], inc=sp)
        for dc in range(4, 8):
            tg = (xlr3[:, dc, :] if want_xlr else xl3(dc))
            mdone_v = op("v", lambda e, dc=dc, tg=tg: e.tensor_tensor(
                tg, tg, mbrb[:, 512:1024], ALU.mult),
                waits=[Wv(sp, pdone[dc - 4])], inc=sv)
        vf = fence("v", sv)
        if want_xlr:
            for dc in range(NDC):
                op("g", lambda e, dc=dc: e.tensor_copy(
                    xl3(dc), xlr3[:, dc, :]),
                    waits=[Wv(sv, mdone_v)] if dc == 0 else ())
        pf = fence("g", sp)
        return {"v": vf, "p": pf}

    def emit_ffn(call, xwait):
        xr3t = r2(XR[:, :], T)
        xl3t = r2(XL[:, :], T)
        k_total = NTB * NE

        d1_end = {}
        d2_end = {}
        relu_end = {}
        sq_end = {}
        wd1_ld = {}
        wd2_ld = {}

        def xlr_of(tb):
            return r2((XLR if tb % 2 == 0 else XLR2)[:, :], 512)

        def issue_wd1(k, waits):
            e_ = k % NE
            wd1_ld[k] = op("y", lambda e, k=k, e_=e_: e.dma_start(
                WD1[:, (k % 2) * 2048:(k % 2) * 2048 + 2048], d_wd1[e_, :, :]),
                waits=waits, inc=sdy, by=16)

        def issue_wd2(k, waits):
            e_ = k % NE
            wd2_ld[k] = op("y", lambda e, k=k, e_=e_: e.dma_start(
                WD2[:, (k % 2) * 2048:(k % 2) * 2048 + 2048], d_wd2[e_, :, :]),
                waits=waits, inc=sdy, by=16)

        issue_wd1(0, [Wv(sdy, consts_loaded)])
        issue_wd2(0, [])
        issue_wd1(1, [])
        issue_wd2(1, [])

        def emit_ln_tb(tb):
            war = [Wv(sdy, xwait[tb])]
            if tb >= 2:
                war.append(Wv(ts, d2_end[((tb - 2) * NE + NE - 1, NDC - 1)]))
            return emit_ln(
                xr3t[:, :, tb * TBS:(tb + 1) * TBS],
                lambda dc, tb=tb: xl3t[:, dc, tb * TBS:(tb + 1) * TBS],
                LNSQ, MBRB, True, war, xlr_of(tb),
                LROW if tb % 2 == 0 else LROW2,
                LROWB if tb % 2 == 0 else LROWB2)

        ln_done = {0: emit_ln_tb(0)}
        for tb in range(NTB):
            xlr3 = xlr_of(tb)
            for e_ in range(NE):
                k = tb * NE + e_
                if e_ == 1 and tb + 1 < NTB:
                    ln_done[tb + 1] = emit_ln_tb(tb + 1)
                wd1k = r2(WD1[:, (k % 2) * 2048:(k % 2) * 2048 + 2048], 256)
                wd2k = r2(WD2[:, (k % 2) * 2048:(k % 2) * 2048 + 2048], 1024)
                for hc in range(2):
                    pd1 = SPA[:, hc * 512:hc * 512 + 512]
                    cnt = 0
                    for dc in range(NDC):
                        waits = []
                        if dc == 0:
                            waits = [Wv(sdy, wd1_ld[k]),
                                     Wv(sv, ln_done[tb]["v"]),
                                     Wv(sp, ln_done[tb]["p"])]
                            if k >= 1:
                                waits.append(Wv(ss, relu_end.get((k - 1, hc), 0)))
                                waits.append(Wv(ts, d2_end[(k - 1, NDC - 1)]))
                        cnt = op("t", lambda e, dc=dc, hc=hc, pd1=pd1, wd1k=wd1k,
                                 xlr3=xlr3:
                                 e.matmul(pd1, wd1k[:, dc, hc * 128:hc * 128 + 128],
                                          xlr3[:, dc, :],
                                          start=(dc == 0), stop=(dc == 7)),
                                 waits=waits, inc=ts if dc == 7 else None)
                    d1_end[k] = cnt
                    relu_end[(k, hc)] = op(
                        "s", lambda e, hc=hc, pd1=pd1, e_=e_: e.activation(
                            HBR[:, hc * 512:hc * 512 + 512], pd1, AF.Relu,
                            bias=BD1[:, e_ * 2 + hc:e_ * 2 + hc + 1]),
                        waits=[Wv(ts, cnt)], inc=ss)
                if k + 2 < k_total:
                    issue_wd1(k + 2, [Wv(ts, d1_end[k])])
                for oc in range(NDC):
                    pd2 = SPB[:, (oc % 2) * 512:(oc % 2) * 512 + 512]
                    waits = [Wv(ss, relu_end[(k, 1)]), Wv(sdy, wd2_ld[k])]
                    if (k, oc - 2) in sq_end:
                        waits.append(Wv(ss, sq_end[(k, oc - 2)]))
                    elif (k - 1, oc + 6) in sq_end:
                        waits.append(Wv(ss, sq_end[(k - 1, oc + 6)]))
                    for hc in range(2):
                        op("t", lambda e, hc=hc, oc=oc, pd2=pd2, wd2k=wd2k:
                           e.matmul(pd2, wd2k[:, hc, oc * 128:oc * 128 + 128],
                                    HBR[:, hc * 512:hc * 512 + 512],
                                    start=(hc == 0), stop=False),
                           waits=waits if hc == 0 else ())
                    cnt = op("t", lambda e, oc=oc, pd2=pd2, xlr3=xlr3: e.matmul(
                        pd2, NID[:, :], xlr3[:, oc, :], start=False, stop=True),
                        inc=ts)
                    d2_end[(k, oc)] = cnt
                    col = e_ * 32 + tb * 8 + oc
                    sq_end[(k, oc)] = op(
                        "s", lambda e, pd2=pd2, e_=e_, oc=oc, col=col: e.activation(
                            SQSC[:, :], pd2, AF.Square,
                            bias=BD2[:, e_ * 8 + oc:e_ * 8 + oc + 1],
                            accum_out=LA[:, col:col + 1]),
                        waits=[Wv(ts, cnt)], inc=ss)
                if k + 2 < k_total:
                    issue_wd2(k + 2, [Wv(ts, d2_end[(k, NDC - 1)])])

        wb_dma = {}
        wb_cast = {}
        bu_end = {}
        ev_b = {}
        wb_dma[0] = op("y", lambda e: e.dma_start(WASL[:, :], d_wbase[0, :, :]),
                       waits=[Wv(sdy, wd2_ld[k_total - 1])], inc=sdy, by=16)
        vp_banks = [VP[:, i * 512:i * 512 + 512] for i in range(4)]
        for oc in range(NDC):
            wl = [Wv(sdy, wb_dma[oc])]
            if oc >= 2:
                wl.append(Wv(ts, bu_end[(oc - 2, 3)]))
            op("g", lambda e, oc=oc: e.tensor_scalar_mul(
                WBC[:, (oc % 2) * 1024:(oc % 2) * 1024 + 1024], WASL[:, :], 0.5),
                waits=wl)
            wb_cast[oc] = fence("g", sp)
            if oc + 1 < NDC:
                wb_dma[oc + 1] = op("y", lambda e, oc=oc: e.dma_start(
                    WASL[:, :], d_wbase[oc + 1, :, :]),
                    waits=[Wv(sp, wb_cast[oc])], inc=sdy, by=16)
            for tt in range(4):
                j = oc * 4 + tt
                pbu = vp_banks[j % 4]
                waits = [Wv(sp, wb_cast[oc])]
                if j >= 4:
                    waits.append(Wv(sv, ev_b[j - 4]))
                else:
                    waits.append(Wv(sv, ln_done[NTB - 1]["v"]))
                cnt = 0
                for dc in range(NDC):
                    cnt = op("t", lambda e, oc=oc, tt=tt, dc=dc, pbu=pbu: e.matmul(
                        pbu, r2(WBC[:, (oc % 2) * 1024:(oc % 2) * 1024 + 1024],
                                128)[:, dc, :],
                        xl3t[:, dc, tt * 512:tt * 512 + 512],
                        start=(dc == 0), stop=(dc == 7)),
                        waits=waits if dc == 0 else (),
                        inc=ts if dc == 7 else None)
                bu_end[(oc, tt)] = cnt
                ev_b[j] = op("v", lambda e, oc=oc, tt=tt, pbu=pbu:
                             e.scalar_tensor_tensor(
                                 xr3t[:, oc, tt * 512:tt * 512 + 512], pbu,
                                 BBH[:, oc:oc + 1],
                                 xr3t[:, oc, tt * 512:tt * 512 + 512],
                                 ALU.add, ALU.add),
                             waits=[Wv(ts, cnt)], inc=sv)

        drain("s")
        sfence = op("s", lambda e: e.square(LS[0:1, 48:49], LS[0:1, 48:49]),
                    waits=[Wv(ss, sq_end[(k_total - 1, NDC - 1)])], inc=ss)
        lpd = 0
        for e_ in range(NE):
            lpd = op("v", lambda e, e_=e_: e.tensor_reduce(
                LP[:, e_:e_ + 1], LA[:, e_ * 32:e_ * 32 + 32], AX.X, ALU.add),
                waits=[Wv(ss, sfence)] if e_ == 0 else (),
                inc=sv)
        lpd = fence("v", sv)
        pl_mm = op("t", lambda e: e.matmul(SPA[0:1, 0:8], ONES1C[:, 0:1], LP[:, :],
                                           start=True, stop=True),
                   waits=[Wv(sv, lpd), Wv(ss, relu_end[(k_total - 1, 1)])], inc=ts)
        op("v", lambda e: e.tensor_copy(LS[0:1, 0:8], SPA[0:1, 0:8]),
           waits=[Wv(ts, pl_mm)])
        drain("v")
        op("v", lambda e: e.tensor_reduce(LS[0:1, 8:9], LS[0:1, 0:8], AX.X, ALU.min))
        drain("v")
        op("v", lambda e: e.tensor_scalar(LS[0:1, 9:17], LS[0:1, 0:8], LS[0:1, 8:9],
                                          None, ALU.is_equal))
        drain("v")
        op("v", lambda e: e.tensor_mul(LS[0:1, 25:33], LS[0:1, 9:17], PW8[0:1, :]))
        drain("v")
        op("v", lambda e: e.tensor_reduce(LS[0:1, 33:34], LS[0:1, 25:33], AX.X,
                                          ALU.max))
        drain("v")
        op("v", lambda e: e.tensor_scalar(LS[0:1, 40:48], LS[0:1, 25:33],
                                          LS[0:1, 33:34], None, ALU.is_equal))
        oh_row = fence("v", sv)
        ohw = op("g", lambda e: e.dma_start(d_s3[0:1, 0:8], LS[0:1, 40:48]),
                 waits=[Wv(sv, oh_row)], inc=sdg, by=16)
        oh_bc = op("g", lambda e: e.dma_start(
            OH[:, :], bass.AP(d_s3.tensor, 0, [[0, 128], [1, 8]])),
            waits=[Wv(sdg, ohw)], inc=sdg, by=16)

        sel_d = 0
        dma_p = {}
        dma_p[0] = op("y", lambda e: e.dma_start(WASL[:, 0:512], d_wadn[0, :, :]),
                      waits=[Wv(sp, wb_cast[NDC - 1])], inc=sdy, by=16)
        dma_p[1] = op("y", lambda e: e.dma_start(WASL[:, 512:1024], d_wadn[1, :, :]),
                      inc=sdy, by=16)
        stt_d = {}
        for e_ in range(NE):
            waits = [Wv(sdy, dma_p[e_]), Wv(sdg, oh_bc)]
            if e_ == 0:
                stt_d[0] = op("v", lambda e: e.tensor_scalar(
                    WDSEL[:, :], WASL[:, 0:512], OH[:, 0:1], None, ALU.mult),
                    waits=waits, inc=sv)
            else:
                drain("v")
                stt_d[e_] = op("v", lambda e, e_=e_: e.scalar_tensor_tensor(
                    WDSEL[:, :], WASL[:, (e_ % 2) * 512:(e_ % 2) * 512 + 512],
                    OH[:, e_:e_ + 1], WDSEL[:, :], ALU.mult, ALU.add),
                    waits=waits, inc=sv)
            sel_d = stt_d[e_]
            if e_ == NE - 1:
                sel_d = fence("v", sv)
            if e_ + 2 < NE:
                dma_p[e_ + 2] = op("y", lambda e, e_=e_: e.dma_start(
                    WASL[:, (e_ % 2) * 512:(e_ % 2) * 512 + 512],
                    d_wadn[e_ + 2, :, :]), waits=[Wv(sv, stt_d[e_])], inc=sdy, by=16)
        sel_u = 0
        for e_ in range(NE):
            wl = [Wv(sv, sel_u)] if e_ >= 1 else [Wv(sv, sel_d)]
            ud = op("y", lambda e, e_=e_: e.dma_start(
                WASL[0:64, :], d_waup[e_, :, :]), waits=wl, inc=sdy, by=16)
            if e_ == 0:
                sel_u = op("v", lambda e: e.tensor_scalar(
                    WAUS[:, :], WASL[0:64, :], OH[0:64, 0:1], None, ALU.mult),
                    waits=[Wv(sdy, ud)], inc=sv)
            else:
                drain("v")
                sel_u = op("v", lambda e, e_=e_: e.scalar_tensor_tensor(
                    WAUS[:, :], WASL[0:64, :], OH[0:64, e_:e_ + 1], WAUS[:, :],
                    ALU.mult, ALU.add), waits=[Wv(sdy, ud)], inc=sv)
                if e_ == NE - 1:
                    sel_u = fence("v", sv)
        op("v", lambda e: e.tensor_mul(BADS[:, 0:1], BAD[:, 0:1], OH[0:64, 0:1]),
           waits=[Wv(sdg, oh_bc)])
        for e_ in range(1, NE):
            drain("v")
            op("v", lambda e, e_=e_: e.scalar_tensor_tensor(
                BADS[:, 0:1], BAD[:, e_:e_ + 1], OH[0:64, e_:e_ + 1], BADS[:, 0:1],
                ALU.mult, ALU.add))
        drain("v")
        op("v", lambda e: e.tensor_scalar_mul(BADS[:, 0:1], BADS[:, 0:1], 0.5))
        bads_done = fence("v", sv)
        for oc in range(NDC):
            op("v", lambda e, oc=oc: e.tensor_mul(
                BCT[:, 0:8], BAU[:, oc * 8:oc * 8 + 8], OH[:, :]),
                waits=[Wv(sdg, oh_bc)] if oc == 0 else ())
            drain("v")
            op("v", lambda e, oc=oc: e.tensor_reduce(BCOMB[:, oc:oc + 1], BCT[:, 0:8],
                                                     AX.X, ALU.add))
            drain("v")
        op("v", lambda e: e.tensor_scalar_mul(BCOMB[:, :], BCOMB[:, :], 0.5))
        bcomb_done = fence("v", sv)

        arelu = {}
        for tt in range(4):
            pda = SPA[0:64, (tt % 2) * 512:(tt % 2) * 512 + 512]
            cnt = 0
            for dc in range(NDC):
                waits = []
                if dc == 0:
                    waits = [Wv(sv, sel_d), Wv(sv, bads_done)]
                    if tt >= 2:
                        waits.append(Wv(ss, arelu[tt - 2]))
                cnt = op("t", lambda e, dc=dc, pda=pda, tt=tt: e.matmul(
                    pda, r2(WDSEL[:, :], 64)[:, dc, :],
                    xl3t[:, dc, tt * 512:tt * 512 + 512],
                    start=(dc == 0), stop=(dc == 7)),
                    waits=waits, inc=ts if dc == 7 else None)
            op("s", lambda e, pda=pda, tt=tt: e.activation(
                ASB[:, tt * 512:tt * 512 + 512], pda, AF.Relu, scale=0.5,
                bias=BADS[:, 0:1]), waits=[Wv(ts, cnt), Wv(sv, bads_done)])
            arelu[tt] = fence("s", ss)

        ev_done = {}
        for oc in range(NDC):
            for tt in range(4):
                j = oc * 4 + tt
                pbu = SPB[:, (j % 2) * 512:(j % 2) * 512 + 512]
                waits = [Wv(ss, arelu[tt]), Wv(sv, sel_u), Wv(sv, bcomb_done)]
                if j >= 2:
                    waits.append(Wv(sv, ev_done[j - 2]))
                else:
                    waits.append(Wv(ss, sq_end[(k_total - 1, NDC - 1 - (1 - j))]))
                cnt = op("t", lambda e, oc=oc, tt=tt, pbu=pbu: e.matmul(
                    pbu, WAUS[:, oc * 128:oc * 128 + 128],
                    ASB[:, tt * 512:tt * 512 + 512], start=True, stop=True),
                    waits=waits, inc=ts)
                ev_done[j] = op("v", lambda e, oc=oc, tt=tt, pbu=pbu:
                                e.scalar_tensor_tensor(
                                    xr3t[:, oc, tt * 512:tt * 512 + 512], pbu,
                                    BCOMB[:, oc:oc + 1],
                                    xr3t[:, oc, tt * 512:tt * 512 + 512],
                                    ALU.add, ALU.add),
                                waits=[Wv(ts, cnt), Wv(sv, ev_b[NDC * 4 - 1])],
                                inc=sv)
        return {"last_v": fence("v", sv)}

    def emit_attn(ffn1):
        spill = op("y", lambda e: e.dma_start(d_xdr[:, :, :], r2(XR[:, :], T)),
                   waits=[Wv(sv, ffn1["last_v"])], inc=sdy, by=16)
        bar = [Wv(ts, ts.n), Wv(ss, ss.n), Wv(sv, sv.n), Wv(sp, sp.n),
               Wv(sdy, sdy.n), Wv(sdg, sdg.n)]
        xrq_load = {0: op("y", lambda e: e.dma_start(
            r2(XRQ[:, :], 512), d_xdr[:, :, 0:512]),
            waits=[Wv(sdy, spill)], inc=sdy, by=16)}
        bv_ld = op("y", lambda e: e.dma_start(LROW[0:1, :], d_bvr), waits=bar,
                   inc=sdy, by=16)
        op("g", lambda e: e.tensor_copy(BVR[0:1, :], LROW[0:1, :]),
           waits=[Wv(sdy, bv_ld), *bar])
        bvr_c = fence("g", sp)
        qk_cast = []
        for i in range(16):
            half = i % 2
            ld = op("y", lambda e, i=i, half=half: e.dma_start(
                RCBT[:, half * 1024:half * 1024 + 1024],
                d_wqk[:, i // 2, (i % 2) * 1024:(i % 2) * 1024 + 1024]),
                waits=list(bar) if i < 2 else [Wv(sp, qk_cast[i - 2])],
                inc=sdy, by=16)
            op("g", lambda e, i=i, half=half: e.tensor_copy(
                WQTQK[:, i * 1024:i * 1024 + 1024],
                RCBT[:, half * 1024:half * 1024 + 1024]),
                waits=[Wv(sdy, ld)])
            qk_cast.append(fence("g", sp))
        wog_cast = []
        for gg in range(8):
            ld = op("y", lambda e, gg=gg: e.dma_start(
                r2(RCBT[0:64, :], 1024),
                bass.AP(d_wog.tensor, gg * 2 * 64 * 1024,
                        [[1024, 64], [65536, 2], [1, 1024]])),
                waits=[Wv(sp, qk_cast[-1] if gg == 0 else wog_cast[-1])],
                inc=sdy, by=16)
            op("g", lambda e, gg=gg: e.tensor_copy(
                WOG[:, gg * 2048:gg * 2048 + 2048], RCBT[0:64, :]),
                waits=[Wv(sdy, ld)])
            wog_cast.append(fence("g", sp))
        vsb_ones = op("v", lambda e: e.memset(
            r3(VSB[:, :], 16, 65)[:, :, :, 64:65], 1.0), waits=bar, inc=sv)

        def xrq_of(q):
            return r2((XRQ if q % 2 == 0 else XRQ2)[:, :], 512)

        qkv3 = r3(QKV[:, :], HQ, 2048)
        vsb4 = r3(VSB[:, :], 16, 65)
        xlq3 = r2(XLQ[:, :], 512)
        wqk4 = r3(WQTQK[:, :], 16, 128)
        vals3 = r2(QKV[0:64, 0:8192], 2048)

        last_wo_ev = 0
        wbs = {}
        lnd_all = {}
        prev_wo = None
        for q in range(NQ):
            if q + 1 < NQ:
                wl = [Wv(sdy, spill)]
                if q >= 1:
                    wl.append(Wv(sdy, wbs[q - 1]))
                xrq_load[q + 1] = op("y", lambda e, q=q: e.dma_start(
                    xrq_of(q + 1), d_xdr[:, :, (q + 1) * 512:(q + 1) * 512 + 512]),
                    waits=wl, inc=sdy, by=16)
            if q == 0:
                lnd_all[0] = emit_ln(xrq_of(0), lambda dc: xlq3[:, dc, :],
                                     LNSQA, MBRBA, False,
                                     [Wv(sdy, xrq_load[0]), Wv(sp, bvr_c)])
            lnd = lnd_all[q]
            pq_banks = [SPA[64:128, 0:512], SPA[64:128, 512:1024],
                        SPB[64:128, 0:512], SPB[64:128, 512:1024]]
            ev_cnt = {}
            for g in range(16):
                for qk in range(2):
                    i = g * 2 + qk
                    pqk = pq_banks[i % 4]
                    cnt = 0
                    for dc in range(NDC):
                        waits = []
                        if dc == 0:
                            waits = [Wv(sv, lnd["v"]), Wv(sp, lnd["p"]),
                                     Wv(sp, qk_cast[-1])]
                            if i >= 4:
                                c, wch = ev_cnt[i - 4]
                                waits.append(Wv(ss if wch == 0 else sv, c))
                            elif prev_wo is not None:
                                waits.append(Wv(sv, prev_wo[4 + i]))
                        cnt = op("t", lambda e, g=g, qk=qk, dc=dc, pqk=pqk:
                                 e.matmul(pqk,
                                          wqk4[:, dc, g, qk * 64:qk * 64 + 64],
                                          xlq3[:, dc, :], start=(dc == 0),
                                          stop=(dc == 7), tile_position=(0, 64)),
                                 waits=waits, inc=ts if dc == 7 else None)
                    dst = qkv3[64:128, qk, :, g * 128:g * 128 + 128]
                    src = r2(pqk, 128)
                    if qk == 0:
                        c = op("s", lambda e, dst=dst, src=src, g=g: e.add(
                            dst, src, BQ2[64:128, g:g + 1]),
                            waits=[Wv(ts, cnt)], inc=ss)
                        ev_cnt[i] = (c, 0)
                    else:
                        c = op("v", lambda e, dst=dst, src=src, g=g: e.tensor_scalar(
                            dst, src, BK2[64:128, g:g + 1], None, ALU.add),
                            waits=[Wv(ts, cnt)], inc=sv)
                        ev_cnt[i] = (c, 1)
            qkv_ev_s = ev_cnt[30][0]
            qkv_ev_v = ev_cnt[31][0]
            pv_of = [SPA[:, 0:1024], SPB[:, 0:1024], VP[:, 0:1024], VP[:, 1024:2048]]
            wv_dma = {}
            vcast = {}
            vmm = {}
            for dc in range(NDC):
                wl = [Wv(ts, vmm.get((3, dc - 2), 0))] if dc >= 2 else \
                    [Wv(ss, qkv_ev_s), Wv(sv, qkv_ev_v)]
                wv_dma[dc] = op("y", lambda e, dc=dc: e.dma_start(
                    RCBT[:, (dc % 2) * 1024:(dc % 2) * 1024 + 1024],
                    d_wv[:, dc, :]), waits=wl, inc=sdy, by=16)
                op("g", lambda e, dc=dc: e.tensor_copy(
                    r2(WQTVC[:, :], 1024)[:, dc % 2, :],
                    RCBT[:, (dc % 2) * 1024:(dc % 2) * 1024 + 1024]),
                    waits=[Wv(sdy, wv_dma[dc])])
                vcast[dc] = fence("g", sp)
                for h in range(HQ):
                    pv = pv_of[h]
                    cnt = 0
                    for half in range(2):
                        waits = []
                        if half == 0:
                            waits.append(Wv(sp, vcast[dc]))
                            if dc == 0:
                                waits.append(Wv(ss, qkv_ev_s))
                                waits.append(Wv(sv, qkv_ev_v))
                        cnt = op("t", lambda e, h=h, dc=dc, half=half, pv=pv:
                                 e.matmul(pv[:, half * 512:half * 512 + 512],
                                          xlq3[:, dc, h * 128:h * 128 + 128],
                                          r2(WQTVC[:, :], 1024)[
                                              :, dc % 2,
                                              half * 512:half * 512 + 512],
                                          start=(dc == 0), stop=False),
                                 waits=waits,
                                 inc=ts if (half == 1 and h == HQ - 1) else None)
                    vmm[(h, dc)] = cnt
            vev = {}
            for h in range(HQ):
                pv = pv_of[h]
                cnt = 0
                for half in range(2):
                    cnt = op("t", lambda e, h=h, half=half, pv=pv: e.matmul(
                        pv[:, half * 512:half * 512 + 512], ONESB[:, :],
                        BVR[0:1, half * 512:half * 512 + 512],
                        start=False, stop=True),
                        waits=[Wv(sp, bvr_c)] if half == 0 else (),
                        inc=ts if half == 1 else None)
                op("v", lambda e, h=h, pv=pv: e.tensor_copy(
                    vsb4[:, h, :, 0:64], r2(pv, 64)),
                    waits=[Wv(ts, cnt), Wv(sv, vsb_ones)])
                vev[h] = fence("v", sv)
            exp_cnt = {}
            scores_cnt = {}
            attnv_cnt = {}
            dance_done = {}
            for h in range(HQ):
                units = [(t1h, g2) for t1h in range(2) for g2 in range(16)]
                for ui, (t1h, g2) in enumerate(units):
                    u = ui
                    sp_t = SPA if u % 2 == 0 else SPB
                    spu = sp_t[:, 0:1024]
                    cnt = 0
                    for kk in range(2):
                        waits = []
                        if kk == 0:
                            if (h, u - 2) in exp_cnt:
                                waits.append(Wv(ss, exp_cnt[(h, u - 2)]))
                            elif (h - 1, u + 30) in exp_cnt:
                                waits.append(Wv(ss, exp_cnt[(h - 1, u + 30)]))
                            else:
                                waits.append(Wv(sv, vev[HQ - 1]))
                        cnt = op("t", lambda e, h=h, t1h=t1h, g2=g2, kk=kk, spu=spu:
                                 e.matmul(
                                     spu[:, kk * 512:kk * 512 + 512],
                                     qkv3[64:128, 1, h, g2 * 128:g2 * 128 + 128],
                                     qkv3[64:128, 0, h, t1h * 1024 + kk * 512:
                                          t1h * 1024 + kk * 512 + 512],
                                     start=True, stop=True),
                                 waits=waits, inc=ts if kk == 1 else None)
                    scores_cnt[(h, u)] = cnt
                    exp_cnt[(h, u)] = op(
                        "s", lambda e, spu=spu, u=u: e.activation(
                            ES[:, (u % 2) * 1024:(u % 2) * 1024 + 1024], spu,
                            AF.Exp, scale=0.125),
                        waits=[Wv(ts, cnt)], inc=ss)
                    if ui >= 1:
                        emit_attnv(h, units[ui - 1][0], units[ui - 1][1],
                                   exp_cnt, attnv_cnt, dance_done, vev)
                    if ui == len(units) - 1:
                        emit_attnv(h, t1h, g2, exp_cnt, attnv_cnt, dance_done, vev)
                        for th in range(2):
                            emit_dance(h, th, attnv_cnt, dance_done)
            if q + 1 < NQ:
                lnd_all[q + 1] = emit_ln(
                    xrq_of(q + 1), lambda dc: xlq3[:, dc, :], LNSQA, MBRBA,
                    False, [Wv(sdy, xrq_load[q + 1]), Wv(sp, bvr_c),
                            Wv(sv, dance_done[(HQ - 1, 1)])])
            wo_ev = {}
            for oc in range(NDC):
                pwo = [SPA[:, 0:512], SPA[:, 512:1024], SPB[:, 0:512],
                       SPB[:, 512:1024]][oc % 4]
                cnt = 0
                for g1 in range(16):
                    waits = []
                    if g1 == 0:
                        waits = [Wv(sv, dance_done[(HQ - 1, 1)]),
                                 Wv(sp, wog_cast[-1])]
                        if oc >= 4:
                            waits.append(Wv(sv, wo_ev[oc - 4]))
                    cnt = op("t", lambda e, oc=oc, g1=g1, pwo=pwo: e.matmul(
                        pwo, WOG[:, g1 * 1024 + oc * 128:g1 * 1024 + oc * 128 + 128],
                        vals3[:, :, g1 * 128:g1 * 128 + 128],
                        start=(g1 == 0), stop=(g1 == 15)),
                        waits=waits, inc=ts if g1 == 15 else None)
                op("v", lambda e, oc=oc, pwo=pwo, q=q: e.scalar_tensor_tensor(
                    xrq_of(q)[:, oc, :], pwo, BO[:, oc:oc + 1],
                    xrq_of(q)[:, oc, :], ALU.add, ALU.add),
                    waits=[Wv(ts, cnt)])
                wo_ev[oc] = fence("v", sv)
            last_wo_ev = wo_ev[NDC - 1]
            wb = op("y", lambda e, q=q: e.dma_start(
                d_xdr[:, :, q * 512:q * 512 + 512], xrq_of(q)),
                waits=[Wv(sv, wo_ev[NDC - 1])] + (
                    [Wv(sdy, spill)] if q == 0 else []), inc=sdy, by=16)
            wbs[q] = wb
            prev_wo = wo_ev
        return {"wb_final": sdy.n, "last_v": last_wo_ev, "wb": wbs}

    def emit_attnv(h, t1h, g2, exp_cnt, attnv_cnt, dance_done, vev):
        u = t1h * 16 + g2
        esl = ES[:, (u % 2) * 1024:(u % 2) * 1024 + 1024]
        vph = VP[0:65, t1h * 1024:t1h * 1024 + 1024]
        vsb4 = r3(VSB[:, :], 16, 65)
        cnt = 0
        for kk in range(2):
            waits = []
            if kk == 0:
                waits.append(Wv(ss, exp_cnt[(h, u)]))
                if g2 == 0:
                    dk = dance_done.get((h - 1, t1h))
                    if dk is not None:
                        waits.append(Wv(sv, dk))
                    else:
                        waits.append(Wv(sv, vev[HQ - 1]))
            cnt = op("t", lambda e, h=h, g2=g2, kk=kk, vph=vph, esl=esl: e.matmul(
                vph[:, kk * 512:kk * 512 + 512], vsb4[:, h, g2, :],
                esl[:, kk * 512:kk * 512 + 512],
                start=(g2 == 0), stop=(g2 == 15)),
                waits=waits, inc=ts if (kk == 1 and g2 == 15) else None)
        if g2 == 15:
            attnv_cnt[(h, t1h)] = cnt

    _dprev = {}

    def emit_dance(h, t1h, attnv_cnt, dance_done):
        t0 = t1h * 1024
        op("v", lambda e, t0=t0: e.tensor_copy(
            RCBT[64:65, t0:t0 + 1024], VP[64:65, t0:t0 + 1024]),
            waits=[Wv(ts, attnv_cnt[(h, t1h)])])
        cp = fence("v", sv)
        d1_ = op("g", lambda e, t0=t0: e.dma_start(
            d_s1[0:1, t0:t0 + 1024], RCBT[64:65, t0:t0 + 1024]),
            waits=[Wv(sv, cp), Wv(sdg, _dprev.get(("s1", t0), 0))], inc=sdg, by=16)
        d2_ = op("g", lambda e, t0=t0: e.dma_start(
            RTRC[:, 0:8], bass.AP(d_s1.tensor, t0, [[8, 128], [1, 8]])),
            waits=[Wv(sdg, d1_)], inc=sdg, by=16)
        op("v", lambda e: e.reciprocal(RTRC[:, 16:24], RTRC[:, 0:8]),
                waits=[Wv(sdg, d2_)])
        rc = fence("v", sv)
        d3_ = op("g", lambda e, t0=t0: e.dma_start(
            bass.AP(d_s2.tensor, t0, [[8, 128], [1, 8]]), RTRC[:, 16:24]),
            waits=[Wv(sv, rc), Wv(sdg, _dprev.get(("s2", t0), 0))], inc=sdg, by=16)
        d4_ = op("g", lambda e, t0=t0: e.dma_start(
            RCBT[0:64, t0:t0 + 1024],
            bass.AP(d_s2.tensor, t0, [[0, 64], [1, 1024]])),
            waits=[Wv(sdg, d3_)], inc=sdg, by=16)
        _dprev[("s1", t0)] = d2_
        _dprev[("s2", t0)] = d4_
        op("v", lambda e, h=h, t0=t0: e.tensor_tensor(
            QKV[0:64, h * 2048 + t0:h * 2048 + t0 + 1024],
            VP[0:64, t0:t0 + 1024], RCBT[0:64, t0:t0 + 1024], ALU.mult),
            waits=[Wv(sdg, d4_)])
        dance_done[(h, t1h)] = fence("v", sv)

    ffn1 = emit_ffn(0, xdma)
    attn = emit_attn(ffn1)
    bar2 = [Wv(ts, ts.n), Wv(ss, ss.n), Wv(sv, sv.n), Wv(sp, sp.n), Wv(sdg, sdg.n)]
    xrl = []
    for tb in range(NTB):
        xrl.append(op("y", lambda e, tb=tb: e.dma_start(
            r2(XR[:, :], T)[:, :, tb * TBS:(tb + 1) * TBS],
            d_xdr[:, :, tb * TBS:(tb + 1) * TBS]),
            waits=[*bar2, Wv(sdy, attn["wb_final"])] if tb == 0 else
            [Wv(sdy, attn["wb"][tb])], inc=sdy, by=16))
    ffn2 = emit_ffn(1, xrl)
    for dc in range(NDC):
        op("y", lambda e, dc=dc: e.dma_start(
            d_y[:, dc, :], r2(XR[:, :], T)[:, dc, :]),
            waits=[Wv(sv, ffn2["last_v"])] if dc == 0 else (), inc=sdy, by=16)
    fin_y = sdy.n
    op("y", lambda e: e.wait_ge(sdy.h, fin_y))
    op("g", lambda e: e.wait_ge(sdy.h, fin_y))

    with nc.Block() as block:
        @block.tensor
        def _(e):
            for eng, fn in ops:
                if eng == "t":
                    fn(e)

        @block.scalar
        def _(e):
            for eng, fn in ops:
                if eng == "s":
                    fn(e)

        @block.vector
        def _(e):
            for eng, fn in ops:
                if eng == "v":
                    fn(e)

        @block.gpsimd
        def _(e):
            for eng, fn in ops:
                if eng == "g":
                    fn(e)

        @block.sync
        def _(e):
            for eng, fn in ops:
                if eng == "y":
                    fn(e)

    es.close()
    return nc


def _prep(kw):
    x = kw["L0_attn_x"]
    wqkv = kw["L0_attn_Wqkv"]
    bqkv = kw["L0_attn_bqkv"]
    wo = kw["L0_attn_Wo"]
    bo = kw["L0_attn_bo"]
    wbase = kw["L0_ffn_W_base"]
    bbase = kw["L0_ffn_b_base"]
    wd1 = kw["L0_ffn_Wd1"]
    bd1 = kw["L0_ffn_bd1"]
    wd2 = kw["L0_ffn_Wd2"]
    bd2 = kw["L0_ffn_bd2"]
    wadn = kw["L0_ffn_Wa_down"]
    badn = kw["L0_ffn_ba_down"]
    waup = kw["L0_ffn_Wa_up"]
    baup = kw["L0_ffn_ba_up"]

    f = np.float32

    def c(a):
        return np.ascontiguousarray(np.asarray(a), dtype=f)

    shared = {}
    shared["wd1"] = c(np.asarray(wd1).reshape(8, 8, 128, 256)
                      .transpose(0, 2, 1, 3).reshape(8, 128, 2048))
    shared["wd2"] = c(np.asarray(wd2).reshape(8, 2, 128, 8, 128)
                      .transpose(0, 2, 1, 3, 4).reshape(8, 128, 2048))
    shared["wbase"] = c(np.asarray(wbase).reshape(8, 128, 8, 128)
                        .transpose(2, 1, 0, 3).reshape(8, 128, 1024))
    shared["wadn"] = c(np.asarray(wadn).reshape(8, 8, 128, 64)
                       .transpose(0, 2, 1, 3).reshape(8, 128, 512))
    shared["waup"] = c(waup)
    wq4 = np.asarray(wqkv).reshape(16, 3, 64, 8, 128)
    shared["wqk"] = c(wq4[:, 0:2].transpose(4, 3, 0, 1, 2).reshape(128, 8, 2048))
    shared["wv"] = c(wq4[:, 2].transpose(3, 2, 0, 1).reshape(128, 8, 1024))
    shared["wog"] = c(np.asarray(wo).T.reshape(16, 64, 1024))
    shared["nid"] = c(-np.eye(128))
    shared["bd1"] = c(np.asarray(bd1).reshape(8, 2, 128).transpose(2, 0, 1)
                      .reshape(128, 16))
    shared["bd2"] = c(np.asarray(bd2).reshape(8, 8, 128).transpose(2, 0, 1)
                      .reshape(128, 64))
    shared["bb"] = c(np.asarray(bbase).reshape(8, 128).T)
    shared["bo"] = c(np.asarray(bo).reshape(8, 128).T)
    shared["bad"] = c(np.asarray(badn).T)
    shared["bau"] = c(np.asarray(baup).reshape(8, 8, 128).transpose(2, 1, 0)
                      .reshape(128, 64))
    shared["pw8"] = c((2.0 ** -np.arange(8)).reshape(1, 8))
    bqr = np.asarray(bqkv).reshape(16, 3, 64)
    bq = np.zeros((128, 16), f)
    bk = np.zeros((128, 16), f)
    bq[64:128, :] = bqr[:, 0, :].T
    bk[64:128, :] = bqr[:, 1, :].T
    shared["bq2"] = bq
    shared["bk2"] = bk
    shared["bvr"] = c(bqr[:, 2, :].reshape(1, 1024))

    in_maps = []
    for ci in range(NC):
        m = dict(shared)
        m["x"] = c(np.asarray(x[ci]).reshape(2048, 8, 128).transpose(2, 1, 0))
        in_maps.append(m)
    return in_maps


def kernel(**kw):
    if "nc" not in _CACHE:
        _CACHE["nc"] = build_program()
    nc = _CACHE["nc"]
    in_maps = _prep(kw)
    res = run_bass_kernel_spmd(nc, in_maps, core_ids=list(range(NC)))
    outs = []
    for ci in range(NC):
        ycm = np.asarray(res.results[ci]["y"])
        outs.append(ycm.transpose(2, 1, 0).reshape(2048, 1024))
    return np.stack(outs, axis=0).astype(np.float32)
